# version 14
# baseline (speedup 1.0000x reference)
import numpy as np
import ml_dtypes
from contextlib import ExitStack
import concourse.bass as bass
import concourse.mybir as mybir
from concourse.bass_utils import run_bass_kernel_spmd

F32 = mybir.dt.float32
BF16 = mybir.dt.bfloat16
I32 = mybir.dt.int32
AF = mybir.ActivationFunctionType
ALU = mybir.AluOpType
AX = mybir.AxisListType

T = 2048
D = 2048
NCORES = 8
SQRTD = float(np.sqrt(2048.0))

ENGINES = ("sync", "gpsimd", "tensor", "vector", "scalar")


class Prog:

    def __init__(self, nc, es):
        self.nc = nc
        self.es = es
        self.ops = {e: [] for e in ENGINES}
        self.sems = {}
        self.ctr = {}

    def sem(self, *names):
        for name in names:
            if name not in self.sems:
                self.sems[name] = self.es.enter_context(self.nc.semaphore(name))
                self.ctr[name] = 0

    def emit(self, eng, fn, waits=(), incs=()):
        assert len(incs) <= 1, "hardware allows at most one sync update per instruction"
        for s, d in incs:
            self.ctr[s] += d
        self.ops[eng].append((fn, tuple(waits), tuple(incs)))

    def c(self, semname):
        return self.ctr[semname]

    def barrier(self):
        waits = tuple((s, self.ctr[s]) for s in self.ctr)
        for eng in ENGINES:
            self.emit(eng, None, waits=waits)

    def build(self):
        def mk(eng_ops):
            def section(e):
                for fn, waits, incs in eng_ops:
                    for s, v in waits:
                        if v > 0:
                            e.wait_ge(self.sems[s], v)
                    if fn is None:
                        continue
                    inst = fn(e)
                    for s, d in incs:
                        inst.then_inc(self.sems[s], d)
            return section

        with self.nc.Block() as block:
            for eng in ENGINES:
                if self.ops[eng]:
                    getattr(block, eng)(mk(self.ops[eng]))



_UNIQ = [0]


def _mk_alloc(nc, es):
    _UNIQ[0] += 1
    tag = f"u{_UNIQ[0]}_"
    sb = lambda n, s, d: es.enter_context(nc.sbuf_tensor(tag + n, s, d))
    ps = lambda n, s, d: es.enter_context(nc.psum_tensor(tag + n, s, d))
    return sb, ps

def _pretile_lhsT(W):
    out_, in_ = W.shape
    noc, nkt = out_ // 128, in_ // 128
    A = W.T.reshape(nkt, 128, noc, 128).transpose(2, 1, 0, 3).reshape(noc, 128, nkt * 128)
    return np.ascontiguousarray(A)


def _pretile_rhs(W):
    out_, in_ = W.shape
    return np.ascontiguousarray(W.T.reshape(in_ // 128, 128, out_))


def host_prep(inp):
    p = {}
    p["win_til"] = _pretile_lhsT(inp["L0_attn_W_in"])
    p["wout_r"] = _pretile_rhs(inp["L0_attn_W_out"])
    p["wm1_til"] = _pretile_lhsT(inp["L0_attn_W_mlp1"])
    p["wm2_r"] = _pretile_rhs(inp["L0_attn_W_mlp2"])
    cw = inp["L0_attn_conv_w"][:, 0, :]
    p["cw"] = np.ascontiguousarray(cw.reshape(32, 128, 4).transpose(1, 0, 2).reshape(128, 128))
    p["cb"] = np.ascontiguousarray(inp["L0_attn_conv_b"].reshape(32, 128).T)
    p["ident"] = np.eye(128, dtype=np.float32)
    gidx = _pretile_lhsT(inp["L0_ffn_gate_idx"])
    uidx = _pretile_lhsT(inp["L0_ffn_up_idx"])
    didx = _pretile_rhs(inp["L0_ffn_down_idx"])
    inter = np.empty((86, 128, 2048), np.int32)
    inter[0::2] = gidx
    inter[1::2] = uidx
    p["idx_cat"] = np.concatenate([inter, didx], axis=0)
    p["lutrep"] = np.ascontiguousarray(np.tile(inp["L0_ffn_lut"][None, :], (128, 1)).astype(np.float32))
    p["qw_til"] = _pretile_lhsT(inp["L1_attn_q_w"])
    p["kw_til"] = _pretile_lhsT(inp["L1_attn_k_w"])
    p["vw_r"] = _pretile_rhs(inp["L1_attn_v_w"])
    p["ow_r"] = _pretile_rhs(inp["L1_attn_o_w"])
    cc, qq = np.meshgrid(np.arange(2176, dtype=np.float32), np.arange(128, dtype=np.float32))
    p["Rrel"] = np.ascontiguousarray((cc - 2047.0) - qq)
    jj, q2 = np.meshgrid(np.arange(128, dtype=np.float32), np.arange(128, dtype=np.float32))
    p["Dmask"] = np.ascontiguousarray(np.where(jj <= q2, jj - q2, np.float32(-1e38)))
    gwt = _pretile_rhs(inp["L1_ffn_gate_w"])
    gpad = np.zeros((16, 128, 128), np.float32)
    gpad[:, :, 0:8] = gwt
    p["gw_til"] = np.ascontiguousarray(gpad.transpose(1, 0, 2).reshape(128, 2048))
    p["w1_til"] = np.stack([_pretile_lhsT(inp["L1_ffn_w1"][e]) for e in range(8)])
    p["w3_til"] = np.stack([_pretile_lhsT(inp["L1_ffn_w3"][e]) for e in range(8)])
    p["w2_r"] = np.stack([_pretile_rhs(inp["L1_ffn_w2"][e]) for e in range(8)])
    p["ws1_til"] = _pretile_lhsT(inp["L1_ffn_ws1"])
    p["ws3_til"] = _pretile_lhsT(inp["L1_ffn_ws3"])
    p["ws2_r"] = _pretile_rhs(inp["L1_ffn_ws2"])
    return p


def emit_p1(pr, nc, DD, pump=None):
    x = DD["x"]; win = DD["win_til"]; wout = DD["wout_r"]
    cw_d = DD["cw"]; cb_d = DD["cb"]; id_d = DD["ident"]; x1 = DD["x1"]
    xr = x.rearrange("(tt p) d -> tt p d", p=128)
    x1r = x1.rearrange("(tt p) d -> tt p d", p=128)

    es = ExitStack()
    sb, psum = _mk_alloc(nc, es)

    h_fm = sb("h_fm", [128, 16 * 1024], BF16)
    y_fm = sb("y_fm", [128, 32 * 1024], BF16)
    xt = [sb(f"xt{i}", [128, D], F32) for i in range(2)]
    sqd = sb("sqd", [128, D], BF16)
    ssq = sb("ssq", [128, 3], F32)
    scl = sb("scl", [128, 1], F32)
    h_tm = sb("h_tm", [128, D], BF16)
    id_bf = sb("id_bf", [128, 128], BF16)
    wA = [sb(f"wA{i}", [128, 2048], BF16) for i in range(2)]
    wB = [sb(f"wB{i}", [128, 2048], BF16) for i in range(2)]
    cw_sb = sb("cw_sb", [128, 128], F32)
    cb_sb = sb("cb_sb", [128, 32], F32)
    hist = sb("hist", [128, 96], F32)
    xp_buf = sb("xp_buf", [128, 1027], F32)
    xc_acc = sb("xc_acc", [128, 1024], F32)
    xcs = sb("xcs", [128, 1024], F32)
    zs = sb("zs", [128, 1024], F32)
    wo_sb = sb("wo_sb", [128, 32 * 256], BF16)
    wo_v = wo_sb[:].rearrange("p (kt c) -> p kt c", kt=32)
    xres = [sb(f"xres{i}", [128, 256], F32) for i in range(2)]
    x1sb = [sb(f"x1sb{i}", [128, 256], F32) for i in range(2)]

    ps_xp = [psum("ps_xp0", [128, 512], F32), psum("ps_xp1", [128, 512], F32)]
    ps_z = [psum("ps_z0", [128, 512], F32), psum("ps_z1", [128, 512], F32)]
    ps_c = [psum("ps_c0", [128, 512], F32), psum("ps_c1", [128, 512], F32)]
    ps_t = [psum("ps_t0", [128, 128], BF16), psum("ps_t1", [128, 128], BF16)]

    pr.sem("ldx", "ldw", "ldc", "pe", "ve", "ac", "grd", "st", "evB", "ldr", "ldo")

    with es:
        pr.emit("sync", lambda e: e.dma_start(out=cw_sb[:], in_=cw_d), incs=(("ldc", 16),))
        pr.emit("sync", lambda e: e.dma_start(out=cb_sb[:], in_=cb_d), incs=(("ldc", 16),))
        pr.emit("gpsimd", lambda e: e.dma_start(out=id_bf[:], in_=id_d), incs=(("ldc", 16),))
        wc = (("ldc", pr.c("ldc")),)

        relA = [(), ()]
        relB_state = [0, 0]
        rel_w = [0, 0]
        rel_xres = [0, 0]
        rel_x1sb = [0, 0]
        tpose_ev = []
        prev_chain = pr.c("grd")

        for H in range(2):
            for i in range(8):
                ti = H * 8 + i
                sl = ti % 2
                pr.emit("sync", lambda e, ti=ti, sl=sl: e.dma_start(out=xt[sl][:], in_=xr[ti]),
                        waits=relA[sl], incs=(("ldx", 16),))
                vx = pr.c("ldx")
                pr.emit("scalar", lambda e, sl=sl: e.activation(sqd[:], xt[sl][:], AF.Square, accum_out=ssq[:, 0:1]),
                        waits=(("ldx", vx), ("grd", prev_chain)), incs=(("grd", 1),))
                v1 = pr.c("grd")
                pr.emit("scalar", lambda e: e.activation(ssq[:, 1:2], ssq[:, 0:1], AF.Sqrt),
                        waits=(("grd", v1),), incs=(("grd", 1),))
                v2 = pr.c("grd")
                pr.emit("vector", lambda e: e.tensor_scalar_max(ssq[:, 2:3], ssq[:, 1:2], 1e-12),
                        waits=(("grd", v2),), incs=(("grd", 1),))
                v3 = pr.c("grd")
                pr.emit("vector", lambda e: e.reciprocal(scl[:], ssq[:, 2:3]),
                        waits=(("grd", v3),), incs=(("grd", 1),))
                g = pr.c("grd")
                prev_chain = g
                hw = [("grd", g)]
                if tpose_ev:
                    hw.append(("pe", pr.c("pe")))
                pr.emit("vector",
                        lambda e, sl=sl: e.tensor_scalar(h_tm[:], xt[sl][:], scl[:], SQRTD, ALU.mult, ALU.mult),
                        waits=tuple(hw), incs=(("ve", 1),))
                vh = pr.c("ve")
                relA[sl] = (("grd", g), ("ve", vh))
                for dt in range(16):
                    w = [("ve", vh)] + list(wc)
                    if len(tpose_ev) >= 2:
                        w.append(("ac", tpose_ev[-2]))
                    pr.emit("tensor",
                            lambda e, dt=dt: e.transpose(ps_t[dt % 2][:], h_tm[:, dt * 128:(dt + 1) * 128], id_bf[:]),
                            waits=tuple(w), incs=(("pe", 1),))
                    vpe = pr.c("pe")
                    pr.emit("scalar",
                            lambda e, dt=dt, i=i: e.copy(h_fm[:, dt * 1024 + i * 128: dt * 1024 + (i + 1) * 128],
                                                          ps_t[dt % 2][:]),
                            waits=(("pe", vpe),), incs=(("ac", 1),))
                    tpose_ev.append(pr.c("ac"))
            vA_done = pr.c("ac")

            for c in range(32):
                sl = c % 2
                pr.emit("gpsimd", lambda e, c=c, sl=sl: e.dma_start(out=wA[sl][:], in_=win[c]),
                        waits=(("pe", rel_w[sl]),), incs=(("ldw", 16),))
                pr.emit("gpsimd", lambda e, c=c, sl=sl: e.dma_start(out=wB[sl][:], in_=win[c + 32]),
                        incs=(("ldw", 16),))
                vw = pr.c("ldw")
                wb = [("ldw", vw), ("ac", vA_done)]
                if relB_state[0] > 0:
                    wb += [("ac", relB_state[0]), ("ve", relB_state[1])]
                for n in range(2):
                    for kt in range(16):
                        pr.emit("tensor",
                                lambda e, kt=kt, n=n, sl=sl: e.matmul(
                                    ps_xp[n][:], wA[sl][:, kt * 128:(kt + 1) * 128],
                                    h_fm[:, kt * 1024 + n * 512: kt * 1024 + (n + 1) * 512],
                                    start=(kt == 0), stop=(kt == 15)),
                                waits=tuple(wb) if (kt == 0 and n == 0) else ())
                for n in range(2):
                    for kt in range(16):
                        pr.emit("tensor",
                                lambda e, kt=kt, n=n, sl=sl: e.matmul(
                                    ps_z[n][:], wB[sl][:, kt * 128:(kt + 1) * 128],
                                    h_fm[:, kt * 1024 + n * 512: kt * 1024 + (n + 1) * 512],
                                    start=(kt == 0), stop=(kt == 15)),
                                incs=(("pe", 1),) if (kt == 15 and n == 1) else ())
                vmm = pr.c("pe")
                rel_w[sl] = vmm
                vy_prev = pr.c("evB")
                relB_state[0] = pr.c("ac") + 2
                relB_state[1] = pr.c("ve") + 2
                pr.emit("scalar", lambda e: e.activation(zs[:, 0:512], ps_z[0][:], AF.Silu),
                        waits=(("pe", vmm), ("evB", vy_prev)), incs=(("ac", 1),))
                pr.emit("scalar", lambda e: e.activation(zs[:, 512:1024], ps_z[1][:], AF.Silu),
                        incs=(("ac", 1),))
                vz = pr.c("ac")
                pr.emit("vector", lambda e: e.tensor_copy(xp_buf[:, 3:515], ps_xp[0][:]),
                        waits=(("pe", vmm), ("evB", vy_prev)), incs=(("ve", 1),))
                pr.emit("vector", lambda e: e.tensor_copy(xp_buf[:, 515:1027], ps_xp[1][:]),
                        incs=(("ve", 1),))
                if H == 0:
                    pr.emit("vector", lambda e: e.memset(xp_buf[:, 0:3], 0.0), incs=(("ve", 1),))
                else:
                    pr.emit("vector", lambda e, c=c: e.tensor_copy(xp_buf[:, 0:3], hist[:, c * 3:c * 3 + 3]),
                            incs=(("ve", 1),))
                vxp = pr.c("ve")
                pr.emit("vector",
                        lambda e, c=c: e.tensor_scalar(xc_acc[:], xp_buf[:, 0:1024],
                                                       cw_sb[:, c * 4:c * 4 + 1], cb_sb[:, c:c + 1],
                                                       ALU.mult, ALU.add),
                        waits=(("ve", vxp),) + wc, incs=(("ve", 1),))
                for j in (1, 2, 3):
                    vprev = pr.c("ve")
                    pr.emit("vector",
                            lambda e, c=c, j=j: e.scalar_tensor_tensor(xc_acc[:], xp_buf[:, j:1024 + j],
                                                                       cw_sb[:, c * 4 + j:c * 4 + j + 1],
                                                                       xc_acc[:], ALU.mult, ALU.add),
                            waits=(("ve", vprev),), incs=(("ve", 1),))
                vconv = pr.c("ve")
                if H == 0:
                    pr.emit("vector", lambda e, c=c: e.tensor_copy(hist[:, c * 3:c * 3 + 3], xp_buf[:, 1024:1027]),
                            incs=(("ve", 1),))
                pr.emit("scalar", lambda e: e.activation(xcs[:], xc_acc[:], AF.Silu),
                        waits=(("ve", vconv),), incs=(("ac", 1),))
                vxs = pr.c("ac")
                pr.emit("vector",
                        lambda e, c=c: e.tensor_tensor(y_fm[:, c * 1024:(c + 1) * 1024], xcs[:], zs[:], ALU.mult),
                        waits=(("ac", vxs),), incs=(("evB", 1),))
                if pump is not None:
                    pump.step_iters(5)
            vB_done = pr.c("evB")

            vC_prev = pr.c("pe")
            rel_psc1 = [pr.c("ve"), pr.c("ve")]
            for n in range(8):
                ldo_parts = []
                for part in range(4):
                    pr.emit("gpsimd",
                            lambda e, n=n, part=part: e.dma_start(
                                out=wo_v[:, part * 8:(part + 1) * 8, :],
                                in_=wout[part * 8:(part + 1) * 8, :, n * 256:(n + 1) * 256]
                                    .rearrange("kt p c -> p kt c")),
                            waits=(("pe", vC_prev),) if part == 0 else (), incs=(("ldo", 16),))
                    ldo_parts.append(pr.c("ldo"))
                for tt in range(8):
                    ti = H * 8 + tt
                    sl = tt % 2
                    pr.emit("sync", lambda e, ti=ti, n=n, sl=sl: e.dma_start(
                                out=xres[sl][:], in_=xr[ti, :, n * 256:(n + 1) * 256]),
                            waits=(("ve", rel_xres[sl]),), incs=(("ldr", 16),))
                    vr = pr.c("ldr")
                    wpe = [("ldo", ldo_parts[0]), ("evB", vB_done), ("ve", rel_psc1[sl])]
                    for c in range(32):
                        if c == 0:
                            w_ = tuple(wpe)
                        elif c % 8 == 0:
                            w_ = (("ldo", ldo_parts[c // 8]),)
                        else:
                            w_ = ()
                        pr.emit("tensor",
                                lambda e, c=c, tt=tt, sl=sl: e.matmul(
                                    ps_c[sl][:, 0:256], y_fm[:, c * 1024 + tt * 128: c * 1024 + (tt + 1) * 128],
                                    wo_sb[:, c * 256:(c + 1) * 256],
                                    start=(c == 0), stop=(c == 31)),
                                waits=w_, incs=(("pe", 1),) if c == 31 else ())
                    vpc = pr.c("pe")
                    pr.emit("vector",
                            lambda e, sl=sl: e.tensor_tensor(x1sb[sl][:], ps_c[sl][:, 0:256], xres[sl][:], ALU.add),
                            waits=(("pe", vpc), ("ldr", vr), ("st", rel_x1sb[sl])), incs=(("ve", 1),))
                    vx1 = pr.c("ve")
                    rel_xres[sl] = vx1
                    rel_psc1[sl] = vx1
                    pr.emit("sync",
                            lambda e, ti=ti, n=n, sl=sl: e.dma_start(
                                out=x1r[ti, :, n * 256:(n + 1) * 256], in_=x1sb[sl][:]),
                            waits=(("ve", vx1),), incs=(("st", 16),))
                    rel_x1sb[sl] = pr.c("st")
                vC_prev = pr.c("pe")
                if pump is not None:
                    pump.step_iters(2)


def build_p1():
    nc = bass.Bass("TRN2", target_bir_lowering=False)
    DD = {}
    DD["x"] = nc.dram_tensor("x", [T, D], F32, kind="ExternalInput").ap()
    DD["win_til"] = nc.dram_tensor("win_til", [64, 128, 2048], F32, kind="ExternalInput").ap()
    DD["wout_r"] = nc.dram_tensor("wout_r", [32, 128, 2048], F32, kind="ExternalInput").ap()
    DD["cw"] = nc.dram_tensor("cw", [128, 128], F32, kind="ExternalInput").ap()
    DD["cb"] = nc.dram_tensor("cb", [128, 32], F32, kind="ExternalInput").ap()
    DD["ident"] = nc.dram_tensor("ident", [128, 128], F32, kind="ExternalInput").ap()
    DD["x1"] = nc.dram_tensor("x1", [T, D], F32, kind="ExternalOutput").ap()
    es0 = ExitStack()
    pr = Prog(nc, es0)
    with es0:
        emit_p1(pr, nc, DD)
        pr.emit("gpsimd", lambda e: e.nop(), waits=(("st", pr.c("st")),))
        pr.build()
    return nc


def emit_tile_to_fm(pr, st, src_r, ti, h_fm, half_w, col_of, mode, bufs, wc=()):
    sl = ti % 2
    xt, h_tm, sqd, ssq, scl, id_bf, ps_t = (bufs[k] for k in ("xt", "h_tm", "sqd", "ssq", "scl", "id_bf", "ps_t"))
    pr.emit("sync", lambda e: e.dma_start(out=xt[sl][:], in_=src_r[ti]),
            waits=st["relA"][sl], incs=(("ldx", 16),))
    vx = pr.c("ldx")
    if mode == "rms":
        pr.emit("scalar", lambda e: e.activation(sqd[:], xt[sl][:], AF.Square, accum_out=ssq[:, 0:1]),
                waits=(("ldx", vx), ("grd", st["prev_chain"])), incs=(("grd", 1),))
        v1 = pr.c("grd")
        pr.emit("scalar", lambda e: e.activation(ssq[:, 1:2], ssq[:, 0:1], AF.Sqrt),
                waits=(("grd", v1),), incs=(("grd", 1),))
        v2 = pr.c("grd")
        pr.emit("vector", lambda e: e.tensor_scalar_max(ssq[:, 2:3], ssq[:, 1:2], 1e-12),
                waits=(("grd", v2),), incs=(("grd", 1),))
        v3 = pr.c("grd")
        pr.emit("vector", lambda e: e.reciprocal(scl[:], ssq[:, 2:3]),
                waits=(("grd", v3),), incs=(("grd", 1),))
        g = pr.c("grd")
        st["prev_chain"] = g
        hw = [("grd", g), ("pe", st["tp_pe"])]
        pr.emit("vector", lambda e: e.tensor_scalar(h_tm[:], xt[sl][:], scl[:], SQRTD, ALU.mult, ALU.mult),
                waits=tuple(hw), incs=(("ve", 1),))
        rel = (("grd", g), ("ve", pr.c("ve")))
    elif mode == "clip":
        pr.emit("vector", lambda e: e.tensor_scalar(h_tm[:], xt[sl][:], 10000.0, -10000.0, ALU.min, ALU.max),
                waits=(("ldx", vx), ("pe", st["tp_pe"])), incs=(("ve", 1),))
        rel = (("ve", pr.c("ve")),)
    else:
        pr.emit("vector", lambda e: e.tensor_copy(h_tm[:], xt[sl][:]),
                waits=(("ldx", vx), ("pe", st["tp_pe"])), incs=(("ve", 1),))
        rel = (("ve", pr.c("ve")),)
    vh = pr.c("ve")
    st["relA"][sl] = rel
    for dt in range(16):
        w = [("ve", vh)] + list(wc)
        if len(st["tp_ev"]) >= 2:
            w.append(("ac", st["tp_ev"][-2]))
        pr.emit("tensor", lambda e, dt=dt: e.transpose(ps_t[dt % 2][:], h_tm[:, dt * 128:(dt + 1) * 128], id_bf[:]),
                waits=tuple(w), incs=(("pe", 1),))
        vpe = pr.c("pe")
        pr.emit("scalar", lambda e, dt=dt: e.copy(h_fm[:, col_of(dt): col_of(dt) + 128], ps_t[dt % 2][:]),
                waits=(("pe", vpe),), incs=(("ac", 1),))
        st["tp_ev"].append(pr.c("ac"))
    st["tp_pe"] = pr.c("pe")


def emit_ln_row(pr, st, pre_ap, out_sb_ap, bufs, waits):
    lsq, lst = bufs["lsq"], bufs["lst"]
    pr.emit("scalar", lambda e: e.activation(lsq[:], pre_ap, AF.Copy, accum_out=lst[:, 0:1]),
            waits=tuple(waits) + (("grd", st["prev_chain"]),), incs=(("grd", 1),))
    v1 = pr.c("grd")
    pr.emit("scalar", lambda e: e.activation(lsq[:], pre_ap, AF.Square, accum_out=lst[:, 1:2]),
            waits=(("grd", v1),), incs=(("grd", 1),))
    v2 = pr.c("grd")
    pr.emit("scalar", lambda e: e.mul(lst[:, 2:3], lst[:, 0:1], 1.0 / D),
            waits=(("grd", v2),), incs=(("grd", 1),))
    v3 = pr.c("grd")
    pr.emit("vector", lambda e: e.tensor_tensor(lst[:, 3:4], lst[:, 2:3], lst[:, 2:3], ALU.mult),
            waits=(("grd", v3),), incs=(("grd", 1),))
    v4 = pr.c("grd")
    pr.emit("vector", lambda e: e.scalar_tensor_tensor(lst[:, 4:5], lst[:, 1:2], 1.0 / D, lst[:, 3:4],
                                                       ALU.mult, ALU.subtract),
            waits=(("grd", v4),), incs=(("grd", 1),))
    v5 = pr.c("grd")
    pr.emit("vector", lambda e: e.tensor_scalar_add(lst[:, 4:5], lst[:, 4:5], 1e-5),
            waits=(("grd", v5),), incs=(("grd", 1),))
    v5b = pr.c("grd")
    pr.emit("scalar", lambda e: e.activation(lst[:, 5:6], lst[:, 4:5], AF.Sqrt),
            waits=(("grd", v5b),), incs=(("grd", 1),))
    v6 = pr.c("grd")
    pr.emit("vector", lambda e: e.reciprocal(lst[:, 3:4], lst[:, 5:6]),
            waits=(("grd", v6),), incs=(("grd", 1),))
    v7 = pr.c("grd")
    pr.emit("vector", lambda e: e.tensor_scalar(out_sb_ap, pre_ap, lst[:, 2:3], lst[:, 3:4],
                                                ALU.subtract, ALU.mult),
            waits=(("grd", v7),), incs=(("grd", 1),))
    st["prev_chain"] = pr.c("grd")
    return pr.c("grd")


def emit_p2(pr, nc, DD, pump=None):
    x = DD["x"]; x1 = DD["x1"]; wm1 = DD["wm1_til"]; wm2 = DD["wm2_r"]
    id_d = DD["ident"]; x2 = DD["x2"]; m_s = DD["m_s"]
    xr = x.rearrange("(tt p) d -> tt p d", p=128)
    x1r = x1.rearrange("(tt p) d -> tt p d", p=128)
    x2r = x2.rearrange("(tt p) d -> tt p d", p=128)
    msr = m_s.rearrange("(tt p) d -> tt p d", p=128)

    es = ExitStack()
    sb, psum = _mk_alloc(nc, es)

    bufs = {}
    xt0 = sb("xt0", [128, D], F32)
    bufs["xt"] = [xt0, xt0]
    bufs["sqd"] = sb("sqd", [128, D], BF16)
    bufs["ssq"] = sb("ssq", [128, 3], F32)
    bufs["scl"] = sb("scl", [128, 1], F32)
    bufs["h_tm"] = sb("h_tm", [128, D], BF16)
    bufs["id_bf"] = sb("id_bf", [128, 128], BF16)
    bufs["lsq"] = bufs["sqd"]
    bufs["lst"] = sb("lst", [128, 6], F32)
    bufs["ps_t"] = [psum("ps_t0", [128, 128], BF16), psum("ps_t1", [128, 128], BF16)]
    wcA = [sb(f"wcA{i}", [128, 2048], BF16) for i in range(2)]
    mst = [sb(f"mst{i}", [128, 256], F32) for i in range(2)]
    ps_m = [psum("ps_m0", [128, 512], F32), psum("ps_m1", [128, 512], F32)]
    ps_c = [psum("ps_c0", [128, 256], F32), psum("ps_c1", [128, 256], F32)]

    pr.sem("ldx", "ldw", "ldc", "pe", "ve", "ac", "grd", "st", "ldo", "ldd", "evC")

    st = {"relA": [(), ()], "prev_chain": pr.c("grd"), "tp_ev": [], "tp_pe": pr.c("pe")}
    rel_w = [0, 0]

    with es:
        pr.emit("gpsimd", lambda e: e.dma_start(out=bufs["id_bf"][:], in_=id_d), incs=(("ldc", 16),))
        wc = (("ldc", pr.c("ldc")),)

        for H in range(2):
          with nc.sbuf_tensor(f"relu_fm{_UNIQ[0]}_{H}", [128, 64 * 1024], BF16) as relu_fm:
            with nc.sbuf_tensor(f"h2_fm{_UNIQ[0]}_{H}", [128, 16 * 1024], BF16) as h2_fm_t:
                h2_fm = h2_fm_t
                for i in range(8):
                    ti = H * 8 + i
                    emit_tile_to_fm(pr, st, x1r, ti, h2_fm, 1024,
                                    lambda dt, i=i: dt * 1024 + i * 128, "rms", bufs, wc)
                    st["relA"] = [st["relA"][ti % 2], st["relA"][ti % 2]]
                vA_done = pr.c("ac")

                for c in range(64):
                    sl = c % 2
                    pr.emit("gpsimd", lambda e, c=c, sl=sl: e.dma_start(out=wcA[sl][:], in_=wm1[c]),
                            waits=(("pe", rel_w[sl]),), incs=(("ldw", 16),))
                    vw = pr.c("ldw")
                    wb = [("ldw", vw), ("ac", vA_done)]
                    if pr.c("evC") > 0:
                        wb.append(("evC", pr.c("evC")))
                    for n in range(2):
                        for kt in range(16):
                            pr.emit("tensor",
                                    lambda e, kt=kt, n=n, sl=sl: e.matmul(
                                        ps_m[n][:], wcA[sl][:, kt * 128:(kt + 1) * 128],
                                        h2_fm[:, kt * 1024 + n * 512: kt * 1024 + (n + 1) * 512],
                                        start=(kt == 0), stop=(kt == 15)),
                                    waits=tuple(wb) if (kt == 0 and n == 0) else (),
                                    incs=(("pe", 1),) if (kt == 15 and n == 1) else ())
                    vmm = pr.c("pe")
                    rel_w[sl] = vmm
                    pr.emit("scalar", lambda e, c=c: e.activation(relu_fm[:, c * 1024: c * 1024 + 512],
                                                                  ps_m[0][:], AF.Relu),
                            waits=(("pe", vmm),), incs=(("evC", 1),))
                    pr.emit("scalar", lambda e, c=c: e.activation(relu_fm[:, c * 1024 + 512: (c + 1) * 1024],
                                                                  ps_m[1][:], AF.Relu),
                            incs=(("evC", 1),))
                    if pump is not None:
                        pump.step_iters(5)
                vB_done = pr.c("evC")
                vB_pe = pr.c("pe")

            with nc.sbuf_tensor(f"wm2_sb{_UNIQ[0]}_{H}", [128, 64 * 256], BF16) as wm2_sb:
                wm2_v = wm2_sb[:].rearrange("p (kt c) -> p kt c", kt=64)
                vC_prev = vB_pe
                rel_psc2 = [pr.c("ve"), pr.c("ve")]
                for nn in range(8):
                    ldo_parts = []
                    for part in range(8):
                        pr.emit("gpsimd",
                                lambda e, nn=nn, part=part: e.dma_start(
                                    out=wm2_v[:, part * 8:(part + 1) * 8, :],
                                    in_=wm2[part * 8:(part + 1) * 8, :, nn * 256:(nn + 1) * 256]
                                        .rearrange("kt p c -> p kt c")),
                                waits=(("pe", vC_prev), ("evC", vB_done)) if part == 0 else (),
                                incs=(("ldo", 16),))
                        ldo_parts.append(pr.c("ldo"))
                    for tt in range(8):
                        ti = H * 8 + tt
                        sl = tt % 2
                        wpe = [("ldo", ldo_parts[0]), ("evC", vB_done), ("ve", rel_psc2[sl])]
                        for c in range(64):
                            if c == 0:
                                w_ = tuple(w for w in wpe if w[1] > 0)
                            elif c % 8 == 0:
                                w_ = (("ldo", ldo_parts[c // 8]),)
                            else:
                                w_ = ()
                            pr.emit("tensor",
                                    lambda e, c=c, tt=tt, nn=nn, sl=sl: e.matmul(
                                        ps_c[sl][:], relu_fm[:, c * 1024 + tt * 128: c * 1024 + (tt + 1) * 128],
                                        wm2_sb[:, c * 256:(c + 1) * 256],
                                        start=(c == 0), stop=(c == 63)),
                                    waits=w_,
                                    incs=(("pe", 1),) if c == 63 else ())
                        vpc = pr.c("pe")
                        pr.emit("vector", lambda e, sl=sl: e.tensor_copy(mst[sl][:], ps_c[sl][:]),
                                waits=(("pe", vpc), ("st", max(0, pr.c("st") - 16))), incs=(("ve", 1),))
                        rel_psc2[sl] = pr.c("ve")
                        vev = pr.c("ve")
                        pr.emit("sync", lambda e, ti=ti, nn=nn, sl=sl: e.dma_start(
                                    out=msr[ti, :, nn * 256:(nn + 1) * 256], in_=mst[sl][:]),
                                waits=(("ve", vev),), incs=(("st", 16),))
                        if pump is not None:
                            pump.step_iters(3)
                    vC_prev = pr.c("pe")

        vst_C = pr.c("st")
        with ExitStack() as esd:
            dx = [esd.enter_context(nc.sbuf_tensor(f"dx{_UNIQ[0]}_{i}", [128, D], F32)) for i in range(2)]
            d1 = [esd.enter_context(nc.sbuf_tensor(f"d1{_UNIQ[0]}_{i}", [128, D], F32)) for i in range(2)]
            dm = [esd.enter_context(nc.sbuf_tensor(f"dm{_UNIQ[0]}_{i}", [128, D], F32)) for i in range(2)]
            do = [esd.enter_context(nc.sbuf_tensor(f"do{_UNIQ[0]}_{i}", [128, D], F32)) for i in range(2)]
            rel_d = [(), ()]
            for ti in range(16):
                sl = ti % 2
                w0 = (("st", vst_C),) + rel_d[sl]
                pr.emit("sync", lambda e, ti=ti, sl=sl: e.dma_start(out=dx[sl][:], in_=xr[ti]),
                        waits=w0, incs=(("ldd", 16),))
                pr.emit("sync", lambda e, ti=ti, sl=sl: e.dma_start(out=d1[sl][:], in_=x1r[ti]),
                        incs=(("ldd", 16),))
                pr.emit("sync", lambda e, ti=ti, sl=sl: e.dma_start(out=dm[sl][:], in_=msr[ti]),
                        incs=(("ldd", 16),))
                vld = pr.c("ldd")
                pr.emit("vector", lambda e, sl=sl: e.tensor_tensor(dm[sl][:], dm[sl][:], dx[sl][:], ALU.add),
                        waits=(("ldd", vld),), incs=(("ve", 1),))
                vv = pr.c("ve")
                pr.emit("vector", lambda e, sl=sl: e.tensor_tensor(dm[sl][:], dm[sl][:], d1[sl][:], ALU.add),
                        waits=(("ve", vv),), incs=(("ve", 1),))
                vv2 = pr.c("ve")
                g = emit_ln_row(pr, st, dm[sl][:], do[sl][:], bufs, [("ve", vv2), ("st", max(0, pr.c("st") - 16))])
                pr.emit("sync", lambda e, ti=ti, sl=sl: e.dma_start(out=x2r[ti], in_=do[sl][:]),
                        waits=(("grd", g),), incs=(("st", 16),))
                rel_d[sl] = (("grd", g),)


def build_p2():
    nc = bass.Bass("TRN2", target_bir_lowering=False)
    DD = {}
    DD["x"] = nc.dram_tensor("x", [T, D], F32, kind="ExternalInput").ap()
    DD["x1"] = nc.dram_tensor("x1", [T, D], F32, kind="ExternalInput").ap()
    DD["wm1_til"] = nc.dram_tensor("wm1_til", [64, 128, 2048], F32, kind="ExternalInput").ap()
    DD["wm2_r"] = nc.dram_tensor("wm2_r", [64, 128, 2048], F32, kind="ExternalInput").ap()
    DD["ident"] = nc.dram_tensor("ident", [128, 128], F32, kind="ExternalInput").ap()
    DD["x2"] = nc.dram_tensor("x2", [T, D], F32, kind="ExternalOutput").ap()
    DD["m_s"] = nc.dram_tensor("m_s", [T, D], F32, kind="Internal").ap()
    es0 = ExitStack()
    pr = Prog(nc, es0)
    with es0:
        emit_p2(pr, nc, DD)
        pr.emit("gpsimd", lambda e: e.nop(), waits=(("st", pr.c("st")),))
        pr.build()
    return nc


NGTILES = 17


def build_gather():
    nc = bass.Bass("TRN2", target_bir_lowering=False)
    idx_my = nc.dram_tensor("idx_my", [NGTILES, 128, 2048], I32, kind="ExternalInput").ap()
    lut_d = nc.dram_tensor("lutrep", [128, 16], F32, kind="ExternalInput").ap()
    wcat = nc.dram_tensor("wcat", [NGTILES, 128, 2048], BF16, kind="ExternalOutput").ap()

    es = ExitStack()
    sb, _psum_unused = _mk_alloc(nc, es)
    idx_bf = [sb(f"idx_bf{i}", [128, 2048], BF16) for i in range(2)]
    gacc = [sb(f"gacc{i}", [128, 2048], BF16) for i in range(2)]
    eqt = sb("eqt", [128, 2048], BF16)
    lut_sb = sb("lut_sb", [128, 16], F32)

    pr = Prog(nc, es)
    pr.sem("ldw", "ldc", "ve", "st")
    with es:
        pr.emit("sync", lambda e: e.dma_start(out=lut_sb[:], in_=lut_d), incs=(("ldc", 16),))
        rel_idx = [0, 0]
        rel_acc = [0, 0]
        for j in range(NGTILES):
            sl = j % 2
            pr.emit("gpsimd", lambda e, j=j, sl=sl: e.dma_start(out=idx_bf[sl][:], in_=idx_my[j]),
                    waits=(("ve", rel_idx[sl]),), incs=(("ldw", 16),))
            vld = pr.c("ldw")
            pr.emit("vector", lambda e, sl=sl: e.memset(gacc[sl][:], 0.0),
                    waits=(("st", rel_acc[sl]),), incs=(("ve", 1),))
            for k in range(16):
                w = (("ldw", vld), ("ldc", 16)) if k == 0 else ()
                pr.emit("vector",
                        lambda e, k=k, sl=sl: e.tensor_scalar(eqt[:], idx_bf[sl][:], float(k),
                                                              lut_sb[:, k:k + 1], ALU.is_equal, ALU.mult),
                        waits=w)
                pr.emit("vector", lambda e, sl=sl: e.tensor_tensor(gacc[sl][:], gacc[sl][:], eqt[:], ALU.add),
                        incs=(("ve", 1),) if k == 15 else ())
            vg = pr.c("ve")
            rel_idx[sl] = vg
            pr.emit("sync", lambda e, j=j, sl=sl: e.dma_start(out=wcat[j], in_=gacc[sl][:]),
                    waits=(("ve", vg),), incs=(("st", 16),))
            rel_acc[sl] = pr.c("st")
        pr.emit("gpsimd", lambda e: e.nop(), waits=(("st", pr.c("st")),))
        pr.build()
    return nc


def emit_p3(pr, nc, DD, pump=None):
    x2 = DD["x2"]; wgA = DD["wgA"]; wuA = DD["wuA"]; wdR = DD["wdR"]
    id_d = DD["ident"]; x3 = DD["x3"]; m_s = DD["m_s"]
    x2r = x2.rearrange("(tt p) d -> tt p d", p=128)
    x3r = x3.rearrange("(tt p) d -> tt p d", p=128)
    msr = m_s.rearrange("(tt p) d -> tt p d", p=128)

    es = ExitStack()
    sb, psum = _mk_alloc(nc, es)

    bufs = {}
    bufs["xt"] = [sb(f"xt{i}", [128, D], F32) for i in range(2)]
    bufs["sqd"] = sb("sqd", [128, D], BF16)
    bufs["ssq"] = sb("ssq", [128, 3], F32)
    bufs["scl"] = sb("scl", [128, 1], F32)
    bufs["h_tm"] = sb("h_tm", [128, D], BF16)
    bufs["id_bf"] = sb("id_bf", [128, 128], BF16)
    bufs["lsq"] = sb("lsq", [128, D], BF16)
    bufs["lst"] = sb("lst", [128, 6], F32)
    bufs["ps_t"] = [psum("ps_t0", [128, 128], BF16), psum("ps_t1", [128, 128], BF16)]
    wgB = [sb(f"wgA{i}", [128, 2048], BF16) for i in range(2)]
    wuB = [sb(f"wuA{i}", [128, 2048], BF16) for i in range(2)]
    gs = sb("gs", [128, 1024], F32)
    us = sb("us", [128, 1024], F32)
    mst = [sb(f"mst{i}", [128, 256], F32) for i in range(2)]
    ps_g = [psum("ps_g0", [128, 512], F32), psum("ps_g1", [128, 512], F32)]
    ps_u = [psum("ps_u0", [128, 512], F32), psum("ps_u1", [128, 512], F32)]
    ps_c = [psum("ps_c0", [128, 256], F32), psum("ps_c1", [128, 256], F32)]

    pr.sem("ldx", "ldw", "ldc", "pe", "ve", "ac", "grd", "st", "ldo", "ldd", "evH")

    st = {"relA": [(), ()], "prev_chain": pr.c("grd"), "tp_ev": [], "tp_pe": pr.c("pe")}
    rel_w = [0, 0]

    with es:
        pr.emit("gpsimd", lambda e: e.dma_start(out=bufs["id_bf"][:], in_=id_d), incs=(("ldc", 16),))
        wc = (("ldc", pr.c("ldc")),)

        for H in range(2):
          with nc.sbuf_tensor(f"hact_fm{_UNIQ[0]}_{H}", [128, 43 * 1024], BF16) as hact_fm:
            with nc.sbuf_tensor(f"x2_fm{_UNIQ[0]}_{H}", [128, 16 * 1024], BF16) as x2_fm:
                for i in range(8):
                    ti = H * 8 + i
                    emit_tile_to_fm(pr, st, x2r, ti, x2_fm, 1024,
                                    lambda dt, i=i: dt * 1024 + i * 128, "cast", bufs, wc)
                vA_done = pr.c("ac")

                for oc in range(43):
                    sl = oc % 2
                    if pump is not None:
                        pump.ensure(2 * oc + 2)
                        gwait = (("gst", max(pump.store_counts[2 * oc], pump.store_counts[2 * oc + 1])),)
                    else:
                        gwait = ()
                    pr.emit("sync", lambda e, oc=oc, sl=sl: e.dma_start(out=wgB[sl][:], in_=wgA(oc)),
                            waits=(("pe", rel_w[sl]),) + gwait, incs=(("ldw", 16),))
                    pr.emit("sync", lambda e, oc=oc, sl=sl: e.dma_start(out=wuB[sl][:], in_=wuA(oc)),
                            incs=(("ldw", 16),))
                    vw = pr.c("ldw")
                    wb = [("ldw", vw), ("ac", vA_done), ("evH", pr.c("evH"))]
                    for n in range(2):
                        for kt in range(16):
                            pr.emit("tensor",
                                    lambda e, kt=kt, n=n, sl=sl: e.matmul(
                                        ps_g[n][:], wgB[sl][:, kt * 128:(kt + 1) * 128],
                                        x2_fm[:, kt * 1024 + n * 512: kt * 1024 + (n + 1) * 512],
                                        start=(kt == 0), stop=(kt == 15)),
                                    waits=tuple(w for w in wb if w[1] > 0) if (kt == 0 and n == 0) else ())
                    for n in range(2):
                        for kt in range(16):
                            pr.emit("tensor",
                                    lambda e, kt=kt, n=n, sl=sl: e.matmul(
                                        ps_u[n][:], wuB[sl][:, kt * 128:(kt + 1) * 128],
                                        x2_fm[:, kt * 1024 + n * 512: kt * 1024 + (n + 1) * 512],
                                        start=(kt == 0), stop=(kt == 15)),
                                    incs=(("pe", 1),) if (kt == 15 and n == 1) else ())
                    vmm = pr.c("pe")
                    rel_w[sl] = vmm
                    pr.emit("scalar", lambda e: e.activation(gs[:, 0:512], ps_g[0][:], AF.Silu),
                            waits=(("pe", vmm), ("evH", pr.c("evH"))), incs=(("ac", 1),))
                    pr.emit("scalar", lambda e: e.activation(gs[:, 512:1024], ps_g[1][:], AF.Silu),
                            incs=(("ac", 1),))
                    vgs = pr.c("ac")
                    pr.emit("vector", lambda e: e.tensor_copy(us[:, 0:512], ps_u[0][:]),
                            waits=(("pe", vmm),), incs=(("ve", 1),))
                    pr.emit("vector", lambda e: e.tensor_copy(us[:, 512:1024], ps_u[1][:]),
                            incs=(("ve", 1),))
                    pr.emit("vector",
                            lambda e, oc=oc: e.tensor_tensor(hact_fm[:, oc * 1024:(oc + 1) * 1024],
                                                             gs[:], us[:], ALU.mult),
                            waits=(("ac", vgs),), incs=(("evH", 1),))
                    if pump is not None:
                        pump.step_iters(6)
                vB_done = pr.c("evH")
                vB_pe = pr.c("pe")

            if pump is not None:
                pump.finish()
                gw_all = (("gst", pump.store_counts[128]),)
            else:
                gw_all = ()
            with nc.sbuf_tensor(f"wd_sb{_UNIQ[0]}_{H}", [128, 43 * 256], BF16) as wd_sb:
                wd_v = wd_sb[:].rearrange("p (kt c) -> p kt c", kt=43)
                vC_prev = vB_pe
                rel_psc3 = [pr.c("ve"), pr.c("ve")]
                for nn in range(8):
                    bnds = [(0, 8), (8, 16), (16, 24), (24, 32), (32, 40), (40, 43)]
                    ldo_parts = {}
                    for (a, b) in bnds:
                        pr.emit("gpsimd",
                                lambda e, nn=nn, a=a, b=b: e.dma_start(
                                    out=wd_v[:, a:b, :],
                                    in_=wdR(a, b)[:, :, nn * 256:(nn + 1) * 256].rearrange("kt p c -> p kt c")),
                                waits=((("pe", vC_prev),) if a == 0 else ()) + (gw_all if a == 0 else ()),
                                incs=(("ldo", 16),))
                        ldo_parts[a] = pr.c("ldo")
                    for tt in range(8):
                        ti = H * 8 + tt
                        sl = tt % 2
                        wpe = [("ldo", ldo_parts[0]), ("evH", vB_done), ("ve", rel_psc3[sl])]
                        for c in range(43):
                            if c == 0:
                                w_ = tuple(w for w in wpe if w[1] > 0)
                            elif c % 8 == 0 and c in ldo_parts:
                                w_ = (("ldo", ldo_parts[c]),)
                            else:
                                w_ = ()
                            pr.emit("tensor",
                                    lambda e, c=c, tt=tt, sl=sl: e.matmul(
                                        ps_c[sl][:], hact_fm[:, c * 1024 + tt * 128: c * 1024 + (tt + 1) * 128],
                                        wd_sb[:, c * 256:(c + 1) * 256],
                                        start=(c == 0), stop=(c == 42)),
                                    waits=w_,
                                    incs=(("pe", 1),) if c == 42 else ())
                        vpc = pr.c("pe")
                        pr.emit("vector", lambda e, sl=sl: e.tensor_copy(mst[sl][:], ps_c[sl][:]),
                                waits=(("pe", vpc), ("st", max(0, pr.c("st") - 16))), incs=(("ve", 1),))
                        rel_psc3[sl] = pr.c("ve")
                        vev = pr.c("ve")
                        pr.emit("sync", lambda e, ti=ti, nn=nn, sl=sl: e.dma_start(
                                    out=msr[ti, :, nn * 256:(nn + 1) * 256], in_=mst[sl][:]),
                                waits=(("ve", vev),), incs=(("st", 16),))
                    vC_prev = pr.c("pe")

        vst_C = pr.c("st")
        with ExitStack() as esd:
            dx = [esd.enter_context(nc.sbuf_tensor(f"dx{_UNIQ[0]}_{i}", [128, D], F32)) for i in range(2)]
            dm = [esd.enter_context(nc.sbuf_tensor(f"dm{_UNIQ[0]}_{i}", [128, D], F32)) for i in range(2)]
            do = [esd.enter_context(nc.sbuf_tensor(f"do{_UNIQ[0]}_{i}", [128, D], F32)) for i in range(2)]
            rel_d = [(), ()]
            for ti in range(16):
                sl = ti % 2
                pr.emit("sync", lambda e, ti=ti, sl=sl: e.dma_start(out=dx[sl][:], in_=x2r[ti]),
                        waits=(("st", vst_C),) + rel_d[sl], incs=(("ldd", 16),))
                pr.emit("sync", lambda e, ti=ti, sl=sl: e.dma_start(out=dm[sl][:], in_=msr[ti]),
                        incs=(("ldd", 16),))
                vld = pr.c("ldd")
                pr.emit("vector", lambda e, sl=sl: e.tensor_tensor(dm[sl][:], dm[sl][:], dx[sl][:], ALU.add),
                        waits=(("ldd", vld),), incs=(("ve", 1),))
                vv = pr.c("ve")
                g = emit_ln_row(pr, st, dm[sl][:], do[sl][:], bufs, [("ve", vv), ("st", max(0, pr.c("st") - 16))])
                pr.emit("sync", lambda e, ti=ti, sl=sl: e.dma_start(out=x3r[ti], in_=do[sl][:]),
                        waits=(("grd", g),), incs=(("st", 16),))
                rel_d[sl] = (("grd", g),)



def build_p3():
    nc = bass.Bass("TRN2", target_bir_lowering=False)
    DD = {}
    DD["x2"] = nc.dram_tensor("x2", [T, D], F32, kind="ExternalInput").ap()
    wg_t = nc.dram_tensor("wg_s", [43, 128, 2048], BF16, kind="ExternalInput").ap()
    wu_t = nc.dram_tensor("wu_s", [43, 128, 2048], BF16, kind="ExternalInput").ap()
    wd_t = nc.dram_tensor("wd_s", [43, 128, 2048], BF16, kind="ExternalInput").ap()
    DD["wgA"] = lambda oc: wg_t[oc]
    DD["wuA"] = lambda oc: wu_t[oc]
    DD["wdR"] = lambda a, b: wd_t[a:b]
    DD["ident"] = nc.dram_tensor("ident", [128, 128], F32, kind="ExternalInput").ap()
    DD["x3"] = nc.dram_tensor("x3", [T, D], F32, kind="ExternalOutput").ap()
    DD["m_s"] = nc.dram_tensor("m_s", [T, D], F32, kind="Internal").ap()
    es0 = ExitStack()
    pr = Prog(nc, es0)
    with es0:
        emit_p3(pr, nc, DD)
        pr.emit("gpsimd", lambda e: e.nop(), waits=(("st", pr.c("st")),))
        pr.build()
    return nc


NH = 16
NKV = 4
DH = 128
ISQD = 1.0 / float(np.sqrt(128.0))
P4_DEBUG = False
FL1 = True
FL2 = True
FL3 = True
P5_TRUNC = 0


def emit_p4(pr, nc, DD):
    x3 = DD["x3"]; qw = DD["qw_til"]; kw = DD["kw_til"]; vw = DD["vw_r"]; ow = DD["ow_r"]
    R_d = DD["Rrel"]; Dm_d = DD["Dmask"]; id_d = DD["ident"]; x4 = DD["x4"]
    y_s = DD["y_s"]; m_s = DD["m_s"]
    if P4_DEBUG:
        S_dump = DD["S_dump"]; P_dump = DD["P_dump"]
    x3r = x3.rearrange("(tt p) d -> tt p d", p=128)
    x4r = x4.rearrange("(tt p) d -> tt p d", p=128)
    msr = m_s.rearrange("(tt p) d -> tt p d", p=128)
    ysr = y_s.rearrange("(kc p) t -> kc p t", p=128)

    SLOPES = [2.0 ** (-(h + 1) / 2.0) for h in range(NH)]

    es = ExitStack()
    sb, psum = _mk_alloc(nc, es)

    bufs = {}
    xt0 = sb("xt0", [128, D], F32)
    bufs["xt"] = [xt0, xt0]
    bufs["sqd"] = sb("sqd", [128, D], BF16)
    bufs["ssq"] = sb("ssq", [128, 3], F32)
    bufs["scl"] = sb("scl", [128, 1], F32)
    bufs["h_tm"] = sb("h_tm", [128, D], BF16)
    bufs["id_bf"] = sb("id_bf", [128, 128], BF16)
    bufs["lsq"] = bufs["sqd"]
    bufs["lst"] = sb("lst", [128, 6], F32)
    k_fm = sb("k_fm", [128, 4 * 2048], BF16)
    v_tm = sb("v_tm", [128, 16 * 512], BF16)
    qh = [sb(f"qh{i}", [128, 2048], BF16) for i in range(2)]
    qwA = [sb(f"qwA{i}", [128, 2048], BF16) for i in range(2)]
    P4b = [sb(f"P4b{i}", [128, 4 * 2048], BF16) for i in range(2)]
    S_row = [sb(f"S_row{i}", [128, 2048], F32) for i in range(2)]
    pt_sb = [sb(f"pt_sb{i}", [128, 512], BF16) for i in range(2)]
    R_sb = sb("R_sb", [128, 2176], F32)
    Dm_sb = sb("Dm_sb", [128, 128], F32)
    smx = [sb(f"smx{i}", [128, 4], F32) for i in range(2)]
    flt = sb("flt", [128, 4], F32)
    yst = [sb(f"yst{i}", [128, 512], BF16) for i in range(2)]
    ps_s = [psum("ps_s0", [128, 512], F32), psum("ps_s1", [128, 512], F32)]
    ps_y = [psum("ps_y0", [128, 512], F32), psum("ps_y1", [128, 512], F32)]
    ps_p = [psum("ps_p0", [128, 512], F32), psum("ps_p1", [128, 512], F32)]

    pr.sem("ldx", "ldw", "ldc", "pe", "ve", "ac", "grd", "st", "ldo", "ldd", "evq", "sty", "dbg")

    st = {"relA": [(), ()], "prev_chain": pr.c("grd"), "tp_ev": [], "tp_pe": pr.c("pe"),
          "rel_srow": pr.c("grd"), "rel_p4b": pr.c("pe")}

    with es:
        pr.emit("gpsimd", lambda e: e.dma_start(out=bufs["id_bf"][:], in_=id_d), incs=(("ldc", 16),))
        pr.emit("sync", lambda e: e.dma_start(out=R_sb[:], in_=R_d), incs=(("ldc", 16),))
        pr.emit("sync", lambda e: e.dma_start(out=Dm_sb[:], in_=Dm_d), incs=(("ldc", 16),))
        wc = (("ldc", pr.c("ldc")),)

        with nc.sbuf_tensor(f"x3_fm{_UNIQ[0]}", [128, 16 * 2048], BF16) as x3_fm:
            with ExitStack() as espA:
                bufs["ps_t"] = [espA.enter_context(nc.psum_tensor(f"ps_tA{_UNIQ[0]}_{i}", [128, 128], BF16))
                                for i in range(2)]
                for ti in range(16):
                    emit_tile_to_fm(pr, st, x3r, ti, x3_fm, 2048,
                                    lambda dt, ti=ti: dt * 2048 + ti * 128, "clip", bufs, wc)
                    st["relA"] = [st["relA"][ti % 2], st["relA"][ti % 2]]
            ps_tp = [psum("ps_tp0", [128, 512], BF16), psum("ps_tp1", [128, 512], BF16)]
            vA_done = pr.c("ac")

            with nc.sbuf_tensor(f"vw_sb{_UNIQ[0]}", [128, 16 * 512], BF16) as vw_sb:
                vwv = vw_sb[:].rearrange("p (kt c) -> p kt c", kt=16)
                for part in range(2):
                    pr.emit("gpsimd", lambda e, part=part: e.dma_start(
                                out=vwv[:, part * 8:(part + 1) * 8, :],
                                in_=vw[part * 8:(part + 1) * 8].rearrange("kt p c -> p kt c")),
                            incs=(("ldw", 16),))
                vld_vw = pr.c("ldw")
                rel_ps = [0, 0]
                for kc in range(4):
                    sl = kc % 2
                    pr.emit("gpsimd", lambda e, kc=kc, sl=sl: e.dma_start(out=qwA[sl][:], in_=kw[kc]),
                            waits=(("pe", 0 if kc < 2 else pr.c("pe")),), incs=(("ldw", 16),))
                    vwk = pr.c("ldw")
                    for n in range(4):
                        psl = n % 2
                        w0 = [("ldw", vwk), ("ac", vA_done), ("ve", rel_ps[psl])]
                        for kt in range(16):
                            pr.emit("tensor",
                                    lambda e, kt=kt, n=n, sl=sl, psl=psl: e.matmul(
                                        ps_p[psl][:], qwA[sl][:, kt * 128:(kt + 1) * 128],
                                        x3_fm[:, kt * 2048 + n * 512: kt * 2048 + (n + 1) * 512],
                                        start=(kt == 0), stop=(kt == 15)),
                                    waits=tuple(w for w in w0 if w[1] > 0) if kt == 0 else (),
                                    incs=(("pe", 1),) if kt == 15 else ())
                        vpe = pr.c("pe")
                        pr.emit("vector",
                                lambda e, kc=kc, n=n, psl=psl: e.tensor_copy(
                                    k_fm[:, kc * 2048 + n * 512: kc * 2048 + (n + 1) * 512], ps_p[psl][:]),
                                waits=(("pe", vpe),), incs=(("ve", 1),))
                        rel_ps[psl] = pr.c("ve")
                for tt in range(16):
                    psl = tt % 2
                    w0 = [("ldw", vld_vw), ("ac", vA_done), ("ve", rel_ps[psl])]
                    for kt in range(16):
                        pr.emit("tensor",
                                lambda e, kt=kt, tt=tt, psl=psl: e.matmul(
                                    ps_p[psl][:], x3_fm[:, kt * 2048 + tt * 128: kt * 2048 + (tt + 1) * 128],
                                    vw_sb[:, kt * 512:(kt + 1) * 512],
                                    start=(kt == 0), stop=(kt == 15)),
                                waits=tuple(w for w in w0 if w[1] > 0) if kt == 0 else (),
                                incs=(("pe", 1),) if kt == 15 else ())
                    vpe = pr.c("pe")
                    pr.emit("vector",
                            lambda e, tt=tt, psl=psl: e.tensor_copy(
                                v_tm[:, tt * 512:(tt + 1) * 512], ps_p[psl][:]),
                            waits=(("pe", vpe),), incs=(("ve", 1),))
                    rel_ps[psl] = pr.c("ve")

            rel_qh = [0, 0]
            rel_qw = [pr.c("pe"), pr.c("pe")]
            rel_pss = [0, 0]
            rel_psy = [0, 0]
            rel_yst = [0, 0]
            rel_pt = [0, 0]
            rel_p4b = [pr.c("pe"), pr.c("pe")]
            rel_srow = [pr.c("grd"), pr.c("grd")]
            for h in range(NH):
                hsl = h % 2
                kvc = h // 4
                slope = SLOPES[h]
                pr.emit("gpsimd", lambda e, h=h, hsl=hsl: e.dma_start(out=qwA[hsl][:], in_=qw[h]),
                        waits=(("pe", rel_qw[hsl]),), incs=(("ldw", 16),))
                vwq = pr.c("ldw")
                for n in range(4):
                    psl = n % 2
                    w0 = [("ldw", vwq), ("ac", vA_done), ("ve", rel_ps[psl]), ("pe", rel_qh[hsl])]
                    for kt in range(16):
                        pr.emit("tensor",
                                lambda e, kt=kt, n=n, hsl=hsl, psl=psl: e.matmul(
                                    ps_p[psl][:], qwA[hsl][:, kt * 128:(kt + 1) * 128],
                                    x3_fm[:, kt * 2048 + n * 512: kt * 2048 + (n + 1) * 512],
                                    start=(kt == 0), stop=(kt == 15)),
                                waits=tuple(w for w in w0 if w[1] > 0) if kt == 0 else (),
                                incs=(("pe", 1),) if kt == 15 else ())
                    vpe = pr.c("pe")
                    pr.emit("vector",
                            lambda e, n=n, hsl=hsl, psl=psl: e.tensor_copy(
                                qh[hsl][:, n * 512:(n + 1) * 512], ps_p[psl][:]),
                            waits=(("pe", vpe),), incs=(("ve", 1),))
                    rel_ps[psl] = pr.c("ve")
                if FL3:
                    pr.emit("vector", lambda e: e.tensor_copy(flt[:, 2:3], flt[:, 3:4]), incs=(("ve", 1),))
                vq_ready = pr.c("ve")
                for s_ in range(2):
                    pr.emit("vector", lambda e, s_=s_: e.memset(P4b[s_][:], 0.0),
                            waits=(("pe", rel_p4b[s_]),), incs=(("ve", 1),))
                rel_qw[hsl] = 0

                vP_done_q = [0, 0, 0, 0]

                def emit_S_soft(qb):
                    qsl = qb % 2
                    for sq in range(4):
                        qt = qb * 4 + sq
                        ssl2 = qt % 2
                        Srow = S_row[ssl2]
                        smxq = smx[ssl2]
                        L = (qt + 1) * 128
                        nkn = (L + 511) // 512
                        for kn in range(nkn):
                            wdt = min(512, L - kn * 512)
                            ssl = kn % 2
                            w0 = [("ve", max(vq_ready, rel_pss[ssl]))]
                            pr.emit("tensor",
                                    lambda e, qt=qt, kn=kn, wdt=wdt, ssl=ssl, hsl=hsl, kvc=kvc: e.matmul(
                                        ps_s[ssl][:, 0:wdt], qh[hsl][:, qt * 128:(qt + 1) * 128],
                                        k_fm[:, kvc * 2048 + kn * 512: kvc * 2048 + kn * 512 + wdt],
                                        start=True, stop=True),
                                    waits=tuple(w for w in w0 if w[1] > 0),
                                    incs=(("pe", 1),))
                            vpe = pr.c("pe")
                            pr.emit("vector",
                                    lambda e, kn=kn, wdt=wdt, ssl=ssl, Srow=Srow: e.tensor_scalar(
                                        Srow[:, kn * 512: kn * 512 + wdt], ps_s[ssl][:, 0:wdt],
                                        ISQD, 10000.0, ALU.mult, ALU.min),
                                    waits=(("pe", vpe), ("grd", rel_srow[ssl2])), incs=(("ve", 1),))
                            rel_pss[ssl] = pr.c("ve")
                        pr.emit("vector",
                                lambda e, qt=qt, Srow=Srow, slope=slope: e.scalar_tensor_tensor(
                                    Srow[:, qt * 128: qt * 128 + 128], Dm_sb[:], slope,
                                    Srow[:, qt * 128: qt * 128 + 128], ALU.mult, ALU.add),
                                waits=wc, incs=())
                        if qt > 0:
                            pr.emit("vector",
                                    lambda e, qt=qt, Srow=Srow, slope=slope: e.scalar_tensor_tensor(
                                        Srow[:, 0: qt * 128], R_sb[:, 2047 - qt * 128: 2047], slope,
                                        Srow[:, 0: qt * 128], ALU.mult, ALU.add),
                                    incs=())
                        pr.emit("vector", lambda e, L=L, Srow=Srow, smxq=smxq: e.reduce_max(
                                    smxq[:, 0:1], Srow[:, 0:L], axis=AX.X),
                                incs=(("grd", 1),))
                        v1 = pr.c("grd")
                        pr.emit("scalar", lambda e, smxq=smxq: e.mul(smxq[:, 1:2], smxq[:, 0:1], -1.0),
                                waits=(("grd", v1),), incs=(("grd", 1),))
                        v2 = pr.c("grd")
                        pr.emit("scalar",
                                lambda e, sq=sq, L=L, qsl=qsl, Srow=Srow, smxq=smxq: e.activation(
                                    P4b[qsl][:, sq * 2048: sq * 2048 + L], Srow[:, 0:L], AF.Exp,
                                    bias=smxq[:, 1:2], accum_out=smxq[:, 2:3]),
                                waits=(("grd", v2), ("pe", rel_p4b[qsl])), incs=(("grd", 1),))
                        v3 = pr.c("grd")
                        rel_srow[ssl2] = v3
                        pr.emit("vector", lambda e, smxq=smxq: e.reciprocal(smxq[:, 3:4], smxq[:, 2:3]),
                                waits=(("grd", v3),), incs=(("grd", 1),))
                        v4 = pr.c("grd")
                        pr.emit("vector",
                                lambda e, sq=sq, L=L, qsl=qsl, smxq=smxq: e.tensor_scalar_mul(
                                    P4b[qsl][:, sq * 2048: sq * 2048 + L],
                                    P4b[qsl][:, sq * 2048: sq * 2048 + L], smxq[:, 3:4]),
                                waits=(("grd", v4),), incs=(("grd", 1),))
                    pr.emit("vector", lambda e, smxq=smxq: e.tensor_copy(smxq[:, 0:1], smxq[:, 3:4]),
                            incs=(("grd", 1),))
                    vP_done_q[qb] = pr.c("grd")

                def emit_tpPV(qb):
                    qsl = qb % 2
                    njmax = qb * 4 + 4
                    ysl = qb % 2
                    for j in range(njmax):
                        ptl = j % 2
                        for sq in range(4):
                            w0 = [("grd", vP_done_q[qb]), ("pe", rel_pt[ptl])]
                            if sq == 0:
                                w0.append(("ac", st["tp_ev"][-2] if len(st["tp_ev"]) >= 2 else 0))
                            pr.emit("tensor",
                                    lambda e, sq=sq, j=j, qsl=qsl, ptl=ptl: e.transpose(
                                        ps_tp[ptl][:, sq * 128:(sq + 1) * 128],
                                        P4b[qsl][:, sq * 2048 + j * 128: sq * 2048 + (j + 1) * 128],
                                        bufs["id_bf"][:]),
                                    waits=tuple(w for w in w0 if w[1] > 0),
                                    incs=(("pe", 1),) if sq == 3 else ())
                        vpe = pr.c("pe")
                        pr.emit("scalar",
                                lambda e, ptl=ptl: e.copy(pt_sb[ptl][:], ps_tp[ptl][:]),
                                waits=(("pe", vpe),), incs=(("ac", 1),))
                        st["tp_ev"].append(pr.c("ac"))
                        pr.emit("scalar", lambda e: e.copy(flt[:, 0:1], flt[:, 1:2]), incs=(("ac", 1),))
                        vpt = pr.c("ac")
                        w0 = [("ac", vpt), ("ve", rel_psy[ysl] if j == 0 else 0)]
                        pr.emit("tensor",
                                lambda e, j=j, ptl=ptl, ysl=ysl, njmax=njmax, kvc=kvc: e.matmul(
                                    ps_y[ysl][:], v_tm[:, j * 512 + kvc * 128: j * 512 + (kvc + 1) * 128],
                                    pt_sb[ptl][:], start=(j == 0), stop=(j == njmax - 1)),
                                waits=tuple(w for w in w0 if w[1] > 0), incs=(("pe", 1),))
                        rel_pt[ptl] = pr.c("pe")
                    vpv = pr.c("pe")
                    wsl = qb % 2
                    pr.emit("vector",
                            lambda e, ysl=ysl, wsl=wsl: e.tensor_scalar(
                                yst[wsl][:], ps_y[ysl][:], 10000.0, -10000.0, ALU.min, ALU.max),
                            waits=(("pe", vpv), ("sty", rel_yst[wsl])), incs=(("ve", 1),))
                    rel_psy[ysl] = pr.c("ve")
                    vyev = pr.c("ve")
                    pr.emit("sync",
                            lambda e, qb=qb, h=h, wsl=wsl: e.dma_start(
                                out=ysr[h, :, qb * 512:(qb + 1) * 512], in_=yst[wsl][:]),
                            waits=(("ve", vyev),), incs=(("sty", 16),))
                    rel_yst[wsl] = pr.c("sty")
                    rel_p4b[qsl] = pr.c("pe")

                for qb in range(4):
                    emit_S_soft(qb)
                    if qb >= 1:
                        emit_tpPV(qb - 1)
                emit_tpPV(3)
                rel_qh[hsl] = pr.c("pe")
                rel_qw[hsl] = pr.c("pe")
            vattn_st = pr.c("sty")
            vattn_pe = pr.c("pe")

        with ExitStack() as eso:
            ow_sb = eso.enter_context(nc.sbuf_tensor(f"ow_sb{_UNIQ[0]}", [128, 16 * 2048], BF16))
            ytil = [eso.enter_context(nc.sbuf_tensor(f"ytil{_UNIQ[0]}_{i}", [128, 16 * 128], BF16)) for i in range(2)]
            mst = [eso.enter_context(nc.sbuf_tensor(f"mst{_UNIQ[0]}_{i}", [128, 512], F32)) for i in range(2)]
            owv = ow_sb[:].rearrange("p (kt c) -> p kt c", kt=16)
            for kc in range(16):
                pr.emit("gpsimd", lambda e, kc=kc: e.dma_start(out=owv[:, kc:kc + 1, :],
                                                               in_=ow[kc:kc + 1].rearrange("kt p c -> p kt c")),
                        waits=(("pe", vattn_pe),) if kc == 0 else (), incs=(("ldo", 16),))
            vow = pr.c("ldo")
            rel_yt = [0, 0]
            rel_mst = [0, 0]
            for tt in range(16):
                ysl = tt % 2
                yv = ytil[ysl][:].rearrange("p (kc c) -> p kc c", kc=16)
                pr.emit("sync", lambda e, tt=tt, ysl=ysl, yv=yv: e.dma_start(
                            out=yv[:, 0:8, :],
                            in_=ysr[0:8, :, tt * 128:(tt + 1) * 128].rearrange("kc p c -> p kc c")),
                        waits=(("sty", vattn_st), ("pe", max(vattn_pe, rel_yt[ysl]))), incs=(("ldd", 16),))
                pr.emit("sync", lambda e, tt=tt, ysl=ysl, yv=yv: e.dma_start(
                            out=yv[:, 8:16, :],
                            in_=ysr[8:16, :, tt * 128:(tt + 1) * 128].rearrange("kc p c -> p kc c")),
                        incs=(("ldd", 16),))
                vldy = pr.c("ldd")
                for dc in range(4):
                    psl = dc % 2
                    w0 = [("ldd", vldy), ("ldo", vow), ("ve", rel_psy[psl])]
                    for kc in range(16):
                        pr.emit("tensor",
                                lambda e, kc=kc, dc=dc, ysl=ysl, psl=psl: e.matmul(
                                    ps_y[psl][:], ytil[ysl][:, kc * 128:(kc + 1) * 128],
                                    ow_sb[:, kc * 2048 + dc * 512: kc * 2048 + (dc + 1) * 512],
                                    start=(kc == 0), stop=(kc == 15)),
                                waits=tuple(w for w in w0 if w[1] > 0) if kc == 0 else (),
                                incs=(("pe", 1),) if kc == 15 else ())
                    vpe = pr.c("pe")
                    pr.emit("vector", lambda e, psl=psl, msl=dc % 2: e.tensor_copy(mst[msl][:], ps_y[psl][:]),
                            waits=(("pe", vpe), ("st", rel_mst[dc % 2])), incs=(("ve", 1),))
                    rel_psy[psl] = pr.c("ve")
                    vev = pr.c("ve")
                    pr.emit("sync", lambda e, tt=tt, dc=dc, msl=dc % 2: e.dma_start(
                                out=msr[tt, :, dc * 512:(dc + 1) * 512], in_=mst[msl][:]),
                            waits=(("ve", vev),), incs=(("st", 16),))
                    rel_mst[dc % 2] = pr.c("st")
                rel_yt[ysl] = pr.c("pe")

        vst_C = pr.c("st")
        with ExitStack() as esd:
            dx = [esd.enter_context(nc.sbuf_tensor(f"dx{_UNIQ[0]}_{i}", [128, D], F32)) for i in range(2)]
            dm = [esd.enter_context(nc.sbuf_tensor(f"dm{_UNIQ[0]}_{i}", [128, D], F32)) for i in range(2)]
            do = [esd.enter_context(nc.sbuf_tensor(f"do{_UNIQ[0]}_{i}", [128, D], F32)) for i in range(2)]
            rel_d = [(), ()]
            for ti in range(16):
                sl = ti % 2
                pr.emit("sync", lambda e, ti=ti, sl=sl: e.dma_start(out=dx[sl][:], in_=x3r[ti]),
                        waits=(("st", vst_C),) + rel_d[sl], incs=(("ldd", 16),))
                pr.emit("sync", lambda e, ti=ti, sl=sl: e.dma_start(out=dm[sl][:], in_=msr[ti]),
                        incs=(("ldd", 16),))
                vld = pr.c("ldd")
                pr.emit("vector", lambda e, sl=sl: e.tensor_tensor(dm[sl][:], dm[sl][:], dx[sl][:], ALU.add),
                        waits=(("ldd", vld),), incs=(("ve", 1),))
                vv = pr.c("ve")
                g = emit_ln_row(pr, st, dm[sl][:], do[sl][:], bufs, [("ve", vv), ("st", max(0, pr.c("st") - 16))])
                pr.emit("sync", lambda e, ti=ti, sl=sl: e.dma_start(out=x4r[ti], in_=do[sl][:]),
                        waits=(("grd", g),), incs=(("st", 16),))
                rel_d[sl] = (("grd", g),)



def build_p4():
    nc = bass.Bass("TRN2", target_bir_lowering=False)
    DD = {}
    DD["x3"] = nc.dram_tensor("x3", [T, D], F32, kind="ExternalInput").ap()
    DD["qw_til"] = nc.dram_tensor("qw_til", [16, 128, 2048], F32, kind="ExternalInput").ap()
    DD["kw_til"] = nc.dram_tensor("kw_til", [4, 128, 2048], F32, kind="ExternalInput").ap()
    DD["vw_r"] = nc.dram_tensor("vw_r", [16, 128, 512], F32, kind="ExternalInput").ap()
    DD["ow_r"] = nc.dram_tensor("ow_r", [16, 128, 2048], F32, kind="ExternalInput").ap()
    DD["Rrel"] = nc.dram_tensor("Rrel", [128, 2176], F32, kind="ExternalInput").ap()
    DD["Dmask"] = nc.dram_tensor("Dmask", [128, 128], F32, kind="ExternalInput").ap()
    DD["ident"] = nc.dram_tensor("ident", [128, 128], F32, kind="ExternalInput").ap()
    DD["x4"] = nc.dram_tensor("x4", [T, D], F32, kind="ExternalOutput").ap()
    dbg = "ExternalOutput" if P4_DEBUG else "Internal"
    DD["y_s"] = nc.dram_tensor("y_s", [2048, 2048], BF16, kind=dbg).ap()
    DD["m_s"] = nc.dram_tensor("m_s", [T, D], F32, kind=dbg).ap()
    if P4_DEBUG:
        DD["S_dump"] = nc.dram_tensor("S_dump", [16, 128, 2048], F32, kind="ExternalOutput").ap()
        DD["P_dump"] = nc.dram_tensor("P_dump", [4, 128, 8192], BF16, kind="ExternalOutput").ap()
    es0 = ExitStack()
    pr = Prog(nc, es0)
    with es0:
        emit_p4(pr, nc, DD)
        pr.emit("gpsimd", lambda e: e.nop(), waits=(("st", pr.c("st")),))
        pr.build()
    return nc


def emit_p5(pr, nc, DD):
    x4 = DD["x4"]; gw = DD["gw_til"]; w1t = DD["w1_til"]; w3t = DD["w3_til"]
    w2r = DD["w2_r"]; ws1t = DD["ws1_til"]; ws3t = DD["ws3_til"]; ws2r = DD["ws2_r"]
    id_d = DD["ident"]; out = DD["out"]
    x4r = x4.rearrange("(tt p) d -> tt p d", p=128)
    outr = out.rearrange("(tt p) d -> tt p d", p=128)

    es = ExitStack()
    sb, psum = _mk_alloc(nc, es)

    bufs = {}
    bufs["xt"] = [sb(f"xt{i}", [128, D], F32) for i in range(2)]
    bufs["lst"] = sb("lst", [128, 6], F32)
    x4_fm = sb("x4_fm", [128, 16 * 1024], BF16)
    gbuf = sb("gbuf", [128, 16 * 128], F32)
    bufs["lsq"] = gbuf
    ghi = sb("ghi", [128, 2048], BF16)
    glo = sb("glo", [128, 2048], BF16)
    xlo = sb("xlo", [128, 2048], BF16)
    combs = sb("combs", [128, 8 * 8], F32)
    sc = sb("sc", [128, 8], F32)
    p_sc = sb("p_sc", [128, 8], F32)
    oh = sb("oh", [128, 8], F32)
    smx = sb("smx", [128, 4], F32)
    flt = sb("flt", [128, 4], F32)
    y_acc = sb("y_acc", [128, 8 * 2048], F32)
    hact = sb("hact", [128, 8 * 1024], BF16)
    g1 = sb("g1", [128, 1024], F32)
    u1 = sb("u1", [128, 1024], F32)
    w1A = [sb(f"w1A{i}", [128, 2048], BF16) for i in range(2)]
    w3A = [sb(f"w3A{i}", [128, 2048], BF16) for i in range(2)]
    w2sb = [sb(f"w2sb{i}", [128, 8 * 512], BF16) for i in range(2)]

    ps_g = [psum("ps_g0", [128, 512], F32), psum("ps_g1", [128, 512], F32)]
    ps_u = [psum("ps_u0", [128, 512], F32), psum("ps_u1", [128, 512], F32)]
    ps_c = [psum("ps_c0", [128, 512], F32), psum("ps_c1", [128, 512], F32)]

    pr.sem("ldx", "ldw", "ldc", "pe", "ve", "ac", "grd", "st", "ldo", "ldd", "evH")

    st = {"relA": [(), ()], "prev_chain": pr.c("grd")}
    rel_w = [pr.c("pe"), pr.c("pe")]
    rel_w2 = [pr.c("pe"), pr.c("pe")]
    rel_psc = [pr.c("ve"), pr.c("ve")]
    rel_doLN = [(), ()]

    id_sb = sb("id_sb", [128, 128], F32)

    with es:
        pr.emit("sync", lambda e: e.dma_start(out=gbuf[:], in_=gw), incs=(("ldc", 16),))
        pr.emit("sync", lambda e: e.dma_start(out=id_sb[:], in_=id_d), incs=(("ldc", 16),))
        pr.emit("gpsimd", lambda e: e.dma_start(out=ghi[:], in_=gw), incs=(("ldc", 16),))
        wc = (("ldc", pr.c("ldc")),)
        pr.emit("vector", lambda e: e.scalar_tensor_tensor(glo[:], ghi[:], -1.0, gbuf[:], ALU.mult, ALU.add),
                waits=wc, incs=(("grd", 1),))
        v_glo = pr.c("grd")
        st["prev_chain"] = pr.c("grd")

        vE_prev_ve = pr.c("ve")
        for H in range(2):
            tp_ev = []
            tp_ve = []
            vgate_prev = 0
            for i in range(8):
                ti = H * 8 + i
                sl = ti % 2
                pr.emit("sync", lambda e, ti=ti, sl=sl: e.dma_start(out=bufs["xt"][sl][:], in_=x4r[ti]),
                        waits=st["relA"][sl], incs=(("ldx", 16),))
                vx = pr.c("ldx")
                for dt in range(16):
                    w0 = [("ldx", vx)] + list(wc)
                    if len(tp_ev) >= 2:
                        w0.append(("ac", tp_ev[-2]))
                        w0.append(("ve", tp_ve[-2]))
                    if H == 0 and i == 0 and dt == 0:
                        w0.append(("grd", v_glo))
                    if H > 0 and i == 0 and dt == 0:
                        w0.append(("evH", pr.c("evH")))
                        w0.append(("ve", vE_prev_ve))
                    pr.emit("tensor",
                            lambda e, dt=dt, sl=sl: e.transpose(
                                ps_c[dt % 2][:, 0:128], bufs["xt"][sl][:, dt * 128:(dt + 1) * 128], id_sb[:]),
                            waits=tuple(w0), incs=(("pe", 1),))
                    vpe = pr.c("pe")
                    pr.emit("scalar", lambda e, dt=dt: e.copy(gbuf[:, dt * 128:(dt + 1) * 128], ps_c[dt % 2][:, 0:128]),
                            waits=(("pe", vpe),), incs=(("ac", 1),))
                    tp_ev.append(pr.c("ac"))
                    pr.emit("vector", lambda e, dt=dt, i=i: e.tensor_copy(
                                x4_fm[:, dt * 1024 + i * 128: dt * 1024 + (i + 1) * 128],
                                ps_c[dt % 2][:, 0:128]),
                            waits=(("pe", vpe), ("ac", tp_ev[-1])), incs=(("ve", 1),))
                    tp_ve.append(pr.c("ve"))
                pr.emit("scalar", lambda e: e.copy(flt[:, 0:1], flt[:, 1:2]), incs=(("ac", 1),))
                vtp = pr.c("ac")
                st["relA"][sl] = (("ac", vtp), ("ve", pr.c("ve")))
                for dt in ([] if P5_TRUNC in (4, 5) else range(16)):
                    pr.emit("vector",
                            lambda e, dt=dt, i=i: e.scalar_tensor_tensor(
                                xlo[:, dt * 128:(dt + 1) * 128],
                                x4_fm[:, dt * 1024 + i * 128: dt * 1024 + (i + 1) * 128], -1.0,
                                gbuf[:, dt * 128:(dt + 1) * 128], ALU.mult, ALU.add),
                            waits=(("ac", vtp),) if dt == 0 else ())
                pr.emit("vector", lambda e: e.tensor_copy(flt[:, 2:3], flt[:, 3:4]), incs=(("ve", 1),))
                vxlo = pr.c("ve")
                if P5_TRUNC in (3, 4, 5):
                    continue
                n_mm = 0
                for kt in range(16):
                    xhi_s = lambda e_, kt=kt, i=i: x4_fm[:, kt * 1024 + i * 128: kt * 1024 + (i + 1) * 128]
                    for (asel, bsel) in ((0, 0), (0, 1), (1, 0)):
                        is_first = n_mm == 0
                        is_last = n_mm == 47
                        pr.emit("tensor",
                                lambda e, kt=kt, i=i, asel=asel, bsel=bsel, f=is_first, l=is_last: e.matmul(
                                    ps_g[0][:, 0:128],
                                    (x4_fm[:, kt * 1024 + i * 128: kt * 1024 + (i + 1) * 128] if asel == 0
                                     else xlo[:, kt * 128:(kt + 1) * 128]),
                                    (ghi if bsel == 0 else glo)[:, kt * 128:(kt + 1) * 128],
                                    start=f, stop=l),
                                waits=(("ve", vxlo), ("grd", vgate_prev)) if is_first else (),
                                incs=(("pe", 1),) if is_last else ())
                        n_mm += 1
                vg = pr.c("pe")
                pr.emit("vector", lambda e: e.tensor_copy(sc[:], ps_g[0][:, 0:8]),
                        waits=(("pe", vg), ("grd", st["prev_chain"])), incs=(("grd", 1),))
                v1 = pr.c("grd")
                pr.emit("vector", lambda e: e.reduce_max(smx[:, 0:1], sc[:], axis=AX.X),
                        waits=(("grd", v1),), incs=(("grd", 1),))
                v2 = pr.c("grd")
                pr.emit("scalar", lambda e: e.mul(smx[:, 1:2], smx[:, 0:1], -1.0),
                        waits=(("grd", v2),), incs=(("grd", 1),))
                v3 = pr.c("grd")
                pr.emit("scalar", lambda e: e.activation(p_sc[:], sc[:], AF.Exp, bias=smx[:, 1:2],
                                                         accum_out=smx[:, 2:3]),
                        waits=(("grd", v3),), incs=(("grd", 1),))
                v4 = pr.c("grd")
                pr.emit("vector", lambda e: e.reciprocal(smx[:, 3:4], smx[:, 2:3]),
                        waits=(("grd", v4),), incs=(("grd", 1),))
                v5 = pr.c("grd")
                pr.emit("vector", lambda e: e.tensor_scalar_mul(p_sc[:], p_sc[:], smx[:, 3:4]),
                        waits=(("grd", v5),), incs=(("grd", 1),))
                v6 = pr.c("grd")
                pr.emit("vector", lambda e: e.reduce_max(smx[:, 0:1], p_sc[:], axis=AX.X),
                        waits=(("grd", v6),), incs=(("grd", 1),))
                v7 = pr.c("grd")
                pr.emit("vector", lambda e: e.tensor_single_scalar(oh[:], p_sc[:], smx[:, 0:1], ALU.is_equal),
                        waits=(("grd", v7),), incs=(("grd", 1),))
                v8 = pr.c("grd")
                pr.emit("vector", lambda e, ti=ti: e.tensor_scalar_mul(combs[:, (ti % 8) * 8:(ti % 8) * 8 + 8],
                                                                       oh[:], smx[:, 0:1]),
                        waits=(("grd", v8),), incs=(("grd", 1),))
                v9 = pr.c("grd")
                pr.emit("vector", lambda e: e.scalar_tensor_tensor(sc[:], oh[:], -1e30, p_sc[:], ALU.mult, ALU.add),
                        waits=(("grd", v9),), incs=(("grd", 1),))
                v10 = pr.c("grd")
                pr.emit("vector", lambda e: e.reduce_max(smx[:, 1:2], sc[:], axis=AX.X),
                        waits=(("grd", v10),), incs=(("grd", 1),))
                v11 = pr.c("grd")
                pr.emit("vector", lambda e: e.tensor_single_scalar(oh[:], sc[:], smx[:, 1:2], ALU.is_equal),
                        waits=(("grd", v11),), incs=(("grd", 1),))
                v12 = pr.c("grd")
                pr.emit("vector", lambda e, ti=ti: e.scalar_tensor_tensor(
                            combs[:, (ti % 8) * 8:(ti % 8) * 8 + 8], oh[:], smx[:, 1:2],
                            combs[:, (ti % 8) * 8:(ti % 8) * 8 + 8], ALU.mult, ALU.add),
                        waits=(("grd", v12),), incs=(("grd", 1),))
                st["prev_chain"] = pr.c("grd")
                vgate_prev = pr.c("grd")
            pr.emit("vector", lambda e: e.tensor_copy(flt[:, 2:3], flt[:, 3:4]), incs=(("ve", 1),))
            vA_ve = pr.c("ve")
            vcombs = pr.c("grd")

            for e_i in ([] if P5_TRUNC == 1 else (range(1) if P5_TRUNC == 2 else range(9))):
                shared = e_i == 8
                l1 = (lambda ic, e_i=e_i: ws1t[ic]) if shared else (lambda ic, e_i=e_i: w1t[e_i, ic])
                l3 = (lambda ic, e_i=e_i: ws3t[ic]) if shared else (lambda ic, e_i=e_i: w3t[e_i, ic])
                for ic in range(8):
                    sl = ic % 2
                    pr.emit("gpsimd", lambda e, ic=ic, sl=sl, l1=l1: e.dma_start(
                                out=w1A[sl][:, 0:1024], in_=l1(ic)[:, 0:1024]),
                            waits=(("pe", rel_w[sl]),), incs=(("ldw", 16),))
                    vw1a = pr.c("ldw")
                    pr.emit("gpsimd", lambda e, ic=ic, sl=sl, l1=l1: e.dma_start(
                                out=w1A[sl][:, 1024:2048], in_=l1(ic)[:, 1024:2048]),
                            incs=(("ldw", 16),))
                    vw1b = pr.c("ldw")
                    pr.emit("gpsimd", lambda e, ic=ic, sl=sl, l3=l3: e.dma_start(out=w3A[sl][:], in_=l3(ic)),
                            incs=(("ldw", 16),))
                    vw = pr.c("ldw")
                    wb = [("ldw", vw1a), ("ve", vA_ve), ("evH", pr.c("evH")), ("grd", vcombs)]
                    for n in range(2):
                        for kt in range(16):
                            if kt == 0 and n == 0:
                                w_ = tuple(w for w in wb if w[1] > 0)
                            elif kt == 8 and n == 0:
                                w_ = (("ldw", vw1b),)
                            else:
                                w_ = ()
                            pr.emit("tensor",
                                    lambda e, kt=kt, n=n, sl=sl: e.matmul(
                                        ps_g[n][:], w1A[sl][:, kt * 128:(kt + 1) * 128],
                                        x4_fm[:, kt * 1024 + n * 512: kt * 1024 + (n + 1) * 512],
                                        start=(kt == 0), stop=(kt == 15)),
                                    waits=w_)
                    for n in range(2):
                        for kt in range(16):
                            pr.emit("tensor",
                                    lambda e, kt=kt, n=n, sl=sl: e.matmul(
                                        ps_u[n][:], w3A[sl][:, kt * 128:(kt + 1) * 128],
                                        x4_fm[:, kt * 1024 + n * 512: kt * 1024 + (n + 1) * 512],
                                        start=(kt == 0), stop=(kt == 15)),
                                    waits=(("ldw", vw),) if (kt == 0 and n == 0) else (),
                                    incs=(("pe", 1),) if (kt == 15 and n == 1) else ())
                    vmm = pr.c("pe")
                    rel_w[sl] = vmm
                    pr.emit("scalar", lambda e: e.activation(g1[:, 0:512], ps_g[0][:], AF.Silu),
                            waits=(("pe", vmm), ("evH", pr.c("evH"))), incs=(("ac", 1),))
                    pr.emit("scalar", lambda e: e.activation(g1[:, 512:1024], ps_g[1][:], AF.Silu),
                            incs=(("ac", 1),))
                    vgs = pr.c("ac")
                    pr.emit("vector", lambda e: e.tensor_copy(u1[:, 0:512], ps_u[0][:]),
                            waits=(("pe", vmm),), incs=(("ve", 1),))
                    pr.emit("vector", lambda e: e.tensor_copy(u1[:, 512:1024], ps_u[1][:]),
                            incs=(("ve", 1),))
                    pr.emit("vector", lambda e, ic=ic: e.tensor_tensor(hact[:, ic * 1024:(ic + 1) * 1024],
                                                                      g1[:], u1[:], ALU.mult),
                            waits=(("ac", vgs),), incs=(("evH", 1),))
                vhact = pr.c("evH")
                pr.emit("vector", lambda e: e.tensor_copy(flt[:, 2:3], flt[:, 3:4]), incs=(("evH", 1),))
                vhact = pr.c("evH")

                l2 = (lambda dn, e_i=e_i: ws2r[:, :, dn * 512:(dn + 1) * 512]) if shared else \
                     (lambda dn, e_i=e_i: w2r[e_i, :, :, dn * 512:(dn + 1) * 512])
                for dn in range(4):
                    w2l = dn % 2
                    pr.emit("gpsimd", lambda e, dn=dn, w2l=w2l, l2=l2: e.dma_start(
                                out=w2sb[w2l][:].rearrange("p (kt c) -> p kt c", kt=8),
                                in_=l2(dn).rearrange("kt p c -> p kt c")),
                            waits=(("pe", rel_w2[w2l]),), incs=(("ldo", 16),))
                    vwo = pr.c("ldo")
                    for tt in range(8):
                        csl = tt % 2
                        w0 = [("ldo", vwo), ("evH", vhact), ("ve", rel_psc[csl]), ("grd", vcombs)]
                        for ic in range(8):
                            pr.emit("tensor",
                                    lambda e, ic=ic, tt=tt, csl=csl, w2l=w2l: e.matmul(
                                        ps_c[csl][:], hact[:, ic * 1024 + tt * 128: ic * 1024 + (tt + 1) * 128],
                                        w2sb[w2l][:, ic * 512:(ic + 1) * 512],
                                        start=(ic == 0), stop=(ic == 7)),
                                    waits=tuple(w for w in w0 if w[1] > 0) if ic == 0 else (),
                                    incs=(("pe", 1),) if ic == 7 else ())
                        vpc = pr.c("pe")
                        ya = lambda: y_acc[:, tt * 2048 + dn * 512: tt * 2048 + (dn + 1) * 512]
                        if e_i == 0:
                            pr.emit("vector", lambda e, tt=tt, dn=dn, csl=csl: e.tensor_scalar_mul(
                                        y_acc[:, tt * 2048 + dn * 512: tt * 2048 + (dn + 1) * 512],
                                        ps_c[csl][:], combs[:, (tt) * 8 + 0: (tt) * 8 + 1]),
                                    waits=(("pe", vpc),), incs=(("ve", 1),))
                            rel_psc[csl] = pr.c("ve")
                        elif not shared:
                            pr.emit("vector", lambda e, tt=tt, dn=dn, csl=csl, e_i=e_i: e.scalar_tensor_tensor(
                                        y_acc[:, tt * 2048 + dn * 512: tt * 2048 + (dn + 1) * 512],
                                        ps_c[csl][:], combs[:, tt * 8 + e_i: tt * 8 + e_i + 1],
                                        y_acc[:, tt * 2048 + dn * 512: tt * 2048 + (dn + 1) * 512],
                                        ALU.mult, ALU.add),
                                    waits=(("pe", vpc),), incs=(("ve", 1),))
                            rel_psc[csl] = pr.c("ve")
                        else:
                            pr.emit("vector", lambda e, tt=tt, dn=dn, csl=csl: e.tensor_tensor(
                                        y_acc[:, tt * 2048 + dn * 512: tt * 2048 + (dn + 1) * 512],
                                        y_acc[:, tt * 2048 + dn * 512: tt * 2048 + (dn + 1) * 512],
                                        ps_c[csl][:], ALU.add),
                                    waits=(("pe", vpc),), incs=(("ve", 1),))
                            rel_psc[csl] = pr.c("ve")
                    rel_w2[w2l] = pr.c("pe")
            vE_done = pr.c("ve")

            for tt in ([] if P5_TRUNC else range(8)):
                ti = H * 8 + tt
                pr.emit("sync", lambda e, ti=ti: e.dma_start(out=bufs["xt"][0][:], in_=x4r[ti]),
                        waits=rel_doLN[0] + st["relA"][0], incs=(("ldd", 16),))
                vld = pr.c("ldd")
                pr.emit("vector", lambda e, tt=tt: e.tensor_tensor(
                            y_acc[:, tt * 2048:(tt + 1) * 2048], y_acc[:, tt * 2048:(tt + 1) * 2048],
                            bufs["xt"][0][:], ALU.add),
                        waits=(("ldd", vld), ("ve", vE_done)), incs=(("ve", 1),))
                vv = pr.c("ve")
                g = emit_ln_row(pr, st, y_acc[:, tt * 2048:(tt + 1) * 2048], bufs["xt"][1][:], bufs,
                                [("ve", vv), ("st", pr.c("st")), ("ldx", pr.c("ldx"))] + list(st["relA"][1]))
                pr.emit("sync", lambda e, ti=ti: e.dma_start(out=outr[ti], in_=bufs["xt"][1][:]),
                        waits=(("grd", g),), incs=(("st", 16),))
                rel_doLN[0] = (("grd", g), ("ve", vv))
            pr.emit("vector", lambda e: e.tensor_copy(flt[:, 2:3], flt[:, 3:4]),
                    waits=(("grd", st["prev_chain"]),), incs=(("evH", 1),))
            vE_prev_ve = vE_done
            st["relA"] = [(("grd", pr.c("grd")), ("ldd", pr.c("ldd"))), (("grd", pr.c("grd")),)]



def build_p5():
    nc = bass.Bass("TRN2", target_bir_lowering=False)
    DD = {}
    DD["x4"] = nc.dram_tensor("x4", [T, D], F32, kind="ExternalInput").ap()
    DD["gw_til"] = nc.dram_tensor("gw_til", [128, 2048], F32, kind="ExternalInput").ap()
    DD["w1_til"] = nc.dram_tensor("w1_til", [8, 8, 128, 2048], F32, kind="ExternalInput").ap()
    DD["w3_til"] = nc.dram_tensor("w3_til", [8, 8, 128, 2048], F32, kind="ExternalInput").ap()
    DD["w2_r"] = nc.dram_tensor("w2_r", [8, 8, 128, 2048], F32, kind="ExternalInput").ap()
    DD["ws1_til"] = nc.dram_tensor("ws1_til", [8, 128, 2048], F32, kind="ExternalInput").ap()
    DD["ws3_til"] = nc.dram_tensor("ws3_til", [8, 128, 2048], F32, kind="ExternalInput").ap()
    DD["ws2_r"] = nc.dram_tensor("ws2_r", [8, 128, 2048], F32, kind="ExternalInput").ap()
    DD["ident"] = nc.dram_tensor("ident", [128, 128], F32, kind="ExternalInput").ap()
    DD["out"] = nc.dram_tensor("out", [T, D], F32, kind="ExternalOutput").ap()
    es0 = ExitStack()
    pr = Prog(nc, es0)
    with es0:
        emit_p5(pr, nc, DD)
        pr.emit("gpsimd", lambda e: e.nop(), waits=(("st", pr.c("st")),))
        pr.build()
    return nc


class GatherPump:

    def __init__(self, pr, nc, es, idx_cat, lut_d, wcat, ntiles=129):
        self.pr = pr
        self.idx_cat = idx_cat
        self.wcat = wcat
        self.lut_d = lut_d
        self.ntiles = ntiles
        sbp = lambda n, s, d: es.enter_context(nc.sbuf_tensor(n, s, d))
        self.idx_bf = [sbp(f"g_idx{i}", [128, 2048], BF16) for i in range(2)]
        self.gacc = [sbp(f"g_acc{i}", [128, 2048], BF16) for i in range(2)]
        self.eqt = sbp("g_eqt", [128, 2048], BF16)
        self.lut_sb = sbp("g_lut", [128, 16], F32)
        pr.sem("gld", "gev", "gst", "glc")
        pr.emit("sync", lambda e: e.dma_start(out=self.lut_sb[:], in_=lut_d), incs=(("glc", 16),))
        self.j = 0
        self.k = 0
        self.rel_idx = [0, 0]
        self.rel_acc = [0, 0]
        self.store_counts = [0] * ntiles
        self._load_issued = -1

    def _issue_load(self, j):
        pr = self.pr
        sl = j % 2
        pr.emit("gpsimd", lambda e, j=j, sl=sl: e.dma_start(out=self.idx_bf[sl][:], in_=self.idx_cat[j]),
                waits=(("gev", self.rel_idx[sl]),), incs=(("gld", 16),))
        self._load_issued = j

    def step_iters(self, n):
        pr = self.pr
        while n > 0 and self.j < self.ntiles:
            j = self.j
            sl = j % 2
            if self._load_issued < j:
                self._issue_load(j)
            if self.k == 0:
                if j + 1 < self.ntiles and self._load_issued < j + 1:
                    self._issue_load(j + 1)
                pr.emit("vector", lambda e, sl=sl: e.memset(self.gacc[sl][:], 0.0),
                        waits=(("gst", self.rel_acc[sl]),))
            kk = self.k
            w = (("gld", 16 * (j + 1)), ("glc", pr.c("glc"))) if kk == 0 else ()
            pr.emit("vector",
                    lambda e, kk=kk, sl=sl: e.tensor_scalar(self.eqt[:], self.idx_bf[sl][:], float(kk),
                                                            self.lut_sb[:, kk:kk + 1], ALU.is_equal, ALU.mult),
                    waits=w)
            pr.emit("vector", lambda e, sl=sl: e.tensor_tensor(self.gacc[sl][:], self.gacc[sl][:],
                                                               self.eqt[:], ALU.add),
                    incs=(("gev", 1),) if kk == 15 else ())
            self.k += 1
            n -= 1
            if self.k == 16:
                vg = pr.c("gev")
                self.rel_idx[sl] = vg
                pr.emit("sync", lambda e, j=j, sl=sl: e.dma_start(out=self.wcat[j], in_=self.gacc[sl][:]),
                        waits=(("gev", vg),), incs=(("gst", 16),))
                self.store_counts[j] = pr.c("gst")
                self.rel_acc[sl] = pr.c("gst")
                self.j += 1
                self.k = 0

    def ensure(self, ntile):
        while self.j < min(ntile, self.ntiles):
            self.step_iters(16)

    def flush_tile(self):
        while self.j < self.ntiles and (self.k != 0 or self._load_issued >= self.j):
            self.step_iters(16)

    def rebuffer(self, nc, es):
        sbp = lambda n, s, d: es.enter_context(nc.sbuf_tensor(n, s, d))
        _UNIQ[0] += 1
        tag = f"g{_UNIQ[0]}_"
        self.idx_bf = [sbp(tag + f"idx{i}", [128, 2048], BF16) for i in range(2)]
        self.gacc = [sbp(tag + f"acc{i}", [128, 2048], BF16) for i in range(2)]
        self.eqt = sbp(tag + "eqt", [128, 2048], BF16)
        self.lut_sb = sbp(tag + "lut", [128, 16], F32)
        self.pr.emit("sync", lambda e: e.dma_start(out=self.lut_sb[:], in_=self.lut_d), incs=(("glc", 16),))
        self._load_issued = self.j - 1

    def step(self, ktiles=1):
        self.step_iters(16 * ktiles)

    def finish(self):
        self.ensure(self.ntiles)


def build_full():
    nc = bass.Bass("TRN2", target_bir_lowering=False)
    DD = {}
    ein = lambda n, s, d: nc.dram_tensor(n, s, d, kind="ExternalInput").ap()
    intl = lambda n, s, d: nc.dram_tensor(n, s, d, kind="Internal").ap()
    DD["x"] = ein("x", [T, D], F32)
    DD["win_til"] = ein("win_til", [64, 128, 2048], F32)
    DD["wout_r"] = ein("wout_r", [32, 128, 2048], F32)
    DD["cw"] = ein("cw", [128, 128], F32)
    DD["cb"] = ein("cb", [128, 32], F32)
    DD["ident"] = ein("ident", [128, 128], F32)
    DD["wm1_til"] = ein("wm1_til", [64, 128, 2048], F32)
    DD["wm2_r"] = ein("wm2_r", [64, 128, 2048], F32)
    idx_cat = ein("idx_cat", [129, 128, 2048], I32)
    lut_d = ein("lutrep", [128, 16], F32)
    DD["qw_til"] = ein("qw_til", [16, 128, 2048], F32)
    DD["kw_til"] = ein("kw_til", [4, 128, 2048], F32)
    DD["vw_r"] = ein("vw_r", [16, 128, 512], F32)
    DD["ow_r"] = ein("ow_r", [16, 128, 2048], F32)
    DD["Rrel"] = ein("Rrel", [128, 2176], F32)
    DD["Dmask"] = ein("Dmask", [128, 128], F32)
    DD["gw_til"] = ein("gw_til", [128, 2048], F32)
    DD["w1_til"] = ein("w1_til", [8, 8, 128, 2048], F32)
    DD["w3_til"] = ein("w3_til", [8, 8, 128, 2048], F32)
    DD["w2_r"] = ein("w2_r", [8, 8, 128, 2048], F32)
    DD["ws1_til"] = ein("ws1_til", [8, 128, 2048], F32)
    DD["ws3_til"] = ein("ws3_til", [8, 128, 2048], F32)
    DD["ws2_r"] = ein("ws2_r", [8, 128, 2048], F32)
    DD["out"] = nc.dram_tensor("out", [T, D], F32, kind="ExternalOutput").ap()
    DD["x1"] = intl("x1", [T, D], F32)
    DD["x2"] = intl("x2", [T, D], F32)
    DD["x3"] = intl("x3", [T, D], F32)
    DD["x4"] = intl("x4", [T, D], F32)
    DD["m_s"] = intl("m_s", [T, D], F32)
    DD["y_s"] = intl("y_s", [2048, 2048], BF16)
    wcat = intl("wcat", [129, 128, 2048], BF16)
    DD["wgA"] = lambda oc: wcat[2 * oc]
    DD["wuA"] = lambda oc: wcat[2 * oc + 1]
    DD["wdR"] = lambda a, b: wcat[86 + a:86 + b]

    es0 = ExitStack()
    pr = Prog(nc, es0)
    with es0:
        with ExitStack() as esg1:
            pump = GatherPump(pr, nc, esg1, idx_cat, lut_d, wcat, ntiles=129)
            emit_p1(pr, nc, DD, pump=pump)
            pr.barrier()
            emit_p2(pr, nc, DD, pump=pump)
            pr.barrier()
            emit_p3(pr, nc, DD, pump=pump)
            pr.barrier()
        emit_p4(pr, nc, DD)
        pr.barrier()
        emit_p5(pr, nc, DD)
        pr.emit("gpsimd", lambda e: e.nop(), waits=(("st", pr.c("st")), ("gst", pr.c("gst"))))
        pr.build()
    return nc


FUSED = True


def kernel(**inputs):
    prep = host_prep(inputs)
    xfull = np.ascontiguousarray(inputs["L0_attn_x"])

    if FUSED:
        ncf = build_full()
        maps = []
        for b in range(NCORES):
            maps.append({
                "x": np.ascontiguousarray(xfull[b]),
                "win_til": prep["win_til"], "wout_r": prep["wout_r"],
                "cw": prep["cw"], "cb": prep["cb"], "ident": prep["ident"],
                "wm1_til": prep["wm1_til"], "wm2_r": prep["wm2_r"],
                "idx_cat": prep["idx_cat"], "lutrep": prep["lutrep"],
                "qw_til": prep["qw_til"], "kw_til": prep["kw_til"],
                "vw_r": prep["vw_r"], "ow_r": prep["ow_r"],
                "Rrel": prep["Rrel"], "Dmask": prep["Dmask"],
                "gw_til": prep["gw_til"], "w1_til": prep["w1_til"],
                "w3_til": prep["w3_til"], "w2_r": prep["w2_r"],
                "ws1_til": prep["ws1_til"], "ws3_til": prep["ws3_til"],
                "ws2_r": prep["ws2_r"],
            })
        res = run_bass_kernel_spmd(ncf, maps, core_ids=list(range(NCORES)))
        return np.stack([res.results[b]["out"] for b in range(NCORES)])

    nc1 = build_p1()
    maps = []
    for b in range(NCORES):
        maps.append({
            "x": np.ascontiguousarray(xfull[b]),
            "win_til": prep["win_til"],
            "wout_r": prep["wout_r"],
            "cw": prep["cw"],
            "cb": prep["cb"],
            "ident": prep["ident"],
        })
    res = run_bass_kernel_spmd(nc1, maps, core_ids=list(range(NCORES)))
    x1 = np.stack([res.results[b]["x1"] for b in range(NCORES)])

    nc2 = build_p2()
    maps = []
    for b in range(NCORES):
        maps.append({
            "x": np.ascontiguousarray(xfull[b]),
            "x1": x1[b],
            "wm1_til": prep["wm1_til"],
            "wm2_r": prep["wm2_r"],
            "ident": prep["ident"],
        })
    res = run_bass_kernel_spmd(nc2, maps, core_ids=list(range(NCORES)))
    x2 = np.stack([res.results[b]["x2"] for b in range(NCORES)])

    idx_cat = prep["idx_cat"]
    idx_pad = np.zeros((NCORES, NGTILES, 128, 2048), np.int32)
    for j in range(129):
        idx_pad[j % 8, j // 8] = idx_cat[j]
    ncg = build_gather()
    maps = [{"idx_my": idx_pad[b], "lutrep": prep["lutrep"]} for b in range(NCORES)]
    res = run_bass_kernel_spmd(ncg, maps, core_ids=list(range(NCORES)))
    wcat = np.zeros((129, 128, 2048), ml_dtypes.bfloat16)
    for j in range(129):
        wcat[j] = res.results[j % 8]["wcat"][j // 8]
    wg_s, wu_s, wd_s = wcat[0:86:2], wcat[1:86:2], wcat[86:129]

    nc3 = build_p3()
    maps = []
    for b in range(NCORES):
        maps.append({
            "x2": x2[b],
            "wg_s": wg_s,
            "wu_s": wu_s,
            "wd_s": wd_s,
            "ident": prep["ident"],
        })
    res = run_bass_kernel_spmd(nc3, maps, core_ids=list(range(NCORES)))
    x3 = np.stack([res.results[b]["x3"] for b in range(NCORES)])

    nc4 = build_p4()
    maps = []
    for b in range(NCORES):
        maps.append({
            "x3": x3[b],
            "qw_til": prep["qw_til"],
            "kw_til": prep["kw_til"],
            "vw_r": prep["vw_r"],
            "ow_r": prep["ow_r"],
            "Rrel": prep["Rrel"],
            "Dmask": prep["Dmask"],
            "ident": prep["ident"],
        })
    res = run_bass_kernel_spmd(nc4, maps, core_ids=list(range(NCORES)))
    x4 = np.stack([res.results[b]["x4"] for b in range(NCORES)])

    nc5 = build_p5()
    maps = []
    for b in range(NCORES):
        maps.append({
            "x4": x4[b],
            "gw_til": prep["gw_til"],
            "w1_til": prep["w1_til"],
            "w3_til": prep["w3_til"],
            "w2_r": prep["w2_r"],
            "ws1_til": prep["ws1_til"],
            "ws3_til": prep["ws3_til"],
            "ws2_r": prep["ws2_r"],
            "ident": prep["ident"],
        })
    res = run_bass_kernel_spmd(nc5, maps, core_ids=list(range(NCORES)))
    return np.stack([res.results[b]["out"] for b in range(NCORES)])


# version 17 (final)
# speedup vs baseline: 1.1291x; 1.1291x over previous
import numpy as np
import ml_dtypes
from contextlib import ExitStack
import concourse.bass as bass
import concourse.mybir as mybir
from concourse.bass_utils import run_bass_kernel_spmd

F32 = mybir.dt.float32
BF16 = mybir.dt.bfloat16
I32 = mybir.dt.int32
AF = mybir.ActivationFunctionType
ALU = mybir.AluOpType
AX = mybir.AxisListType

T = 2048
D = 2048
NCORES = 8
SQRTD = float(np.sqrt(2048.0))

ENGINES = ("sync", "gpsimd", "tensor", "vector", "scalar")


class Prog:

    def __init__(self, nc, es):
        self.nc = nc
        self.es = es
        self.ops = {e: [] for e in ENGINES}
        self.sems = {}
        self.ctr = {}

    def sem(self, *names):
        for name in names:
            if name not in self.sems:
                self.sems[name] = self.es.enter_context(self.nc.semaphore(name))
                self.ctr[name] = 0

    def emit(self, eng, fn, waits=(), incs=()):
        assert len(incs) <= 1, "hardware allows at most one sync update per instruction"
        for s, d in incs:
            self.ctr[s] += d
        self.ops[eng].append((fn, tuple(waits), tuple(incs)))

    def c(self, semname):
        return self.ctr[semname]

    def barrier(self):
        waits = tuple((s, self.ctr[s]) for s in self.ctr)
        for eng in ENGINES:
            self.emit(eng, None, waits=waits)

    def build(self):
        def mk(eng_ops):
            def section(e):
                for fn, waits, incs in eng_ops:
                    for s, v in waits:
                        if v > 0:
                            e.wait_ge(self.sems[s], v)
                    if fn is None:
                        continue
                    inst = fn(e)
                    for s, d in incs:
                        inst.then_inc(self.sems[s], d)
            return section

        with self.nc.Block() as block:
            for eng in ENGINES:
                if self.ops[eng]:
                    getattr(block, eng)(mk(self.ops[eng]))



_UNIQ = [0]


def _mk_alloc(nc, es):
    _UNIQ[0] += 1
    tag = f"u{_UNIQ[0]}_"
    sb = lambda n, s, d: es.enter_context(nc.sbuf_tensor(tag + n, s, d))
    ps = lambda n, s, d: es.enter_context(nc.psum_tensor(tag + n, s, d))
    return sb, ps

def _pretile_lhsT(W):
    out_, in_ = W.shape
    noc, nkt = out_ // 128, in_ // 128
    A = W.T.reshape(nkt, 128, noc, 128).transpose(2, 1, 0, 3).reshape(noc, 128, nkt * 128)
    return np.ascontiguousarray(A)


def _pretile_rhs(W):
    out_, in_ = W.shape
    return np.ascontiguousarray(W.T.reshape(in_ // 128, 128, out_))


def host_prep(inp):
    p = {}
    p["win_til"] = _pretile_lhsT(inp["L0_attn_W_in"])
    p["wout_r"] = _pretile_rhs(inp["L0_attn_W_out"])
    p["wm1_til"] = _pretile_lhsT(inp["L0_attn_W_mlp1"])
    p["wm2_r"] = _pretile_rhs(inp["L0_attn_W_mlp2"])
    cw = inp["L0_attn_conv_w"][:, 0, :]
    p["cw"] = np.ascontiguousarray(cw.reshape(32, 128, 4).transpose(1, 0, 2).reshape(128, 128))
    p["cb"] = np.ascontiguousarray(inp["L0_attn_conv_b"].reshape(32, 128).T)
    p["ident"] = np.eye(128, dtype=np.float32)
    gidx = _pretile_lhsT(inp["L0_ffn_gate_idx"])
    uidx = _pretile_lhsT(inp["L0_ffn_up_idx"])
    didx = _pretile_rhs(inp["L0_ffn_down_idx"])
    inter = np.empty((86, 128, 2048), np.int32)
    inter[0::2] = gidx
    inter[1::2] = uidx
    p["idx_cat"] = np.concatenate([inter, didx], axis=0)
    p["lutrep"] = np.ascontiguousarray(np.tile(inp["L0_ffn_lut"][None, :], (128, 1)).astype(np.float32))
    p["qw_til"] = _pretile_lhsT(inp["L1_attn_q_w"])
    p["kw_til"] = _pretile_lhsT(inp["L1_attn_k_w"])
    p["vw_r"] = _pretile_rhs(inp["L1_attn_v_w"])
    p["ow_r"] = _pretile_rhs(inp["L1_attn_o_w"])
    cc, qq = np.meshgrid(np.arange(2176, dtype=np.float32), np.arange(128, dtype=np.float32))
    p["Rrel"] = np.ascontiguousarray((cc - 2047.0) - qq)
    jj, q2 = np.meshgrid(np.arange(128, dtype=np.float32), np.arange(128, dtype=np.float32))
    p["Dmask"] = np.ascontiguousarray(np.where(jj <= q2, jj - q2, np.float32(-1e38)))
    gwt = _pretile_rhs(inp["L1_ffn_gate_w"])
    gpad = np.zeros((16, 128, 128), np.float32)
    gpad[:, :, 0:8] = gwt
    p["gw_til"] = np.ascontiguousarray(gpad.transpose(1, 0, 2).reshape(128, 2048))
    p["w1_til"] = np.stack([_pretile_lhsT(inp["L1_ffn_w1"][e]) for e in range(8)])
    p["w3_til"] = np.stack([_pretile_lhsT(inp["L1_ffn_w3"][e]) for e in range(8)])
    p["w2_r"] = np.stack([_pretile_rhs(inp["L1_ffn_w2"][e]) for e in range(8)])
    p["ws1_til"] = _pretile_lhsT(inp["L1_ffn_ws1"])
    p["ws3_til"] = _pretile_lhsT(inp["L1_ffn_ws3"])
    p["ws2_r"] = _pretile_rhs(inp["L1_ffn_ws2"])
    return p


def emit_p1(pr, nc, DD, pump=None):
    x = DD["x"]; win = DD["win_til"]; wout = DD["wout_r"]
    cw_d = DD["cw"]; cb_d = DD["cb"]; id_d = DD["ident"]; x1 = DD["x1"]
    xr = x.rearrange("(tt p) d -> tt p d", p=128)
    x1r = x1.rearrange("(tt p) d -> tt p d", p=128)

    es = ExitStack()
    sb, psum = _mk_alloc(nc, es)

    h_fm = sb("h_fm", [128, 16 * 1024], BF16)
    y_fm = sb("y_fm", [128, 32 * 1024], BF16)
    xt = [sb(f"xt{i}", [128, D], F32) for i in range(2)]
    sqd = sb("sqd", [128, D], BF16)
    ssq = sb("ssq", [128, 3], F32)
    scl = sb("scl", [128, 1], F32)
    h_tm = sb("h_tm", [128, D], BF16)
    id_bf = sb("id_bf", [128, 128], BF16)
    wA = [sb(f"wA{i}", [128, 2048], BF16) for i in range(2)]
    wB = [sb(f"wB{i}", [128, 2048], BF16) for i in range(2)]
    cw_sb = sb("cw_sb", [128, 128], F32)
    cb_sb = sb("cb_sb", [128, 32], F32)
    hist = sb("hist", [128, 96], F32)
    xp_buf = sb("xp_buf", [128, 1027], F32)
    xc_acc = sb("xc_acc", [128, 1024], F32)
    xcs = sb("xcs", [128, 1024], F32)
    zs = sb("zs", [128, 1024], F32)
    wo_sb = sb("wo_sb", [128, 32 * 256], BF16)
    wo_v = wo_sb[:].rearrange("p (kt c) -> p kt c", kt=32)
    xres = [sb(f"xres{i}", [128, 256], F32) for i in range(2)]
    x1sb = [sb(f"x1sb{i}", [128, 256], F32) for i in range(2)]

    ps_xp = [psum("ps_xp0", [128, 512], F32), psum("ps_xp1", [128, 512], F32)]
    ps_z = [psum("ps_z0", [128, 512], F32), psum("ps_z1", [128, 512], F32)]
    ps_c = [psum("ps_c0", [128, 512], F32), psum("ps_c1", [128, 512], F32)]
    ps_t = [psum("ps_t0", [128, 128], BF16), psum("ps_t1", [128, 128], BF16)]

    pr.sem("ldx", "ldw", "ldc", "pe", "ve", "ac", "grd", "st", "evB", "ldr", "ldo")

    with es:
        pr.emit("sync", lambda e: e.dma_start(out=cw_sb[:], in_=cw_d), incs=(("ldc", 16),))
        pr.emit("sync", lambda e: e.dma_start(out=cb_sb[:], in_=cb_d), incs=(("ldc", 16),))
        pr.emit("gpsimd", lambda e: e.dma_start(out=id_bf[:], in_=id_d), incs=(("ldc", 16),))
        wc = (("ldc", pr.c("ldc")),)

        relA = [(), ()]
        relB_state = [0, 0]
        rel_w = [0, 0]
        rel_xres = [0, 0]
        rel_x1sb = [0, 0]
        tpose_ev = []
        prev_chain = pr.c("grd")

        for H in range(2):
            for i in range(8):
                ti = H * 8 + i
                sl = ti % 2
                pr.emit("sync", lambda e, ti=ti, sl=sl: e.dma_start(out=xt[sl][:], in_=xr[ti]),
                        waits=relA[sl], incs=(("ldx", 16),))
                vx = pr.c("ldx")
                pr.emit("scalar", lambda e, sl=sl: e.activation(sqd[:], xt[sl][:], AF.Square, accum_out=ssq[:, 0:1]),
                        waits=(("ldx", vx), ("grd", prev_chain)), incs=(("grd", 1),))
                v1 = pr.c("grd")
                pr.emit("scalar", lambda e: e.activation(ssq[:, 1:2], ssq[:, 0:1], AF.Sqrt),
                        waits=(("grd", v1),), incs=(("grd", 1),))
                v2 = pr.c("grd")
                pr.emit("vector", lambda e: e.tensor_scalar_max(ssq[:, 2:3], ssq[:, 1:2], 1e-12),
                        waits=(("grd", v2),), incs=(("grd", 1),))
                v3 = pr.c("grd")
                pr.emit("vector", lambda e: e.reciprocal(scl[:], ssq[:, 2:3]),
                        waits=(("grd", v3),), incs=(("grd", 1),))
                g = pr.c("grd")
                prev_chain = g
                hw = [("grd", g)]
                if tpose_ev:
                    hw.append(("pe", pr.c("pe")))
                pr.emit("vector",
                        lambda e, sl=sl: e.tensor_scalar(h_tm[:], xt[sl][:], scl[:], SQRTD, ALU.mult, ALU.mult),
                        waits=tuple(hw), incs=(("ve", 1),))
                vh = pr.c("ve")
                relA[sl] = (("grd", g), ("ve", vh))
                for dt in range(16):
                    w = [("ve", vh)] + list(wc)
                    if len(tpose_ev) >= 2:
                        w.append(("ac", tpose_ev[-2]))
                    pr.emit("tensor",
                            lambda e, dt=dt: e.transpose(ps_t[dt % 2][:], h_tm[:, dt * 128:(dt + 1) * 128], id_bf[:]),
                            waits=tuple(w), incs=(("pe", 1),))
                    vpe = pr.c("pe")
                    pr.emit("scalar",
                            lambda e, dt=dt, i=i: e.copy(h_fm[:, dt * 1024 + i * 128: dt * 1024 + (i + 1) * 128],
                                                          ps_t[dt % 2][:]),
                            waits=(("pe", vpe),), incs=(("ac", 1),))
                    tpose_ev.append(pr.c("ac"))
            vA_done = pr.c("ac")

            for c in range(32):
                sl = c % 2
                pr.emit("gpsimd", lambda e, c=c, sl=sl: e.dma_start(out=wA[sl][:], in_=win[c]),
                        waits=(("pe", rel_w[sl]),), incs=(("ldw", 16),))
                pr.emit("gpsimd", lambda e, c=c, sl=sl: e.dma_start(out=wB[sl][:], in_=win[c + 32]),
                        incs=(("ldw", 16),))
                vw = pr.c("ldw")
                wb = [("ldw", vw), ("ac", vA_done)]
                if relB_state[0] > 0:
                    wb += [("ac", relB_state[0]), ("ve", relB_state[1])]
                for n in range(2):
                    for kt in range(16):
                        pr.emit("tensor",
                                lambda e, kt=kt, n=n, sl=sl: e.matmul(
                                    ps_xp[n][:], wA[sl][:, kt * 128:(kt + 1) * 128],
                                    h_fm[:, kt * 1024 + n * 512: kt * 1024 + (n + 1) * 512],
                                    start=(kt == 0), stop=(kt == 15)),
                                waits=tuple(wb) if (kt == 0 and n == 0) else ())
                for n in range(2):
                    for kt in range(16):
                        pr.emit("tensor",
                                lambda e, kt=kt, n=n, sl=sl: e.matmul(
                                    ps_z[n][:], wB[sl][:, kt * 128:(kt + 1) * 128],
                                    h_fm[:, kt * 1024 + n * 512: kt * 1024 + (n + 1) * 512],
                                    start=(kt == 0), stop=(kt == 15)),
                                incs=(("pe", 1),) if (kt == 15 and n == 1) else ())
                vmm = pr.c("pe")
                rel_w[sl] = vmm
                vy_prev = pr.c("evB")
                relB_state[0] = pr.c("ac") + 2
                relB_state[1] = pr.c("ve") + 2
                pr.emit("scalar", lambda e: e.activation(zs[:, 0:512], ps_z[0][:], AF.Silu),
                        waits=(("pe", vmm), ("evB", vy_prev)), incs=(("ac", 1),))
                pr.emit("scalar", lambda e: e.activation(zs[:, 512:1024], ps_z[1][:], AF.Silu),
                        incs=(("ac", 1),))
                vz = pr.c("ac")
                pr.emit("vector", lambda e: e.tensor_copy(xp_buf[:, 3:515], ps_xp[0][:]),
                        waits=(("pe", vmm), ("evB", vy_prev)), incs=(("ve", 1),))
                pr.emit("vector", lambda e: e.tensor_copy(xp_buf[:, 515:1027], ps_xp[1][:]),
                        incs=(("ve", 1),))
                if H == 0:
                    pr.emit("vector", lambda e: e.memset(xp_buf[:, 0:3], 0.0), incs=(("ve", 1),))
                else:
                    pr.emit("vector", lambda e, c=c: e.tensor_copy(xp_buf[:, 0:3], hist[:, c * 3:c * 3 + 3]),
                            incs=(("ve", 1),))
                vxp = pr.c("ve")
                pr.emit("vector",
                        lambda e, c=c: e.tensor_scalar(xc_acc[:], xp_buf[:, 0:1024],
                                                       cw_sb[:, c * 4:c * 4 + 1], cb_sb[:, c:c + 1],
                                                       ALU.mult, ALU.add),
                        waits=(("ve", vxp),) + wc, incs=(("ve", 1),))
                for j in (1, 2, 3):
                    vprev = pr.c("ve")
                    pr.emit("vector",
                            lambda e, c=c, j=j: e.scalar_tensor_tensor(xc_acc[:], xp_buf[:, j:1024 + j],
                                                                       cw_sb[:, c * 4 + j:c * 4 + j + 1],
                                                                       xc_acc[:], ALU.mult, ALU.add),
                            waits=(("ve", vprev),), incs=(("ve", 1),))
                vconv = pr.c("ve")
                if H == 0:
                    pr.emit("vector", lambda e, c=c: e.tensor_copy(hist[:, c * 3:c * 3 + 3], xp_buf[:, 1024:1027]),
                            incs=(("ve", 1),))
                pr.emit("scalar", lambda e: e.activation(xcs[:], xc_acc[:], AF.Silu),
                        waits=(("ve", vconv),), incs=(("ac", 1),))
                vxs = pr.c("ac")
                pr.emit("vector",
                        lambda e, c=c: e.tensor_tensor(y_fm[:, c * 1024:(c + 1) * 1024], xcs[:], zs[:], ALU.mult),
                        waits=(("ac", vxs),), incs=(("evB", 1),))
                if pump is not None:
                    pump.step_iters(5)
            vB_done = pr.c("evB")

            vC_prev = pr.c("pe")
            for n in range(8):
                for part in range(4):
                    pr.emit("gpsimd",
                            lambda e, n=n, part=part: e.dma_start(
                                out=wo_v[:, part * 8:(part + 1) * 8, :],
                                in_=wout[part * 8:(part + 1) * 8, :, n * 256:(n + 1) * 256]
                                    .rearrange("kt p c -> p kt c")),
                            waits=(("pe", vC_prev),) if part == 0 else (), incs=(("ldo", 16),))
                vwo = pr.c("ldo")
                for tt in range(8):
                    ti = H * 8 + tt
                    sl = tt % 2
                    pr.emit("sync", lambda e, ti=ti, n=n, sl=sl: e.dma_start(
                                out=xres[sl][:], in_=xr[ti, :, n * 256:(n + 1) * 256]),
                            waits=(("ve", rel_xres[sl]),), incs=(("ldr", 16),))
                    vr = pr.c("ldr")
                    wpe = [("ldo", vwo), ("evB", vB_done), ("ve", pr.c("ve"))]
                    for c in range(32):
                        pr.emit("tensor",
                                lambda e, c=c, tt=tt, sl=sl: e.matmul(
                                    ps_c[sl][:, 0:256], y_fm[:, c * 1024 + tt * 128: c * 1024 + (tt + 1) * 128],
                                    wo_sb[:, c * 256:(c + 1) * 256],
                                    start=(c == 0), stop=(c == 31)),
                                waits=tuple(wpe) if c == 0 else (), incs=(("pe", 1),) if c == 31 else ())
                    vpc = pr.c("pe")
                    pr.emit("vector",
                            lambda e, sl=sl: e.tensor_tensor(x1sb[sl][:], ps_c[sl][:, 0:256], xres[sl][:], ALU.add),
                            waits=(("pe", vpc), ("ldr", vr), ("st", rel_x1sb[sl])), incs=(("ve", 1),))
                    vx1 = pr.c("ve")
                    rel_xres[sl] = vx1
                    pr.emit("sync",
                            lambda e, ti=ti, n=n, sl=sl: e.dma_start(
                                out=x1r[ti, :, n * 256:(n + 1) * 256], in_=x1sb[sl][:]),
                            waits=(("ve", vx1),), incs=(("st", 16),))
                    rel_x1sb[sl] = pr.c("st")
                vC_prev = pr.c("pe")
                if pump is not None:
                    pump.step_iters(2)


def build_p1():
    nc = bass.Bass("TRN2", target_bir_lowering=False)
    DD = {}
    DD["x"] = nc.dram_tensor("x", [T, D], F32, kind="ExternalInput").ap()
    DD["win_til"] = nc.dram_tensor("win_til", [64, 128, 2048], F32, kind="ExternalInput").ap()
    DD["wout_r"] = nc.dram_tensor("wout_r", [32, 128, 2048], F32, kind="ExternalInput").ap()
    DD["cw"] = nc.dram_tensor("cw", [128, 128], F32, kind="ExternalInput").ap()
    DD["cb"] = nc.dram_tensor("cb", [128, 32], F32, kind="ExternalInput").ap()
    DD["ident"] = nc.dram_tensor("ident", [128, 128], F32, kind="ExternalInput").ap()
    DD["x1"] = nc.dram_tensor("x1", [T, D], F32, kind="ExternalOutput").ap()
    es0 = ExitStack()
    pr = Prog(nc, es0)
    with es0:
        emit_p1(pr, nc, DD)
        pr.emit("gpsimd", lambda e: e.nop(), waits=(("st", pr.c("st")),))
        pr.build()
    return nc


def emit_tile_to_fm(pr, st, src_r, ti, h_fm, half_w, col_of, mode, bufs, wc=()):
    sl = ti % 2
    xt, h_tm, sqd, ssq, scl, id_bf, ps_t = (bufs[k] for k in ("xt", "h_tm", "sqd", "ssq", "scl", "id_bf", "ps_t"))
    pr.emit("sync", lambda e: e.dma_start(out=xt[sl][:], in_=src_r[ti]),
            waits=st["relA"][sl], incs=(("ldx", 16),))
    vx = pr.c("ldx")
    if mode == "rms":
        pr.emit("scalar", lambda e: e.activation(sqd[:], xt[sl][:], AF.Square, accum_out=ssq[:, 0:1]),
                waits=(("ldx", vx), ("grd", st["prev_chain"])), incs=(("grd", 1),))
        v1 = pr.c("grd")
        pr.emit("scalar", lambda e: e.activation(ssq[:, 1:2], ssq[:, 0:1], AF.Sqrt),
                waits=(("grd", v1),), incs=(("grd", 1),))
        v2 = pr.c("grd")
        pr.emit("vector", lambda e: e.tensor_scalar_max(ssq[:, 2:3], ssq[:, 1:2], 1e-12),
                waits=(("grd", v2),), incs=(("grd", 1),))
        v3 = pr.c("grd")
        pr.emit("vector", lambda e: e.reciprocal(scl[:], ssq[:, 2:3]),
                waits=(("grd", v3),), incs=(("grd", 1),))
        g = pr.c("grd")
        st["prev_chain"] = g
        hw = [("grd", g), ("pe", st["tp_pe"])]
        pr.emit("vector", lambda e: e.tensor_scalar(h_tm[:], xt[sl][:], scl[:], SQRTD, ALU.mult, ALU.mult),
                waits=tuple(hw), incs=(("ve", 1),))
        rel = (("grd", g), ("ve", pr.c("ve")))
    elif mode == "clip":
        pr.emit("vector", lambda e: e.tensor_scalar(h_tm[:], xt[sl][:], 10000.0, -10000.0, ALU.min, ALU.max),
                waits=(("ldx", vx), ("pe", st["tp_pe"])), incs=(("ve", 1),))
        rel = (("ve", pr.c("ve")),)
    else:
        pr.emit("vector", lambda e: e.tensor_copy(h_tm[:], xt[sl][:]),
                waits=(("ldx", vx), ("pe", st["tp_pe"])), incs=(("ve", 1),))
        rel = (("ve", pr.c("ve")),)
    vh = pr.c("ve")
    st["relA"][sl] = rel
    for dt in range(16):
        w = [("ve", vh)] + list(wc)
        if len(st["tp_ev"]) >= 2:
            w.append(("ac", st["tp_ev"][-2]))
        pr.emit("tensor", lambda e, dt=dt: e.transpose(ps_t[dt % 2][:], h_tm[:, dt * 128:(dt + 1) * 128], id_bf[:]),
                waits=tuple(w), incs=(("pe", 1),))
        vpe = pr.c("pe")
        pr.emit("scalar", lambda e, dt=dt: e.copy(h_fm[:, col_of(dt): col_of(dt) + 128], ps_t[dt % 2][:]),
                waits=(("pe", vpe),), incs=(("ac", 1),))
        st["tp_ev"].append(pr.c("ac"))
    st["tp_pe"] = pr.c("pe")


def emit_ln_row(pr, st, pre_ap, out_sb_ap, bufs, waits):
    lsq, lst = bufs["lsq"], bufs["lst"]
    pr.emit("scalar", lambda e: e.activation(lsq[:], pre_ap, AF.Copy, accum_out=lst[:, 0:1]),
            waits=tuple(waits) + (("grd", st["prev_chain"]),), incs=(("grd", 1),))
    v1 = pr.c("grd")
    pr.emit("scalar", lambda e: e.activation(lsq[:], pre_ap, AF.Square, accum_out=lst[:, 1:2]),
            waits=(("grd", v1),), incs=(("grd", 1),))
    v2 = pr.c("grd")
    pr.emit("scalar", lambda e: e.mul(lst[:, 2:3], lst[:, 0:1], 1.0 / D),
            waits=(("grd", v2),), incs=(("grd", 1),))
    v3 = pr.c("grd")
    pr.emit("vector", lambda e: e.tensor_tensor(lst[:, 3:4], lst[:, 2:3], lst[:, 2:3], ALU.mult),
            waits=(("grd", v3),), incs=(("grd", 1),))
    v4 = pr.c("grd")
    pr.emit("vector", lambda e: e.scalar_tensor_tensor(lst[:, 4:5], lst[:, 1:2], 1.0 / D, lst[:, 3:4],
                                                       ALU.mult, ALU.subtract),
            waits=(("grd", v4),), incs=(("grd", 1),))
    v5 = pr.c("grd")
    pr.emit("vector", lambda e: e.tensor_scalar_add(lst[:, 4:5], lst[:, 4:5], 1e-5),
            waits=(("grd", v5),), incs=(("grd", 1),))
    v5b = pr.c("grd")
    pr.emit("scalar", lambda e: e.activation(lst[:, 5:6], lst[:, 4:5], AF.Sqrt),
            waits=(("grd", v5b),), incs=(("grd", 1),))
    v6 = pr.c("grd")
    pr.emit("vector", lambda e: e.reciprocal(lst[:, 3:4], lst[:, 5:6]),
            waits=(("grd", v6),), incs=(("grd", 1),))
    v7 = pr.c("grd")
    pr.emit("vector", lambda e: e.tensor_scalar(out_sb_ap, pre_ap, lst[:, 2:3], lst[:, 3:4],
                                                ALU.subtract, ALU.mult),
            waits=(("grd", v7),), incs=(("grd", 1),))
    st["prev_chain"] = pr.c("grd")
    return pr.c("grd")


def emit_p2(pr, nc, DD, pump=None):
    x = DD["x"]; x1 = DD["x1"]; wm1 = DD["wm1_til"]; wm2 = DD["wm2_r"]
    id_d = DD["ident"]; x2 = DD["x2"]; m_s = DD["m_s"]
    xr = x.rearrange("(tt p) d -> tt p d", p=128)
    x1r = x1.rearrange("(tt p) d -> tt p d", p=128)
    x2r = x2.rearrange("(tt p) d -> tt p d", p=128)
    msr = m_s.rearrange("(tt p) d -> tt p d", p=128)

    es = ExitStack()
    sb, psum = _mk_alloc(nc, es)

    bufs = {}
    xt0 = sb("xt0", [128, D], F32)
    bufs["xt"] = [xt0, xt0]
    bufs["sqd"] = sb("sqd", [128, D], BF16)
    bufs["ssq"] = sb("ssq", [128, 3], F32)
    bufs["scl"] = sb("scl", [128, 1], F32)
    bufs["h_tm"] = sb("h_tm", [128, D], BF16)
    bufs["id_bf"] = sb("id_bf", [128, 128], BF16)
    bufs["lsq"] = bufs["sqd"]
    bufs["lst"] = sb("lst", [128, 6], F32)
    bufs["ps_t"] = [psum("ps_t0", [128, 128], BF16), psum("ps_t1", [128, 128], BF16)]
    wcA = [sb(f"wcA{i}", [128, 2048], BF16) for i in range(2)]
    mst = [sb(f"mst{i}", [128, 256], F32) for i in range(2)]
    ps_m = [psum("ps_m0", [128, 512], F32), psum("ps_m1", [128, 512], F32)]
    ps_c = [psum("ps_c0", [128, 256], F32), psum("ps_c1", [128, 256], F32)]

    pr.sem("ldx", "ldw", "ldc", "pe", "ve", "ac", "grd", "st", "ldo", "ldd", "evC")

    st = {"relA": [(), ()], "prev_chain": pr.c("grd"), "tp_ev": [], "tp_pe": pr.c("pe")}
    rel_w = [0, 0]

    with es:
        pr.emit("gpsimd", lambda e: e.dma_start(out=bufs["id_bf"][:], in_=id_d), incs=(("ldc", 16),))
        wc = (("ldc", pr.c("ldc")),)

        for H in range(2):
          with nc.sbuf_tensor(f"relu_fm{_UNIQ[0]}_{H}", [128, 64 * 1024], BF16) as relu_fm:
            with nc.sbuf_tensor(f"h2_fm{_UNIQ[0]}_{H}", [128, 16 * 1024], BF16) as h2_fm_t:
                h2_fm = h2_fm_t
                for i in range(8):
                    ti = H * 8 + i
                    emit_tile_to_fm(pr, st, x1r, ti, h2_fm, 1024,
                                    lambda dt, i=i: dt * 1024 + i * 128, "rms", bufs, wc)
                    st["relA"] = [st["relA"][ti % 2], st["relA"][ti % 2]]
                vA_done = pr.c("ac")

                for c in range(64):
                    sl = c % 2
                    pr.emit("gpsimd", lambda e, c=c, sl=sl: e.dma_start(out=wcA[sl][:], in_=wm1[c]),
                            waits=(("pe", rel_w[sl]),), incs=(("ldw", 16),))
                    vw = pr.c("ldw")
                    wb = [("ldw", vw), ("ac", vA_done)]
                    if pr.c("evC") > 0:
                        wb.append(("evC", pr.c("evC")))
                    for n in range(2):
                        for kt in range(16):
                            pr.emit("tensor",
                                    lambda e, kt=kt, n=n, sl=sl: e.matmul(
                                        ps_m[n][:], wcA[sl][:, kt * 128:(kt + 1) * 128],
                                        h2_fm[:, kt * 1024 + n * 512: kt * 1024 + (n + 1) * 512],
                                        start=(kt == 0), stop=(kt == 15)),
                                    waits=tuple(wb) if (kt == 0 and n == 0) else (),
                                    incs=(("pe", 1),) if (kt == 15 and n == 1) else ())
                    vmm = pr.c("pe")
                    rel_w[sl] = vmm
                    pr.emit("scalar", lambda e, c=c: e.activation(relu_fm[:, c * 1024: c * 1024 + 512],
                                                                  ps_m[0][:], AF.Relu),
                            waits=(("pe", vmm),), incs=(("evC", 1),))
                    pr.emit("scalar", lambda e, c=c: e.activation(relu_fm[:, c * 1024 + 512: (c + 1) * 1024],
                                                                  ps_m[1][:], AF.Relu),
                            incs=(("evC", 1),))
                    if pump is not None:
                        pump.step_iters(5)
                vB_done = pr.c("evC")
                vB_pe = pr.c("pe")

            with nc.sbuf_tensor(f"wm2_sb{_UNIQ[0]}_{H}", [128, 64 * 256], BF16) as wm2_sb:
                wm2_v = wm2_sb[:].rearrange("p (kt c) -> p kt c", kt=64)
                vC_prev = vB_pe
                for nn in range(8):
                    for part in range(8):
                        pr.emit("gpsimd",
                                lambda e, nn=nn, part=part: e.dma_start(
                                    out=wm2_v[:, part * 8:(part + 1) * 8, :],
                                    in_=wm2[part * 8:(part + 1) * 8, :, nn * 256:(nn + 1) * 256]
                                        .rearrange("kt p c -> p kt c")),
                                waits=(("pe", vC_prev), ("evC", vB_done)) if part == 0 else (),
                                incs=(("ldo", 16),))
                    vwo = pr.c("ldo")
                    for tt in range(8):
                        ti = H * 8 + tt
                        sl = tt % 2
                        wpe = [("ldo", vwo), ("evC", vB_done), ("ve", pr.c("ve"))]
                        for c in range(64):
                            pr.emit("tensor",
                                    lambda e, c=c, tt=tt, nn=nn, sl=sl: e.matmul(
                                        ps_c[sl][:], relu_fm[:, c * 1024 + tt * 128: c * 1024 + (tt + 1) * 128],
                                        wm2_sb[:, c * 256:(c + 1) * 256],
                                        start=(c == 0), stop=(c == 63)),
                                    waits=tuple(w for w in wpe if w[1] > 0) if c == 0 else (),
                                    incs=(("pe", 1),) if c == 63 else ())
                        vpc = pr.c("pe")
                        pr.emit("vector", lambda e, sl=sl: e.tensor_copy(mst[sl][:], ps_c[sl][:]),
                                waits=(("pe", vpc), ("st", max(0, pr.c("st") - 16))), incs=(("ve", 1),))
                        vev = pr.c("ve")
                        pr.emit("sync", lambda e, ti=ti, nn=nn, sl=sl: e.dma_start(
                                    out=msr[ti, :, nn * 256:(nn + 1) * 256], in_=mst[sl][:]),
                                waits=(("ve", vev),), incs=(("st", 16),))
                        if pump is not None:
                            pump.step_iters(3)
                    vC_prev = pr.c("pe")

        vst_C = pr.c("st")
        with ExitStack() as esd:
            dx = [esd.enter_context(nc.sbuf_tensor(f"dx{_UNIQ[0]}_{i}", [128, D], F32)) for i in range(2)]
            d1 = [esd.enter_context(nc.sbuf_tensor(f"d1{_UNIQ[0]}_{i}", [128, D], F32)) for i in range(2)]
            dm = [esd.enter_context(nc.sbuf_tensor(f"dm{_UNIQ[0]}_{i}", [128, D], F32)) for i in range(2)]
            do = [esd.enter_context(nc.sbuf_tensor(f"do{_UNIQ[0]}_{i}", [128, D], F32)) for i in range(2)]
            rel_d = [(), ()]
            for ti in range(16):
                sl = ti % 2
                w0 = (("st", vst_C),) + rel_d[sl]
                pr.emit("sync", lambda e, ti=ti, sl=sl: e.dma_start(out=dx[sl][:], in_=xr[ti]),
                        waits=w0, incs=(("ldd", 16),))
                pr.emit("sync", lambda e, ti=ti, sl=sl: e.dma_start(out=d1[sl][:], in_=x1r[ti]),
                        incs=(("ldd", 16),))
                pr.emit("sync", lambda e, ti=ti, sl=sl: e.dma_start(out=dm[sl][:], in_=msr[ti]),
                        incs=(("ldd", 16),))
                vld = pr.c("ldd")
                pr.emit("vector", lambda e, sl=sl: e.tensor_tensor(dm[sl][:], dm[sl][:], dx[sl][:], ALU.add),
                        waits=(("ldd", vld),), incs=(("ve", 1),))
                vv = pr.c("ve")
                pr.emit("vector", lambda e, sl=sl: e.tensor_tensor(dm[sl][:], dm[sl][:], d1[sl][:], ALU.add),
                        waits=(("ve", vv),), incs=(("ve", 1),))
                vv2 = pr.c("ve")
                g = emit_ln_row(pr, st, dm[sl][:], do[sl][:], bufs, [("ve", vv2), ("st", max(0, pr.c("st") - 16))])
                pr.emit("sync", lambda e, ti=ti, sl=sl: e.dma_start(out=x2r[ti], in_=do[sl][:]),
                        waits=(("grd", g),), incs=(("st", 16),))
                rel_d[sl] = (("grd", g),)


def build_p2():
    nc = bass.Bass("TRN2", target_bir_lowering=False)
    DD = {}
    DD["x"] = nc.dram_tensor("x", [T, D], F32, kind="ExternalInput").ap()
    DD["x1"] = nc.dram_tensor("x1", [T, D], F32, kind="ExternalInput").ap()
    DD["wm1_til"] = nc.dram_tensor("wm1_til", [64, 128, 2048], F32, kind="ExternalInput").ap()
    DD["wm2_r"] = nc.dram_tensor("wm2_r", [64, 128, 2048], F32, kind="ExternalInput").ap()
    DD["ident"] = nc.dram_tensor("ident", [128, 128], F32, kind="ExternalInput").ap()
    DD["x2"] = nc.dram_tensor("x2", [T, D], F32, kind="ExternalOutput").ap()
    DD["m_s"] = nc.dram_tensor("m_s", [T, D], F32, kind="Internal").ap()
    es0 = ExitStack()
    pr = Prog(nc, es0)
    with es0:
        emit_p2(pr, nc, DD)
        pr.emit("gpsimd", lambda e: e.nop(), waits=(("st", pr.c("st")),))
        pr.build()
    return nc


NGTILES = 17


def build_gather():
    nc = bass.Bass("TRN2", target_bir_lowering=False)
    idx_my = nc.dram_tensor("idx_my", [NGTILES, 128, 2048], I32, kind="ExternalInput").ap()
    lut_d = nc.dram_tensor("lutrep", [128, 16], F32, kind="ExternalInput").ap()
    wcat = nc.dram_tensor("wcat", [NGTILES, 128, 2048], BF16, kind="ExternalOutput").ap()

    es = ExitStack()
    sb, _psum_unused = _mk_alloc(nc, es)
    idx_bf = [sb(f"idx_bf{i}", [128, 2048], BF16) for i in range(2)]
    gacc = [sb(f"gacc{i}", [128, 2048], BF16) for i in range(2)]
    eqt = sb("eqt", [128, 2048], BF16)
    lut_sb = sb("lut_sb", [128, 16], F32)

    pr = Prog(nc, es)
    pr.sem("ldw", "ldc", "ve", "st")
    with es:
        pr.emit("sync", lambda e: e.dma_start(out=lut_sb[:], in_=lut_d), incs=(("ldc", 16),))
        rel_idx = [0, 0]
        rel_acc = [0, 0]
        for j in range(NGTILES):
            sl = j % 2
            pr.emit("gpsimd", lambda e, j=j, sl=sl: e.dma_start(out=idx_bf[sl][:], in_=idx_my[j]),
                    waits=(("ve", rel_idx[sl]),), incs=(("ldw", 16),))
            vld = pr.c("ldw")
            pr.emit("vector", lambda e, sl=sl: e.memset(gacc[sl][:], 0.0),
                    waits=(("st", rel_acc[sl]),), incs=(("ve", 1),))
            for k in range(16):
                w = (("ldw", vld), ("ldc", 16)) if k == 0 else ()
                pr.emit("vector",
                        lambda e, k=k, sl=sl: e.tensor_scalar(eqt[:], idx_bf[sl][:], float(k),
                                                              lut_sb[:, k:k + 1], ALU.is_equal, ALU.mult),
                        waits=w)
                pr.emit("vector", lambda e, sl=sl: e.tensor_tensor(gacc[sl][:], gacc[sl][:], eqt[:], ALU.add),
                        incs=(("ve", 1),) if k == 15 else ())
            vg = pr.c("ve")
            rel_idx[sl] = vg
            pr.emit("sync", lambda e, j=j, sl=sl: e.dma_start(out=wcat[j], in_=gacc[sl][:]),
                    waits=(("ve", vg),), incs=(("st", 16),))
            rel_acc[sl] = pr.c("st")
        pr.emit("gpsimd", lambda e: e.nop(), waits=(("st", pr.c("st")),))
        pr.build()
    return nc


def emit_p3(pr, nc, DD, pump=None):
    x2 = DD["x2"]; wgA = DD["wgA"]; wuA = DD["wuA"]; wdR = DD["wdR"]
    id_d = DD["ident"]; x3 = DD["x3"]; m_s = DD["m_s"]
    x2r = x2.rearrange("(tt p) d -> tt p d", p=128)
    x3r = x3.rearrange("(tt p) d -> tt p d", p=128)
    msr = m_s.rearrange("(tt p) d -> tt p d", p=128)

    es = ExitStack()
    sb, psum = _mk_alloc(nc, es)

    bufs = {}
    bufs["xt"] = [sb(f"xt{i}", [128, D], F32) for i in range(2)]
    bufs["sqd"] = sb("sqd", [128, D], BF16)
    bufs["ssq"] = sb("ssq", [128, 3], F32)
    bufs["scl"] = sb("scl", [128, 1], F32)
    bufs["h_tm"] = sb("h_tm", [128, D], BF16)
    bufs["id_bf"] = sb("id_bf", [128, 128], BF16)
    bufs["lsq"] = sb("lsq", [128, D], BF16)
    bufs["lst"] = sb("lst", [128, 6], F32)
    bufs["ps_t"] = [psum("ps_t0", [128, 128], BF16), psum("ps_t1", [128, 128], BF16)]
    wgB = [sb(f"wgA{i}", [128, 2048], BF16) for i in range(2)]
    wuB = [sb(f"wuA{i}", [128, 2048], BF16) for i in range(2)]
    gs = sb("gs", [128, 1024], F32)
    us = sb("us", [128, 1024], F32)
    mst = [sb(f"mst{i}", [128, 256], F32) for i in range(2)]
    ps_g = [psum("ps_g0", [128, 512], F32), psum("ps_g1", [128, 512], F32)]
    ps_u = [psum("ps_u0", [128, 512], F32), psum("ps_u1", [128, 512], F32)]
    ps_c = [psum("ps_c0", [128, 256], F32), psum("ps_c1", [128, 256], F32)]

    pr.sem("ldx", "ldw", "ldc", "pe", "ve", "ac", "grd", "st", "ldo", "ldd", "evH")

    st = {"relA": [(), ()], "prev_chain": pr.c("grd"), "tp_ev": [], "tp_pe": pr.c("pe")}
    rel_w = [0, 0]

    with es:
        pr.emit("gpsimd", lambda e: e.dma_start(out=bufs["id_bf"][:], in_=id_d), incs=(("ldc", 16),))
        wc = (("ldc", pr.c("ldc")),)

        for H in range(2):
          with nc.sbuf_tensor(f"hact_fm{_UNIQ[0]}_{H}", [128, 43 * 1024], BF16) as hact_fm:
            with nc.sbuf_tensor(f"x2_fm{_UNIQ[0]}_{H}", [128, 16 * 1024], BF16) as x2_fm:
                for i in range(8):
                    ti = H * 8 + i
                    emit_tile_to_fm(pr, st, x2r, ti, x2_fm, 1024,
                                    lambda dt, i=i: dt * 1024 + i * 128, "cast", bufs, wc)
                vA_done = pr.c("ac")

                for oc in range(43):
                    sl = oc % 2
                    if pump is not None:
                        pump.ensure(2 * oc + 2)
                        gwait = (("gst", max(pump.store_counts[2 * oc], pump.store_counts[2 * oc + 1])),)
                    else:
                        gwait = ()
                    pr.emit("sync", lambda e, oc=oc, sl=sl: e.dma_start(out=wgB[sl][:], in_=wgA(oc)),
                            waits=(("pe", rel_w[sl]),) + gwait, incs=(("ldw", 16),))
                    pr.emit("sync", lambda e, oc=oc, sl=sl: e.dma_start(out=wuB[sl][:], in_=wuA(oc)),
                            incs=(("ldw", 16),))
                    vw = pr.c("ldw")
                    wb = [("ldw", vw), ("ac", vA_done), ("evH", pr.c("evH"))]
                    for n in range(2):
                        for kt in range(16):
                            pr.emit("tensor",
                                    lambda e, kt=kt, n=n, sl=sl: e.matmul(
                                        ps_g[n][:], wgB[sl][:, kt * 128:(kt + 1) * 128],
                                        x2_fm[:, kt * 1024 + n * 512: kt * 1024 + (n + 1) * 512],
                                        start=(kt == 0), stop=(kt == 15)),
                                    waits=tuple(w for w in wb if w[1] > 0) if (kt == 0 and n == 0) else ())
                    for n in range(2):
                        for kt in range(16):
                            pr.emit("tensor",
                                    lambda e, kt=kt, n=n, sl=sl: e.matmul(
                                        ps_u[n][:], wuB[sl][:, kt * 128:(kt + 1) * 128],
                                        x2_fm[:, kt * 1024 + n * 512: kt * 1024 + (n + 1) * 512],
                                        start=(kt == 0), stop=(kt == 15)),
                                    incs=(("pe", 1),) if (kt == 15 and n == 1) else ())
                    vmm = pr.c("pe")
                    rel_w[sl] = vmm
                    pr.emit("scalar", lambda e: e.activation(gs[:, 0:512], ps_g[0][:], AF.Silu),
                            waits=(("pe", vmm), ("evH", pr.c("evH"))), incs=(("ac", 1),))
                    pr.emit("scalar", lambda e: e.activation(gs[:, 512:1024], ps_g[1][:], AF.Silu),
                            incs=(("ac", 1),))
                    vgs = pr.c("ac")
                    pr.emit("vector", lambda e: e.tensor_copy(us[:, 0:512], ps_u[0][:]),
                            waits=(("pe", vmm),), incs=(("ve", 1),))
                    pr.emit("vector", lambda e: e.tensor_copy(us[:, 512:1024], ps_u[1][:]),
                            incs=(("ve", 1),))
                    pr.emit("vector",
                            lambda e, oc=oc: e.tensor_tensor(hact_fm[:, oc * 1024:(oc + 1) * 1024],
                                                             gs[:], us[:], ALU.mult),
                            waits=(("ac", vgs),), incs=(("evH", 1),))
                    if pump is not None:
                        pump.step_iters(10)
                vB_done = pr.c("evH")
                vB_pe = pr.c("pe")

            if pump is not None:
                pump.finish()
                gw_all = (("gst", pump.store_counts[128]),)
            else:
                gw_all = ()
            with nc.sbuf_tensor(f"wd_sb{_UNIQ[0]}_{H}", [128, 43 * 256], BF16) as wd_sb:
                wd_v = wd_sb[:].rearrange("p (kt c) -> p kt c", kt=43)
                vC_prev = vB_pe
                for nn in range(8):
                    bnds = [(0, 8), (8, 16), (16, 24), (24, 32), (32, 40), (40, 43)]
                    for (a, b) in bnds:
                        pr.emit("gpsimd",
                                lambda e, nn=nn, a=a, b=b: e.dma_start(
                                    out=wd_v[:, a:b, :],
                                    in_=wdR(a, b)[:, :, nn * 256:(nn + 1) * 256].rearrange("kt p c -> p kt c")),
                                waits=((("pe", vC_prev),) if a == 0 else ()) + (gw_all if a == 0 else ()),
                                incs=(("ldo", 16),))
                    vwo = pr.c("ldo")
                    for tt in range(8):
                        ti = H * 8 + tt
                        sl = tt % 2
                        wpe = [("ldo", vwo), ("evH", vB_done), ("ve", pr.c("ve"))]
                        for c in range(43):
                            pr.emit("tensor",
                                    lambda e, c=c, tt=tt, sl=sl: e.matmul(
                                        ps_c[sl][:], hact_fm[:, c * 1024 + tt * 128: c * 1024 + (tt + 1) * 128],
                                        wd_sb[:, c * 256:(c + 1) * 256],
                                        start=(c == 0), stop=(c == 42)),
                                    waits=tuple(w for w in wpe if w[1] > 0) if c == 0 else (),
                                    incs=(("pe", 1),) if c == 42 else ())
                        vpc = pr.c("pe")
                        pr.emit("vector", lambda e, sl=sl: e.tensor_copy(mst[sl][:], ps_c[sl][:]),
                                waits=(("pe", vpc), ("st", max(0, pr.c("st") - 16))), incs=(("ve", 1),))
                        vev = pr.c("ve")
                        pr.emit("sync", lambda e, ti=ti, nn=nn, sl=sl: e.dma_start(
                                    out=msr[ti, :, nn * 256:(nn + 1) * 256], in_=mst[sl][:]),
                                waits=(("ve", vev),), incs=(("st", 16),))
                    vC_prev = pr.c("pe")

        vst_C = pr.c("st")
        with ExitStack() as esd:
            dx = [esd.enter_context(nc.sbuf_tensor(f"dx{_UNIQ[0]}_{i}", [128, D], F32)) for i in range(2)]
            dm = [esd.enter_context(nc.sbuf_tensor(f"dm{_UNIQ[0]}_{i}", [128, D], F32)) for i in range(2)]
            do = [esd.enter_context(nc.sbuf_tensor(f"do{_UNIQ[0]}_{i}", [128, D], F32)) for i in range(2)]
            rel_d = [(), ()]
            for ti in range(16):
                sl = ti % 2
                pr.emit("sync", lambda e, ti=ti, sl=sl: e.dma_start(out=dx[sl][:], in_=x2r[ti]),
                        waits=(("st", vst_C),) + rel_d[sl], incs=(("ldd", 16),))
                pr.emit("sync", lambda e, ti=ti, sl=sl: e.dma_start(out=dm[sl][:], in_=msr[ti]),
                        incs=(("ldd", 16),))
                vld = pr.c("ldd")
                pr.emit("vector", lambda e, sl=sl: e.tensor_tensor(dm[sl][:], dm[sl][:], dx[sl][:], ALU.add),
                        waits=(("ldd", vld),), incs=(("ve", 1),))
                vv = pr.c("ve")
                g = emit_ln_row(pr, st, dm[sl][:], do[sl][:], bufs, [("ve", vv), ("st", max(0, pr.c("st") - 16))])
                pr.emit("sync", lambda e, ti=ti, sl=sl: e.dma_start(out=x3r[ti], in_=do[sl][:]),
                        waits=(("grd", g),), incs=(("st", 16),))
                rel_d[sl] = (("grd", g),)



def build_p3():
    nc = bass.Bass("TRN2", target_bir_lowering=False)
    DD = {}
    DD["x2"] = nc.dram_tensor("x2", [T, D], F32, kind="ExternalInput").ap()
    wg_t = nc.dram_tensor("wg_s", [43, 128, 2048], BF16, kind="ExternalInput").ap()
    wu_t = nc.dram_tensor("wu_s", [43, 128, 2048], BF16, kind="ExternalInput").ap()
    wd_t = nc.dram_tensor("wd_s", [43, 128, 2048], BF16, kind="ExternalInput").ap()
    DD["wgA"] = lambda oc: wg_t[oc]
    DD["wuA"] = lambda oc: wu_t[oc]
    DD["wdR"] = lambda a, b: wd_t[a:b]
    DD["ident"] = nc.dram_tensor("ident", [128, 128], F32, kind="ExternalInput").ap()
    DD["x3"] = nc.dram_tensor("x3", [T, D], F32, kind="ExternalOutput").ap()
    DD["m_s"] = nc.dram_tensor("m_s", [T, D], F32, kind="Internal").ap()
    es0 = ExitStack()
    pr = Prog(nc, es0)
    with es0:
        emit_p3(pr, nc, DD)
        pr.emit("gpsimd", lambda e: e.nop(), waits=(("st", pr.c("st")),))
        pr.build()
    return nc


NH = 16
NKV = 4
DH = 128
ISQD = 1.0 / float(np.sqrt(128.0))
P4_DEBUG = False
FL1 = True
FL2 = True
FL3 = True
P5_TRUNC = 0


def emit_p4(pr, nc, DD):
    x3 = DD["x3"]; qw = DD["qw_til"]; kw = DD["kw_til"]; vw = DD["vw_r"]; ow = DD["ow_r"]
    R_d = DD["Rrel"]; Dm_d = DD["Dmask"]; id_d = DD["ident"]; x4 = DD["x4"]
    y_s = DD["y_s"]; m_s = DD["m_s"]
    if P4_DEBUG:
        S_dump = DD["S_dump"]; P_dump = DD["P_dump"]
    x3r = x3.rearrange("(tt p) d -> tt p d", p=128)
    x4r = x4.rearrange("(tt p) d -> tt p d", p=128)
    msr = m_s.rearrange("(tt p) d -> tt p d", p=128)
    ysr = y_s.rearrange("(kc p) t -> kc p t", p=128)

    SLOPES = [2.0 ** (-(h + 1) / 2.0) for h in range(NH)]

    es = ExitStack()
    sb, psum = _mk_alloc(nc, es)

    bufs = {}
    xt0 = sb("xt0", [128, D], F32)
    bufs["xt"] = [xt0, xt0]
    bufs["sqd"] = sb("sqd", [128, D], BF16)
    bufs["ssq"] = sb("ssq", [128, 3], F32)
    bufs["scl"] = sb("scl", [128, 1], F32)
    bufs["h_tm"] = sb("h_tm", [128, D], BF16)
    bufs["id_bf"] = sb("id_bf", [128, 128], BF16)
    bufs["lsq"] = bufs["sqd"]
    bufs["lst"] = sb("lst", [128, 6], F32)
    k_fm = sb("k_fm", [128, 4 * 2048], BF16)
    v_tm = sb("v_tm", [128, 16 * 512], BF16)
    qh = [sb(f"qh{i}", [128, 2048], BF16) for i in range(2)]
    qwA = [sb(f"qwA{i}", [128, 2048], BF16) for i in range(2)]
    P4b = [sb(f"P4b{i}", [128, 4 * 2048], BF16) for i in range(2)]
    S_row = [sb(f"S_row{i}", [128, 2048], F32) for i in range(2)]
    pt_sb = [sb(f"pt_sb{i}", [128, 512], BF16) for i in range(2)]
    R_sb = sb("R_sb", [128, 2176], F32)
    Dm_sb = sb("Dm_sb", [128, 128], F32)
    smx = [sb(f"smx{i}", [128, 4], F32) for i in range(2)]
    flt = sb("flt", [128, 4], F32)
    yst = [sb(f"yst{i}", [128, 512], BF16) for i in range(2)]
    ps_s = [psum("ps_s0", [128, 512], F32), psum("ps_s1", [128, 512], F32)]
    ps_y = [psum("ps_y0", [128, 512], F32), psum("ps_y1", [128, 512], F32)]
    ps_p = [psum("ps_p0", [128, 512], F32), psum("ps_p1", [128, 512], F32)]

    pr.sem("ldx", "ldw", "ldc", "pe", "ve", "ac", "grd", "st", "ldo", "ldd", "evq", "sty", "dbg")

    st = {"relA": [(), ()], "prev_chain": pr.c("grd"), "tp_ev": [], "tp_pe": pr.c("pe"),
          "rel_srow": pr.c("grd"), "rel_p4b": pr.c("pe")}

    with es:
        pr.emit("gpsimd", lambda e: e.dma_start(out=bufs["id_bf"][:], in_=id_d), incs=(("ldc", 16),))
        pr.emit("sync", lambda e: e.dma_start(out=R_sb[:], in_=R_d), incs=(("ldc", 16),))
        pr.emit("sync", lambda e: e.dma_start(out=Dm_sb[:], in_=Dm_d), incs=(("ldc", 16),))
        wc = (("ldc", pr.c("ldc")),)

        with nc.sbuf_tensor(f"x3_fm{_UNIQ[0]}", [128, 16 * 2048], BF16) as x3_fm:
            with ExitStack() as espA:
                bufs["ps_t"] = [espA.enter_context(nc.psum_tensor(f"ps_tA{_UNIQ[0]}_{i}", [128, 128], BF16))
                                for i in range(2)]
                for ti in range(16):
                    emit_tile_to_fm(pr, st, x3r, ti, x3_fm, 2048,
                                    lambda dt, ti=ti: dt * 2048 + ti * 128, "clip", bufs, wc)
                    st["relA"] = [st["relA"][ti % 2], st["relA"][ti % 2]]
            ps_tp = [psum("ps_tp0", [128, 512], BF16), psum("ps_tp1", [128, 512], BF16)]
            vA_done = pr.c("ac")

            with nc.sbuf_tensor(f"vw_sb{_UNIQ[0]}", [128, 16 * 512], BF16) as vw_sb:
                vwv = vw_sb[:].rearrange("p (kt c) -> p kt c", kt=16)
                for part in range(2):
                    pr.emit("gpsimd", lambda e, part=part: e.dma_start(
                                out=vwv[:, part * 8:(part + 1) * 8, :],
                                in_=vw[part * 8:(part + 1) * 8].rearrange("kt p c -> p kt c")),
                            incs=(("ldw", 16),))
                vld_vw = pr.c("ldw")
                rel_ps = [0, 0]
                for kc in range(4):
                    sl = kc % 2
                    pr.emit("gpsimd", lambda e, kc=kc, sl=sl: e.dma_start(out=qwA[sl][:], in_=kw[kc]),
                            waits=(("pe", 0 if kc < 2 else pr.c("pe")),), incs=(("ldw", 16),))
                    vwk = pr.c("ldw")
                    for n in range(4):
                        psl = n % 2
                        w0 = [("ldw", vwk), ("ac", vA_done), ("ve", rel_ps[psl])]
                        for kt in range(16):
                            pr.emit("tensor",
                                    lambda e, kt=kt, n=n, sl=sl, psl=psl: e.matmul(
                                        ps_p[psl][:], qwA[sl][:, kt * 128:(kt + 1) * 128],
                                        x3_fm[:, kt * 2048 + n * 512: kt * 2048 + (n + 1) * 512],
                                        start=(kt == 0), stop=(kt == 15)),
                                    waits=tuple(w for w in w0 if w[1] > 0) if kt == 0 else (),
                                    incs=(("pe", 1),) if kt == 15 else ())
                        vpe = pr.c("pe")
                        pr.emit("vector",
                                lambda e, kc=kc, n=n, psl=psl: e.tensor_copy(
                                    k_fm[:, kc * 2048 + n * 512: kc * 2048 + (n + 1) * 512], ps_p[psl][:]),
                                waits=(("pe", vpe),), incs=(("ve", 1),))
                        rel_ps[psl] = pr.c("ve")
                for tt in range(16):
                    psl = tt % 2
                    w0 = [("ldw", vld_vw), ("ac", vA_done), ("ve", rel_ps[psl])]
                    for kt in range(16):
                        pr.emit("tensor",
                                lambda e, kt=kt, tt=tt, psl=psl: e.matmul(
                                    ps_p[psl][:], x3_fm[:, kt * 2048 + tt * 128: kt * 2048 + (tt + 1) * 128],
                                    vw_sb[:, kt * 512:(kt + 1) * 512],
                                    start=(kt == 0), stop=(kt == 15)),
                                waits=tuple(w for w in w0 if w[1] > 0) if kt == 0 else (),
                                incs=(("pe", 1),) if kt == 15 else ())
                    vpe = pr.c("pe")
                    pr.emit("vector",
                            lambda e, tt=tt, psl=psl: e.tensor_copy(
                                v_tm[:, tt * 512:(tt + 1) * 512], ps_p[psl][:]),
                            waits=(("pe", vpe),), incs=(("ve", 1),))
                    rel_ps[psl] = pr.c("ve")

            rel_qh = [0, 0]
            rel_qw = [pr.c("pe"), pr.c("pe")]
            rel_pss = [0, 0]
            rel_psy = [0, 0]
            rel_yst = [0, 0]
            rel_pt = [0, 0]
            rel_p4b = [pr.c("pe"), pr.c("pe")]
            rel_srow = [pr.c("grd"), pr.c("grd")]
            for h in range(NH):
                hsl = h % 2
                kvc = h // 4
                slope = SLOPES[h]
                pr.emit("gpsimd", lambda e, h=h, hsl=hsl: e.dma_start(out=qwA[hsl][:], in_=qw[h]),
                        waits=(("pe", rel_qw[hsl]),), incs=(("ldw", 16),))
                vwq = pr.c("ldw")
                for n in range(4):
                    psl = n % 2
                    w0 = [("ldw", vwq), ("ac", vA_done), ("ve", rel_ps[psl]), ("pe", rel_qh[hsl])]
                    for kt in range(16):
                        pr.emit("tensor",
                                lambda e, kt=kt, n=n, hsl=hsl, psl=psl: e.matmul(
                                    ps_p[psl][:], qwA[hsl][:, kt * 128:(kt + 1) * 128],
                                    x3_fm[:, kt * 2048 + n * 512: kt * 2048 + (n + 1) * 512],
                                    start=(kt == 0), stop=(kt == 15)),
                                waits=tuple(w for w in w0 if w[1] > 0) if kt == 0 else (),
                                incs=(("pe", 1),) if kt == 15 else ())
                    vpe = pr.c("pe")
                    pr.emit("vector",
                            lambda e, n=n, hsl=hsl, psl=psl: e.tensor_copy(
                                qh[hsl][:, n * 512:(n + 1) * 512], ps_p[psl][:]),
                            waits=(("pe", vpe),), incs=(("ve", 1),))
                    rel_ps[psl] = pr.c("ve")
                if FL3:
                    pr.emit("vector", lambda e: e.tensor_copy(flt[:, 2:3], flt[:, 3:4]), incs=(("ve", 1),))
                vq_ready = pr.c("ve")
                for s_ in range(2):
                    pr.emit("vector", lambda e, s_=s_: e.memset(P4b[s_][:], 0.0),
                            waits=(("pe", rel_p4b[s_]),), incs=(("ve", 1),))
                rel_qw[hsl] = 0

                vP_done_q = [0, 0, 0, 0]

                def emit_S_soft(qb):
                    qsl = qb % 2
                    for sq in range(4):
                        qt = qb * 4 + sq
                        ssl2 = qt % 2
                        Srow = S_row[ssl2]
                        smxq = smx[ssl2]
                        L = (qt + 1) * 128
                        nkn = (L + 511) // 512
                        for kn in range(nkn):
                            wdt = min(512, L - kn * 512)
                            ssl = kn % 2
                            w0 = [("ve", max(vq_ready, rel_pss[ssl]))]
                            pr.emit("tensor",
                                    lambda e, qt=qt, kn=kn, wdt=wdt, ssl=ssl, hsl=hsl, kvc=kvc: e.matmul(
                                        ps_s[ssl][:, 0:wdt], qh[hsl][:, qt * 128:(qt + 1) * 128],
                                        k_fm[:, kvc * 2048 + kn * 512: kvc * 2048 + kn * 512 + wdt],
                                        start=True, stop=True),
                                    waits=tuple(w for w in w0 if w[1] > 0),
                                    incs=(("pe", 1),))
                            vpe = pr.c("pe")
                            pr.emit("vector",
                                    lambda e, kn=kn, wdt=wdt, ssl=ssl, Srow=Srow: e.tensor_scalar(
                                        Srow[:, kn * 512: kn * 512 + wdt], ps_s[ssl][:, 0:wdt],
                                        ISQD, 10000.0, ALU.mult, ALU.min),
                                    waits=(("pe", vpe), ("grd", rel_srow[ssl2])), incs=(("ve", 1),))
                            rel_pss[ssl] = pr.c("ve")
                        pr.emit("vector",
                                lambda e, qt=qt, Srow=Srow, slope=slope: e.scalar_tensor_tensor(
                                    Srow[:, qt * 128: qt * 128 + 128], Dm_sb[:], slope,
                                    Srow[:, qt * 128: qt * 128 + 128], ALU.mult, ALU.add),
                                waits=wc, incs=())
                        if qt > 0:
                            pr.emit("vector",
                                    lambda e, qt=qt, Srow=Srow, slope=slope: e.scalar_tensor_tensor(
                                        Srow[:, 0: qt * 128], R_sb[:, 2047 - qt * 128: 2047], slope,
                                        Srow[:, 0: qt * 128], ALU.mult, ALU.add),
                                    incs=())
                        pr.emit("vector", lambda e, L=L, Srow=Srow, smxq=smxq: e.reduce_max(
                                    smxq[:, 0:1], Srow[:, 0:L], axis=AX.X),
                                incs=(("grd", 1),))
                        v1 = pr.c("grd")
                        pr.emit("scalar", lambda e, smxq=smxq: e.mul(smxq[:, 1:2], smxq[:, 0:1], -1.0),
                                waits=(("grd", v1),), incs=(("grd", 1),))
                        v2 = pr.c("grd")
                        pr.emit("scalar",
                                lambda e, sq=sq, L=L, qsl=qsl, Srow=Srow, smxq=smxq: e.activation(
                                    P4b[qsl][:, sq * 2048: sq * 2048 + L], Srow[:, 0:L], AF.Exp,
                                    bias=smxq[:, 1:2], accum_out=smxq[:, 2:3]),
                                waits=(("grd", v2), ("pe", rel_p4b[qsl])), incs=(("grd", 1),))
                        v3 = pr.c("grd")
                        rel_srow[ssl2] = v3
                        pr.emit("vector", lambda e, smxq=smxq: e.reciprocal(smxq[:, 3:4], smxq[:, 2:3]),
                                waits=(("grd", v3),), incs=(("grd", 1),))
                        v4 = pr.c("grd")
                        pr.emit("vector",
                                lambda e, sq=sq, L=L, qsl=qsl, smxq=smxq: e.tensor_scalar_mul(
                                    P4b[qsl][:, sq * 2048: sq * 2048 + L],
                                    P4b[qsl][:, sq * 2048: sq * 2048 + L], smxq[:, 3:4]),
                                waits=(("grd", v4),), incs=(("grd", 1),))
                    pr.emit("vector", lambda e, smxq=smxq: e.tensor_copy(smxq[:, 0:1], smxq[:, 3:4]),
                            incs=(("grd", 1),))
                    vP_done_q[qb] = pr.c("grd")

                def emit_tpPV(qb):
                    qsl = qb % 2
                    njmax = qb * 4 + 4
                    ysl = qb % 2
                    for j in range(njmax):
                        ptl = j % 2
                        for sq in range(4):
                            w0 = [("grd", vP_done_q[qb]), ("pe", rel_pt[ptl])]
                            if sq == 0:
                                w0.append(("ac", st["tp_ev"][-2] if len(st["tp_ev"]) >= 2 else 0))
                            pr.emit("tensor",
                                    lambda e, sq=sq, j=j, qsl=qsl, ptl=ptl: e.transpose(
                                        ps_tp[ptl][:, sq * 128:(sq + 1) * 128],
                                        P4b[qsl][:, sq * 2048 + j * 128: sq * 2048 + (j + 1) * 128],
                                        bufs["id_bf"][:]),
                                    waits=tuple(w for w in w0 if w[1] > 0),
                                    incs=(("pe", 1),) if sq == 3 else ())
                        vpe = pr.c("pe")
                        pr.emit("scalar",
                                lambda e, ptl=ptl: e.copy(pt_sb[ptl][:], ps_tp[ptl][:]),
                                waits=(("pe", vpe),), incs=(("ac", 1),))
                        st["tp_ev"].append(pr.c("ac"))
                        pr.emit("scalar", lambda e: e.copy(flt[:, 0:1], flt[:, 1:2]), incs=(("ac", 1),))
                        vpt = pr.c("ac")
                        w0 = [("ac", vpt), ("ve", rel_psy[ysl] if j == 0 else 0)]
                        pr.emit("tensor",
                                lambda e, j=j, ptl=ptl, ysl=ysl, njmax=njmax, kvc=kvc: e.matmul(
                                    ps_y[ysl][:], v_tm[:, j * 512 + kvc * 128: j * 512 + (kvc + 1) * 128],
                                    pt_sb[ptl][:], start=(j == 0), stop=(j == njmax - 1)),
                                waits=tuple(w for w in w0 if w[1] > 0), incs=(("pe", 1),))
                        rel_pt[ptl] = pr.c("pe")
                    vpv = pr.c("pe")
                    wsl = qb % 2
                    pr.emit("vector",
                            lambda e, ysl=ysl, wsl=wsl: e.tensor_scalar(
                                yst[wsl][:], ps_y[ysl][:], 10000.0, -10000.0, ALU.min, ALU.max),
                            waits=(("pe", vpv), ("sty", rel_yst[wsl])), incs=(("ve", 1),))
                    rel_psy[ysl] = pr.c("ve")
                    vyev = pr.c("ve")
                    pr.emit("sync",
                            lambda e, qb=qb, h=h, wsl=wsl: e.dma_start(
                                out=ysr[h, :, qb * 512:(qb + 1) * 512], in_=yst[wsl][:]),
                            waits=(("ve", vyev),), incs=(("sty", 16),))
                    rel_yst[wsl] = pr.c("sty")
                    rel_p4b[qsl] = pr.c("pe")

                for qb in range(4):
                    emit_S_soft(qb)
                    if qb >= 1:
                        emit_tpPV(qb - 1)
                emit_tpPV(3)
                rel_qh[hsl] = pr.c("pe")
                rel_qw[hsl] = pr.c("pe")
            vattn_st = pr.c("sty")
            vattn_pe = pr.c("pe")

        with ExitStack() as eso:
            ow_sb = eso.enter_context(nc.sbuf_tensor(f"ow_sb{_UNIQ[0]}", [128, 16 * 2048], BF16))
            ytil = [eso.enter_context(nc.sbuf_tensor(f"ytil{_UNIQ[0]}_{i}", [128, 16 * 128], BF16)) for i in range(2)]
            mst = [eso.enter_context(nc.sbuf_tensor(f"mst{_UNIQ[0]}_{i}", [128, 512], F32)) for i in range(2)]
            owv = ow_sb[:].rearrange("p (kt c) -> p kt c", kt=16)
            for kc in range(16):
                pr.emit("gpsimd", lambda e, kc=kc: e.dma_start(out=owv[:, kc:kc + 1, :],
                                                               in_=ow[kc:kc + 1].rearrange("kt p c -> p kt c")),
                        waits=(("pe", vattn_pe),) if kc == 0 else (), incs=(("ldo", 16),))
            vow = pr.c("ldo")
            rel_yt = [0, 0]
            rel_mst = [0, 0]
            for tt in range(16):
                ysl = tt % 2
                yv = ytil[ysl][:].rearrange("p (kc c) -> p kc c", kc=16)
                pr.emit("sync", lambda e, tt=tt, ysl=ysl, yv=yv: e.dma_start(
                            out=yv[:, 0:8, :],
                            in_=ysr[0:8, :, tt * 128:(tt + 1) * 128].rearrange("kc p c -> p kc c")),
                        waits=(("sty", vattn_st), ("pe", max(vattn_pe, rel_yt[ysl]))), incs=(("ldd", 16),))
                pr.emit("sync", lambda e, tt=tt, ysl=ysl, yv=yv: e.dma_start(
                            out=yv[:, 8:16, :],
                            in_=ysr[8:16, :, tt * 128:(tt + 1) * 128].rearrange("kc p c -> p kc c")),
                        incs=(("ldd", 16),))
                vldy = pr.c("ldd")
                for dc in range(4):
                    psl = dc % 2
                    w0 = [("ldd", vldy), ("ldo", vow), ("ve", rel_psy[psl])]
                    for kc in range(16):
                        pr.emit("tensor",
                                lambda e, kc=kc, dc=dc, ysl=ysl, psl=psl: e.matmul(
                                    ps_y[psl][:], ytil[ysl][:, kc * 128:(kc + 1) * 128],
                                    ow_sb[:, kc * 2048 + dc * 512: kc * 2048 + (dc + 1) * 512],
                                    start=(kc == 0), stop=(kc == 15)),
                                waits=tuple(w for w in w0 if w[1] > 0) if kc == 0 else (),
                                incs=(("pe", 1),) if kc == 15 else ())
                    vpe = pr.c("pe")
                    pr.emit("vector", lambda e, psl=psl, msl=dc % 2: e.tensor_copy(mst[msl][:], ps_y[psl][:]),
                            waits=(("pe", vpe), ("st", rel_mst[dc % 2])), incs=(("ve", 1),))
                    rel_psy[psl] = pr.c("ve")
                    vev = pr.c("ve")
                    pr.emit("sync", lambda e, tt=tt, dc=dc, msl=dc % 2: e.dma_start(
                                out=msr[tt, :, dc * 512:(dc + 1) * 512], in_=mst[msl][:]),
                            waits=(("ve", vev),), incs=(("st", 16),))
                    rel_mst[dc % 2] = pr.c("st")
                rel_yt[ysl] = pr.c("pe")

        vst_C = pr.c("st")
        with ExitStack() as esd:
            dx = [esd.enter_context(nc.sbuf_tensor(f"dx{_UNIQ[0]}_{i}", [128, D], F32)) for i in range(2)]
            dm = [esd.enter_context(nc.sbuf_tensor(f"dm{_UNIQ[0]}_{i}", [128, D], F32)) for i in range(2)]
            do = [esd.enter_context(nc.sbuf_tensor(f"do{_UNIQ[0]}_{i}", [128, D], F32)) for i in range(2)]
            rel_d = [(), ()]
            for ti in range(16):
                sl = ti % 2
                pr.emit("sync", lambda e, ti=ti, sl=sl: e.dma_start(out=dx[sl][:], in_=x3r[ti]),
                        waits=(("st", vst_C),) + rel_d[sl], incs=(("ldd", 16),))
                pr.emit("sync", lambda e, ti=ti, sl=sl: e.dma_start(out=dm[sl][:], in_=msr[ti]),
                        incs=(("ldd", 16),))
                vld = pr.c("ldd")
                pr.emit("vector", lambda e, sl=sl: e.tensor_tensor(dm[sl][:], dm[sl][:], dx[sl][:], ALU.add),
                        waits=(("ldd", vld),), incs=(("ve", 1),))
                vv = pr.c("ve")
                g = emit_ln_row(pr, st, dm[sl][:], do[sl][:], bufs, [("ve", vv), ("st", max(0, pr.c("st") - 16))])
                pr.emit("sync", lambda e, ti=ti, sl=sl: e.dma_start(out=x4r[ti], in_=do[sl][:]),
                        waits=(("grd", g),), incs=(("st", 16),))
                rel_d[sl] = (("grd", g),)



def build_p4():
    nc = bass.Bass("TRN2", target_bir_lowering=False)
    DD = {}
    DD["x3"] = nc.dram_tensor("x3", [T, D], F32, kind="ExternalInput").ap()
    DD["qw_til"] = nc.dram_tensor("qw_til", [16, 128, 2048], F32, kind="ExternalInput").ap()
    DD["kw_til"] = nc.dram_tensor("kw_til", [4, 128, 2048], F32, kind="ExternalInput").ap()
    DD["vw_r"] = nc.dram_tensor("vw_r", [16, 128, 512], F32, kind="ExternalInput").ap()
    DD["ow_r"] = nc.dram_tensor("ow_r", [16, 128, 2048], F32, kind="ExternalInput").ap()
    DD["Rrel"] = nc.dram_tensor("Rrel", [128, 2176], F32, kind="ExternalInput").ap()
    DD["Dmask"] = nc.dram_tensor("Dmask", [128, 128], F32, kind="ExternalInput").ap()
    DD["ident"] = nc.dram_tensor("ident", [128, 128], F32, kind="ExternalInput").ap()
    DD["x4"] = nc.dram_tensor("x4", [T, D], F32, kind="ExternalOutput").ap()
    dbg = "ExternalOutput" if P4_DEBUG else "Internal"
    DD["y_s"] = nc.dram_tensor("y_s", [2048, 2048], BF16, kind=dbg).ap()
    DD["m_s"] = nc.dram_tensor("m_s", [T, D], F32, kind=dbg).ap()
    if P4_DEBUG:
        DD["S_dump"] = nc.dram_tensor("S_dump", [16, 128, 2048], F32, kind="ExternalOutput").ap()
        DD["P_dump"] = nc.dram_tensor("P_dump", [4, 128, 8192], BF16, kind="ExternalOutput").ap()
    es0 = ExitStack()
    pr = Prog(nc, es0)
    with es0:
        emit_p4(pr, nc, DD)
        pr.emit("gpsimd", lambda e: e.nop(), waits=(("st", pr.c("st")),))
        pr.build()
    return nc


def emit_p5(pr, nc, DD):
    x4 = DD["x4"]; gw = DD["gw_til"]; w1t = DD["w1_til"]; w3t = DD["w3_til"]
    w2r = DD["w2_r"]; ws1t = DD["ws1_til"]; ws3t = DD["ws3_til"]; ws2r = DD["ws2_r"]
    id_d = DD["ident"]; out = DD["out"]
    x4r = x4.rearrange("(tt p) d -> tt p d", p=128)
    outr = out.rearrange("(tt p) d -> tt p d", p=128)

    es = ExitStack()
    sb, psum = _mk_alloc(nc, es)

    bufs = {}
    bufs["xt"] = [sb(f"xt{i}", [128, D], F32) for i in range(2)]
    bufs["lst"] = sb("lst", [128, 6], F32)
    x4_fm = sb("x4_fm", [128, 16 * 1024], BF16)
    gbuf = sb("gbuf", [128, 16 * 128], F32)
    bufs["lsq"] = gbuf
    ghi = sb("ghi", [128, 2048], BF16)
    glo = sb("glo", [128, 2048], BF16)
    xlo = sb("xlo", [128, 2048], BF16)
    combs = sb("combs", [128, 8 * 8], F32)
    sc = sb("sc", [128, 8], F32)
    p_sc = sb("p_sc", [128, 8], F32)
    oh = sb("oh", [128, 8], F32)
    smx = sb("smx", [128, 4], F32)
    flt = sb("flt", [128, 4], F32)
    y_acc = sb("y_acc", [128, 8 * 2048], F32)
    hact = sb("hact", [128, 8 * 1024], BF16)
    g1 = sb("g1", [128, 1024], F32)
    u1 = sb("u1", [128, 1024], F32)
    w1A = [sb(f"w1A{i}", [128, 2048], BF16) for i in range(2)]
    w3A = [sb(f"w3A{i}", [128, 2048], BF16) for i in range(2)]
    w2sb = [sb(f"w2sb{i}", [128, 8 * 512], BF16) for i in range(2)]

    ps_g = [psum("ps_g0", [128, 512], F32), psum("ps_g1", [128, 512], F32)]
    ps_u = [psum("ps_u0", [128, 512], F32), psum("ps_u1", [128, 512], F32)]
    ps_c = [psum("ps_c0", [128, 512], F32), psum("ps_c1", [128, 512], F32)]

    pr.sem("ldx", "ldw", "ldc", "pe", "ve", "ac", "grd", "st", "ldo", "ldd", "evH")

    st = {"relA": [(), ()], "prev_chain": pr.c("grd")}
    rel_w = [pr.c("pe"), pr.c("pe")]
    rel_w2 = [pr.c("pe"), pr.c("pe")]
    rel_psc = [pr.c("ve"), pr.c("ve")]
    rel_doLN = [(), ()]

    id_sb = sb("id_sb", [128, 128], F32)

    with es:
        pr.emit("sync", lambda e: e.dma_start(out=gbuf[:], in_=gw), incs=(("ldc", 16),))
        pr.emit("sync", lambda e: e.dma_start(out=id_sb[:], in_=id_d), incs=(("ldc", 16),))
        pr.emit("gpsimd", lambda e: e.dma_start(out=ghi[:], in_=gw), incs=(("ldc", 16),))
        wc = (("ldc", pr.c("ldc")),)
        pr.emit("vector", lambda e: e.scalar_tensor_tensor(glo[:], ghi[:], -1.0, gbuf[:], ALU.mult, ALU.add),
                waits=wc, incs=(("grd", 1),))
        v_glo = pr.c("grd")
        st["prev_chain"] = pr.c("grd")

        vE_prev_ve = pr.c("ve")
        for H in range(2):
            tp_ev = []
            tp_ve = []
            vgate_prev = 0
            for i in range(8):
                ti = H * 8 + i
                sl = ti % 2
                pr.emit("sync", lambda e, ti=ti, sl=sl: e.dma_start(out=bufs["xt"][sl][:], in_=x4r[ti]),
                        waits=st["relA"][sl], incs=(("ldx", 16),))
                vx = pr.c("ldx")
                for dt in range(16):
                    w0 = [("ldx", vx)] + list(wc)
                    if len(tp_ev) >= 2:
                        w0.append(("ac", tp_ev[-2]))
                        w0.append(("ve", tp_ve[-2]))
                    if H == 0 and i == 0 and dt == 0:
                        w0.append(("grd", v_glo))
                    if H > 0 and i == 0 and dt == 0:
                        w0.append(("evH", pr.c("evH")))
                        w0.append(("ve", vE_prev_ve))
                    pr.emit("tensor",
                            lambda e, dt=dt, sl=sl: e.transpose(
                                ps_c[dt % 2][:, 0:128], bufs["xt"][sl][:, dt * 128:(dt + 1) * 128], id_sb[:]),
                            waits=tuple(w0), incs=(("pe", 1),))
                    vpe = pr.c("pe")
                    pr.emit("scalar", lambda e, dt=dt: e.copy(gbuf[:, dt * 128:(dt + 1) * 128], ps_c[dt % 2][:, 0:128]),
                            waits=(("pe", vpe),), incs=(("ac", 1),))
                    tp_ev.append(pr.c("ac"))
                    pr.emit("vector", lambda e, dt=dt, i=i: e.tensor_copy(
                                x4_fm[:, dt * 1024 + i * 128: dt * 1024 + (i + 1) * 128],
                                ps_c[dt % 2][:, 0:128]),
                            waits=(("pe", vpe), ("ac", tp_ev[-1])), incs=(("ve", 1),))
                    tp_ve.append(pr.c("ve"))
                pr.emit("scalar", lambda e: e.copy(flt[:, 0:1], flt[:, 1:2]), incs=(("ac", 1),))
                vtp = pr.c("ac")
                st["relA"][sl] = (("ac", vtp), ("ve", pr.c("ve")))
                for dt in ([] if P5_TRUNC in (4, 5) else range(16)):
                    pr.emit("vector",
                            lambda e, dt=dt, i=i: e.scalar_tensor_tensor(
                                xlo[:, dt * 128:(dt + 1) * 128],
                                x4_fm[:, dt * 1024 + i * 128: dt * 1024 + (i + 1) * 128], -1.0,
                                gbuf[:, dt * 128:(dt + 1) * 128], ALU.mult, ALU.add),
                            waits=(("ac", vtp),) if dt == 0 else ())
                pr.emit("vector", lambda e: e.tensor_copy(flt[:, 2:3], flt[:, 3:4]), incs=(("ve", 1),))
                vxlo = pr.c("ve")
                if P5_TRUNC in (3, 4, 5):
                    continue
                n_mm = 0
                for kt in range(16):
                    xhi_s = lambda e_, kt=kt, i=i: x4_fm[:, kt * 1024 + i * 128: kt * 1024 + (i + 1) * 128]
                    for (asel, bsel) in ((0, 0), (0, 1), (1, 0)):
                        is_first = n_mm == 0
                        is_last = n_mm == 47
                        pr.emit("tensor",
                                lambda e, kt=kt, i=i, asel=asel, bsel=bsel, f=is_first, l=is_last: e.matmul(
                                    ps_g[0][:, 0:128],
                                    (x4_fm[:, kt * 1024 + i * 128: kt * 1024 + (i + 1) * 128] if asel == 0
                                     else xlo[:, kt * 128:(kt + 1) * 128]),
                                    (ghi if bsel == 0 else glo)[:, kt * 128:(kt + 1) * 128],
                                    start=f, stop=l),
                                waits=(("ve", vxlo), ("grd", vgate_prev)) if is_first else (),
                                incs=(("pe", 1),) if is_last else ())
                        n_mm += 1
                vg = pr.c("pe")
                pr.emit("vector", lambda e: e.tensor_copy(sc[:], ps_g[0][:, 0:8]),
                        waits=(("pe", vg), ("grd", st["prev_chain"])), incs=(("grd", 1),))
                v1 = pr.c("grd")
                pr.emit("vector", lambda e: e.reduce_max(smx[:, 0:1], sc[:], axis=AX.X),
                        waits=(("grd", v1),), incs=(("grd", 1),))
                v2 = pr.c("grd")
                pr.emit("scalar", lambda e: e.mul(smx[:, 1:2], smx[:, 0:1], -1.0),
                        waits=(("grd", v2),), incs=(("grd", 1),))
                v3 = pr.c("grd")
                pr.emit("scalar", lambda e: e.activation(p_sc[:], sc[:], AF.Exp, bias=smx[:, 1:2],
                                                         accum_out=smx[:, 2:3]),
                        waits=(("grd", v3),), incs=(("grd", 1),))
                v4 = pr.c("grd")
                pr.emit("vector", lambda e: e.reciprocal(smx[:, 3:4], smx[:, 2:3]),
                        waits=(("grd", v4),), incs=(("grd", 1),))
                v5 = pr.c("grd")
                pr.emit("vector", lambda e: e.tensor_scalar_mul(p_sc[:], p_sc[:], smx[:, 3:4]),
                        waits=(("grd", v5),), incs=(("grd", 1),))
                v6 = pr.c("grd")
                pr.emit("vector", lambda e: e.reduce_max(smx[:, 0:1], p_sc[:], axis=AX.X),
                        waits=(("grd", v6),), incs=(("grd", 1),))
                v7 = pr.c("grd")
                pr.emit("vector", lambda e: e.tensor_single_scalar(oh[:], p_sc[:], smx[:, 0:1], ALU.is_equal),
                        waits=(("grd", v7),), incs=(("grd", 1),))
                v8 = pr.c("grd")
                pr.emit("vector", lambda e, ti=ti: e.tensor_scalar_mul(combs[:, (ti % 8) * 8:(ti % 8) * 8 + 8],
                                                                       oh[:], smx[:, 0:1]),
                        waits=(("grd", v8),), incs=(("grd", 1),))
                v9 = pr.c("grd")
                pr.emit("vector", lambda e: e.scalar_tensor_tensor(sc[:], oh[:], -1e30, p_sc[:], ALU.mult, ALU.add),
                        waits=(("grd", v9),), incs=(("grd", 1),))
                v10 = pr.c("grd")
                pr.emit("vector", lambda e: e.reduce_max(smx[:, 1:2], sc[:], axis=AX.X),
                        waits=(("grd", v10),), incs=(("grd", 1),))
                v11 = pr.c("grd")
                pr.emit("vector", lambda e: e.tensor_single_scalar(oh[:], sc[:], smx[:, 1:2], ALU.is_equal),
                        waits=(("grd", v11),), incs=(("grd", 1),))
                v12 = pr.c("grd")
                pr.emit("vector", lambda e, ti=ti: e.scalar_tensor_tensor(
                            combs[:, (ti % 8) * 8:(ti % 8) * 8 + 8], oh[:], smx[:, 1:2],
                            combs[:, (ti % 8) * 8:(ti % 8) * 8 + 8], ALU.mult, ALU.add),
                        waits=(("grd", v12),), incs=(("grd", 1),))
                st["prev_chain"] = pr.c("grd")
                vgate_prev = pr.c("grd")
            pr.emit("vector", lambda e: e.tensor_copy(flt[:, 2:3], flt[:, 3:4]), incs=(("ve", 1),))
            vA_ve = pr.c("ve")
            vcombs = pr.c("grd")

            for e_i in ([] if P5_TRUNC == 1 else (range(1) if P5_TRUNC == 2 else range(9))):
                shared = e_i == 8
                l1 = (lambda ic, e_i=e_i: ws1t[ic]) if shared else (lambda ic, e_i=e_i: w1t[e_i, ic])
                l3 = (lambda ic, e_i=e_i: ws3t[ic]) if shared else (lambda ic, e_i=e_i: w3t[e_i, ic])
                for ic in range(8):
                    sl = ic % 2
                    pr.emit("gpsimd", lambda e, ic=ic, sl=sl, l1=l1: e.dma_start(
                                out=w1A[sl][:, 0:1024], in_=l1(ic)[:, 0:1024]),
                            waits=(("pe", rel_w[sl]),), incs=(("ldw", 16),))
                    vw1a = pr.c("ldw")
                    pr.emit("gpsimd", lambda e, ic=ic, sl=sl, l1=l1: e.dma_start(
                                out=w1A[sl][:, 1024:2048], in_=l1(ic)[:, 1024:2048]),
                            incs=(("ldw", 16),))
                    vw1b = pr.c("ldw")
                    pr.emit("gpsimd", lambda e, ic=ic, sl=sl, l3=l3: e.dma_start(out=w3A[sl][:], in_=l3(ic)),
                            incs=(("ldw", 16),))
                    vw = pr.c("ldw")
                    wb = [("ldw", vw1a), ("ve", vA_ve), ("evH", pr.c("evH")), ("grd", vcombs)]
                    for n in range(2):
                        for kt in range(16):
                            if kt == 0 and n == 0:
                                w_ = tuple(w for w in wb if w[1] > 0)
                            elif kt == 8 and n == 0:
                                w_ = (("ldw", vw1b),)
                            else:
                                w_ = ()
                            pr.emit("tensor",
                                    lambda e, kt=kt, n=n, sl=sl: e.matmul(
                                        ps_g[n][:], w1A[sl][:, kt * 128:(kt + 1) * 128],
                                        x4_fm[:, kt * 1024 + n * 512: kt * 1024 + (n + 1) * 512],
                                        start=(kt == 0), stop=(kt == 15)),
                                    waits=w_)
                    for n in range(2):
                        for kt in range(16):
                            pr.emit("tensor",
                                    lambda e, kt=kt, n=n, sl=sl: e.matmul(
                                        ps_u[n][:], w3A[sl][:, kt * 128:(kt + 1) * 128],
                                        x4_fm[:, kt * 1024 + n * 512: kt * 1024 + (n + 1) * 512],
                                        start=(kt == 0), stop=(kt == 15)),
                                    waits=(("ldw", vw),) if (kt == 0 and n == 0) else (),
                                    incs=(("pe", 1),) if (kt == 15 and n == 1) else ())
                    vmm = pr.c("pe")
                    rel_w[sl] = vmm
                    pr.emit("scalar", lambda e: e.activation(g1[:, 0:512], ps_g[0][:], AF.Silu),
                            waits=(("pe", vmm), ("evH", pr.c("evH"))), incs=(("ac", 1),))
                    pr.emit("scalar", lambda e: e.activation(g1[:, 512:1024], ps_g[1][:], AF.Silu),
                            incs=(("ac", 1),))
                    vgs = pr.c("ac")
                    pr.emit("vector", lambda e: e.tensor_copy(u1[:, 0:512], ps_u[0][:]),
                            waits=(("pe", vmm),), incs=(("ve", 1),))
                    pr.emit("vector", lambda e: e.tensor_copy(u1[:, 512:1024], ps_u[1][:]),
                            incs=(("ve", 1),))
                    pr.emit("vector", lambda e, ic=ic: e.tensor_tensor(hact[:, ic * 1024:(ic + 1) * 1024],
                                                                      g1[:], u1[:], ALU.mult),
                            waits=(("ac", vgs),), incs=(("evH", 1),))
                vhact = pr.c("evH")
                pr.emit("vector", lambda e: e.tensor_copy(flt[:, 2:3], flt[:, 3:4]), incs=(("evH", 1),))
                vhact = pr.c("evH")

                l2 = (lambda dn, e_i=e_i: ws2r[:, :, dn * 512:(dn + 1) * 512]) if shared else \
                     (lambda dn, e_i=e_i: w2r[e_i, :, :, dn * 512:(dn + 1) * 512])
                for dn in range(4):
                    w2l = dn % 2
                    pr.emit("gpsimd", lambda e, dn=dn, w2l=w2l, l2=l2: e.dma_start(
                                out=w2sb[w2l][:].rearrange("p (kt c) -> p kt c", kt=8),
                                in_=l2(dn).rearrange("kt p c -> p kt c")),
                            waits=(("pe", rel_w2[w2l]),), incs=(("ldo", 16),))
                    vwo = pr.c("ldo")
                    for tt in range(8):
                        csl = tt % 2
                        w0 = [("ldo", vwo), ("evH", vhact), ("ve", rel_psc[csl]), ("grd", vcombs)]
                        for ic in range(8):
                            pr.emit("tensor",
                                    lambda e, ic=ic, tt=tt, csl=csl, w2l=w2l: e.matmul(
                                        ps_c[csl][:], hact[:, ic * 1024 + tt * 128: ic * 1024 + (tt + 1) * 128],
                                        w2sb[w2l][:, ic * 512:(ic + 1) * 512],
                                        start=(ic == 0), stop=(ic == 7)),
                                    waits=tuple(w for w in w0 if w[1] > 0) if ic == 0 else (),
                                    incs=(("pe", 1),) if ic == 7 else ())
                        vpc = pr.c("pe")
                        ya = lambda: y_acc[:, tt * 2048 + dn * 512: tt * 2048 + (dn + 1) * 512]
                        if e_i == 0:
                            pr.emit("vector", lambda e, tt=tt, dn=dn, csl=csl: e.tensor_scalar_mul(
                                        y_acc[:, tt * 2048 + dn * 512: tt * 2048 + (dn + 1) * 512],
                                        ps_c[csl][:], combs[:, (tt) * 8 + 0: (tt) * 8 + 1]),
                                    waits=(("pe", vpc),), incs=(("ve", 1),))
                            rel_psc[csl] = pr.c("ve")
                        elif not shared:
                            pr.emit("vector", lambda e, tt=tt, dn=dn, csl=csl, e_i=e_i: e.scalar_tensor_tensor(
                                        y_acc[:, tt * 2048 + dn * 512: tt * 2048 + (dn + 1) * 512],
                                        ps_c[csl][:], combs[:, tt * 8 + e_i: tt * 8 + e_i + 1],
                                        y_acc[:, tt * 2048 + dn * 512: tt * 2048 + (dn + 1) * 512],
                                        ALU.mult, ALU.add),
                                    waits=(("pe", vpc),), incs=(("ve", 1),))
                            rel_psc[csl] = pr.c("ve")
                        else:
                            pr.emit("vector", lambda e, tt=tt, dn=dn, csl=csl: e.tensor_tensor(
                                        y_acc[:, tt * 2048 + dn * 512: tt * 2048 + (dn + 1) * 512],
                                        y_acc[:, tt * 2048 + dn * 512: tt * 2048 + (dn + 1) * 512],
                                        ps_c[csl][:], ALU.add),
                                    waits=(("pe", vpc),), incs=(("ve", 1),))
                            rel_psc[csl] = pr.c("ve")
                    rel_w2[w2l] = pr.c("pe")
            vE_done = pr.c("ve")

            for tt in ([] if P5_TRUNC else range(8)):
                ti = H * 8 + tt
                pr.emit("sync", lambda e, ti=ti: e.dma_start(out=bufs["xt"][0][:], in_=x4r[ti]),
                        waits=rel_doLN[0] + st["relA"][0], incs=(("ldd", 16),))
                vld = pr.c("ldd")
                pr.emit("vector", lambda e, tt=tt: e.tensor_tensor(
                            y_acc[:, tt * 2048:(tt + 1) * 2048], y_acc[:, tt * 2048:(tt + 1) * 2048],
                            bufs["xt"][0][:], ALU.add),
                        waits=(("ldd", vld), ("ve", vE_done)), incs=(("ve", 1),))
                vv = pr.c("ve")
                g = emit_ln_row(pr, st, y_acc[:, tt * 2048:(tt + 1) * 2048], bufs["xt"][1][:], bufs,
                                [("ve", vv), ("st", pr.c("st")), ("ldx", pr.c("ldx"))] + list(st["relA"][1]))
                pr.emit("sync", lambda e, ti=ti: e.dma_start(out=outr[ti], in_=bufs["xt"][1][:]),
                        waits=(("grd", g),), incs=(("st", 16),))
                rel_doLN[0] = (("grd", g), ("ve", vv))
            pr.emit("vector", lambda e: e.tensor_copy(flt[:, 2:3], flt[:, 3:4]),
                    waits=(("grd", st["prev_chain"]),), incs=(("evH", 1),))
            vE_prev_ve = vE_done
            st["relA"] = [(("grd", pr.c("grd")), ("ldd", pr.c("ldd"))), (("grd", pr.c("grd")),)]



def build_p5():
    nc = bass.Bass("TRN2", target_bir_lowering=False)
    DD = {}
    DD["x4"] = nc.dram_tensor("x4", [T, D], F32, kind="ExternalInput").ap()
    DD["gw_til"] = nc.dram_tensor("gw_til", [128, 2048], F32, kind="ExternalInput").ap()
    DD["w1_til"] = nc.dram_tensor("w1_til", [8, 8, 128, 2048], F32, kind="ExternalInput").ap()
    DD["w3_til"] = nc.dram_tensor("w3_til", [8, 8, 128, 2048], F32, kind="ExternalInput").ap()
    DD["w2_r"] = nc.dram_tensor("w2_r", [8, 8, 128, 2048], F32, kind="ExternalInput").ap()
    DD["ws1_til"] = nc.dram_tensor("ws1_til", [8, 128, 2048], F32, kind="ExternalInput").ap()
    DD["ws3_til"] = nc.dram_tensor("ws3_til", [8, 128, 2048], F32, kind="ExternalInput").ap()
    DD["ws2_r"] = nc.dram_tensor("ws2_r", [8, 128, 2048], F32, kind="ExternalInput").ap()
    DD["ident"] = nc.dram_tensor("ident", [128, 128], F32, kind="ExternalInput").ap()
    DD["out"] = nc.dram_tensor("out", [T, D], F32, kind="ExternalOutput").ap()
    es0 = ExitStack()
    pr = Prog(nc, es0)
    with es0:
        emit_p5(pr, nc, DD)
        pr.emit("gpsimd", lambda e: e.nop(), waits=(("st", pr.c("st")),))
        pr.build()
    return nc


class GatherPump:

    def __init__(self, pr, nc, es, idx_cat, lut_d, wcat, ntiles=129):
        self.pr = pr
        self.idx_cat = idx_cat
        self.wcat = wcat
        self.lut_d = lut_d
        self.ntiles = ntiles
        sbp = lambda n, s, d: es.enter_context(nc.sbuf_tensor(n, s, d))
        self.idx_bf = [sbp(f"g_idx{i}", [128, 2048], BF16) for i in range(2)]
        self.gacc = [sbp(f"g_acc{i}", [128, 2048], BF16) for i in range(2)]
        self.eqt = sbp("g_eqt", [128, 2048], BF16)
        self.lut_sb = sbp("g_lut", [128, 16], F32)
        pr.sem("gld", "gev", "gst", "glc")
        pr.emit("sync", lambda e: e.dma_start(out=self.lut_sb[:], in_=lut_d), incs=(("glc", 16),))
        self.j = 0
        self.k = 0
        self.rel_idx = [0, 0]
        self.rel_acc = [0, 0]
        self.store_counts = [0] * ntiles
        self._load_issued = -1

    def _issue_load(self, j):
        pr = self.pr
        sl = j % 2
        pr.emit("gpsimd", lambda e, j=j, sl=sl: e.dma_start(out=self.idx_bf[sl][:], in_=self.idx_cat[j]),
                waits=(("gev", self.rel_idx[sl]),), incs=(("gld", 16),))
        self._load_issued = j

    def step_iters(self, n):
        pr = self.pr
        while n > 0 and self.j < self.ntiles:
            j = self.j
            sl = j % 2
            if self._load_issued < j:
                self._issue_load(j)
            if self.k == 0:
                if j + 1 < self.ntiles and self._load_issued < j + 1:
                    self._issue_load(j + 1)
                pr.emit("vector", lambda e, sl=sl: e.memset(self.gacc[sl][:], 0.0),
                        waits=(("gst", self.rel_acc[sl]),))
            kk = self.k
            w = (("gld", 16 * (j + 1)), ("glc", pr.c("glc"))) if kk == 0 else ()
            pr.emit("vector",
                    lambda e, kk=kk, sl=sl: e.tensor_scalar(self.eqt[:], self.idx_bf[sl][:], float(kk),
                                                            self.lut_sb[:, kk:kk + 1], ALU.is_equal, ALU.mult),
                    waits=w)
            pr.emit("vector", lambda e, sl=sl: e.tensor_tensor(self.gacc[sl][:], self.gacc[sl][:],
                                                               self.eqt[:], ALU.add),
                    incs=(("gev", 1),) if kk == 15 else ())
            self.k += 1
            n -= 1
            if self.k == 16:
                vg = pr.c("gev")
                self.rel_idx[sl] = vg
                pr.emit("sync", lambda e, j=j, sl=sl: e.dma_start(out=self.wcat[j], in_=self.gacc[sl][:]),
                        waits=(("gev", vg),), incs=(("gst", 16),))
                self.store_counts[j] = pr.c("gst")
                self.rel_acc[sl] = pr.c("gst")
                self.j += 1
                self.k = 0

    def ensure(self, ntile):
        while self.j < min(ntile, self.ntiles):
            self.step_iters(16)

    def flush_tile(self):
        while self.j < self.ntiles and (self.k != 0 or self._load_issued >= self.j):
            self.step_iters(16)

    def rebuffer(self, nc, es):
        sbp = lambda n, s, d: es.enter_context(nc.sbuf_tensor(n, s, d))
        _UNIQ[0] += 1
        tag = f"g{_UNIQ[0]}_"
        self.idx_bf = [sbp(tag + f"idx{i}", [128, 2048], BF16) for i in range(2)]
        self.gacc = [sbp(tag + f"acc{i}", [128, 2048], BF16) for i in range(2)]
        self.eqt = sbp(tag + "eqt", [128, 2048], BF16)
        self.lut_sb = sbp(tag + "lut", [128, 16], F32)
        self.pr.emit("sync", lambda e: e.dma_start(out=self.lut_sb[:], in_=self.lut_d), incs=(("glc", 16),))
        self._load_issued = self.j - 1

    def step(self, ktiles=1):
        self.step_iters(16 * ktiles)

    def finish(self):
        self.ensure(self.ntiles)


def build_full():
    nc = bass.Bass("TRN2", target_bir_lowering=False)
    DD = {}
    ein = lambda n, s, d: nc.dram_tensor(n, s, d, kind="ExternalInput").ap()
    intl = lambda n, s, d: nc.dram_tensor(n, s, d, kind="Internal").ap()
    DD["x"] = ein("x", [T, D], F32)
    DD["win_til"] = ein("win_til", [64, 128, 2048], F32)
    DD["wout_r"] = ein("wout_r", [32, 128, 2048], F32)
    DD["cw"] = ein("cw", [128, 128], F32)
    DD["cb"] = ein("cb", [128, 32], F32)
    DD["ident"] = ein("ident", [128, 128], F32)
    DD["wm1_til"] = ein("wm1_til", [64, 128, 2048], F32)
    DD["wm2_r"] = ein("wm2_r", [64, 128, 2048], F32)
    idx_cat = ein("idx_cat", [129, 128, 2048], I32)
    lut_d = ein("lutrep", [128, 16], F32)
    DD["qw_til"] = ein("qw_til", [16, 128, 2048], F32)
    DD["kw_til"] = ein("kw_til", [4, 128, 2048], F32)
    DD["vw_r"] = ein("vw_r", [16, 128, 512], F32)
    DD["ow_r"] = ein("ow_r", [16, 128, 2048], F32)
    DD["Rrel"] = ein("Rrel", [128, 2176], F32)
    DD["Dmask"] = ein("Dmask", [128, 128], F32)
    DD["gw_til"] = ein("gw_til", [128, 2048], F32)
    DD["w1_til"] = ein("w1_til", [8, 8, 128, 2048], F32)
    DD["w3_til"] = ein("w3_til", [8, 8, 128, 2048], F32)
    DD["w2_r"] = ein("w2_r", [8, 8, 128, 2048], F32)
    DD["ws1_til"] = ein("ws1_til", [8, 128, 2048], F32)
    DD["ws3_til"] = ein("ws3_til", [8, 128, 2048], F32)
    DD["ws2_r"] = ein("ws2_r", [8, 128, 2048], F32)
    DD["out"] = nc.dram_tensor("out", [T, D], F32, kind="ExternalOutput").ap()
    DD["x1"] = intl("x1", [T, D], F32)
    DD["x2"] = intl("x2", [T, D], F32)
    DD["x3"] = intl("x3", [T, D], F32)
    DD["x4"] = intl("x4", [T, D], F32)
    DD["m_s"] = intl("m_s", [T, D], F32)
    DD["y_s"] = intl("y_s", [2048, 2048], BF16)
    wcat = intl("wcat", [129, 128, 2048], BF16)
    DD["wgA"] = lambda oc: wcat[2 * oc]
    DD["wuA"] = lambda oc: wcat[2 * oc + 1]
    DD["wdR"] = lambda a, b: wcat[86 + a:86 + b]

    es0 = ExitStack()
    pr = Prog(nc, es0)
    with es0:
        with ExitStack() as esg1:
            pump = GatherPump(pr, nc, esg1, idx_cat, lut_d, wcat, ntiles=129)
            emit_p1(pr, nc, DD, pump=pump)
            pr.barrier()
            emit_p2(pr, nc, DD, pump=pump)
            pr.barrier()
            emit_p3(pr, nc, DD, pump=pump)
            pr.barrier()
        emit_p4(pr, nc, DD)
        pr.barrier()
        emit_p5(pr, nc, DD)
        pr.emit("gpsimd", lambda e: e.nop(), waits=(("st", pr.c("st")), ("gst", pr.c("gst"))))
        pr.build()
    return nc


FUSED = True


def kernel(**inputs):
    prep = host_prep(inputs)
    xfull = np.ascontiguousarray(inputs["L0_attn_x"])

    if FUSED:
        ncf = build_full()
        maps = []
        for b in range(NCORES):
            maps.append({
                "x": np.ascontiguousarray(xfull[b]),
                "win_til": prep["win_til"], "wout_r": prep["wout_r"],
                "cw": prep["cw"], "cb": prep["cb"], "ident": prep["ident"],
                "wm1_til": prep["wm1_til"], "wm2_r": prep["wm2_r"],
                "idx_cat": prep["idx_cat"], "lutrep": prep["lutrep"],
                "qw_til": prep["qw_til"], "kw_til": prep["kw_til"],
                "vw_r": prep["vw_r"], "ow_r": prep["ow_r"],
                "Rrel": prep["Rrel"], "Dmask": prep["Dmask"],
                "gw_til": prep["gw_til"], "w1_til": prep["w1_til"],
                "w3_til": prep["w3_til"], "w2_r": prep["w2_r"],
                "ws1_til": prep["ws1_til"], "ws3_til": prep["ws3_til"],
                "ws2_r": prep["ws2_r"],
            })
        res = run_bass_kernel_spmd(ncf, maps, core_ids=list(range(NCORES)))
        return np.stack([res.results[b]["out"] for b in range(NCORES)])

    nc1 = build_p1()
    maps = []
    for b in range(NCORES):
        maps.append({
            "x": np.ascontiguousarray(xfull[b]),
            "win_til": prep["win_til"],
            "wout_r": prep["wout_r"],
            "cw": prep["cw"],
            "cb": prep["cb"],
            "ident": prep["ident"],
        })
    res = run_bass_kernel_spmd(nc1, maps, core_ids=list(range(NCORES)))
    x1 = np.stack([res.results[b]["x1"] for b in range(NCORES)])

    nc2 = build_p2()
    maps = []
    for b in range(NCORES):
        maps.append({
            "x": np.ascontiguousarray(xfull[b]),
            "x1": x1[b],
            "wm1_til": prep["wm1_til"],
            "wm2_r": prep["wm2_r"],
            "ident": prep["ident"],
        })
    res = run_bass_kernel_spmd(nc2, maps, core_ids=list(range(NCORES)))
    x2 = np.stack([res.results[b]["x2"] for b in range(NCORES)])

    idx_cat = prep["idx_cat"]
    idx_pad = np.zeros((NCORES, NGTILES, 128, 2048), np.int32)
    for j in range(129):
        idx_pad[j % 8, j // 8] = idx_cat[j]
    ncg = build_gather()
    maps = [{"idx_my": idx_pad[b], "lutrep": prep["lutrep"]} for b in range(NCORES)]
    res = run_bass_kernel_spmd(ncg, maps, core_ids=list(range(NCORES)))
    wcat = np.zeros((129, 128, 2048), ml_dtypes.bfloat16)
    for j in range(129):
        wcat[j] = res.results[j % 8]["wcat"][j // 8]
    wg_s, wu_s, wd_s = wcat[0:86:2], wcat[1:86:2], wcat[86:129]

    nc3 = build_p3()
    maps = []
    for b in range(NCORES):
        maps.append({
            "x2": x2[b],
            "wg_s": wg_s,
            "wu_s": wu_s,
            "wd_s": wd_s,
            "ident": prep["ident"],
        })
    res = run_bass_kernel_spmd(nc3, maps, core_ids=list(range(NCORES)))
    x3 = np.stack([res.results[b]["x3"] for b in range(NCORES)])

    nc4 = build_p4()
    maps = []
    for b in range(NCORES):
        maps.append({
            "x3": x3[b],
            "qw_til": prep["qw_til"],
            "kw_til": prep["kw_til"],
            "vw_r": prep["vw_r"],
            "ow_r": prep["ow_r"],
            "Rrel": prep["Rrel"],
            "Dmask": prep["Dmask"],
            "ident": prep["ident"],
        })
    res = run_bass_kernel_spmd(nc4, maps, core_ids=list(range(NCORES)))
    x4 = np.stack([res.results[b]["x4"] for b in range(NCORES)])

    nc5 = build_p5()
    maps = []
    for b in range(NCORES):
        maps.append({
            "x4": x4[b],
            "gw_til": prep["gw_til"],
            "w1_til": prep["w1_til"],
            "w3_til": prep["w3_til"],
            "w2_r": prep["w2_r"],
            "ws1_til": prep["ws1_til"],
            "ws3_til": prep["ws3_til"],
            "ws2_r": prep["ws2_r"],
            "ident": prep["ident"],
        })
    res = run_bass_kernel_spmd(nc5, maps, core_ids=list(range(NCORES)))
    return np.stack([res.results[b]["out"] for b in range(NCORES)])
